# version 12
# baseline (speedup 1.0000x reference)
import os as _os
import numpy as np
import concourse.bass as bass
import concourse.mybir as mybir
from concourse.bass_utils import run_bass_kernel_spmd
from contextlib import ExitStack

F32 = mybir.dt.float32
BF16 = mybir.dt.bfloat16
AF = mybir.ActivationFunctionType
OP = mybir.AluOpType
AX = mybir.AxisListType.X

NCORES = 8
D = 1024
NCH = 8
GT = 2
LN2C = 0.6931471805599453

ENGS = ("sync", "tensor", "vector", "scalar", "gpsimd")


PARANOID = _os.environ.get("KPARANOID", "0") == "1"
SCQ = "sync" if _os.environ.get("KSYNCQ", "0") == "1" else "scalar"


class Prog:

    def __init__(self):
        self.ops = {e: [] for e in ENGS}
        self.cnt = {}
        self.esem = {}

    def set_esems(self, m):
        self.esem = m

    def val(self, sem):
        return self.cnt.get(id(sem), 0)

    def emit(self, eng, builder, waits=(), inc=None, inc_by=1):
        waits = list(waits)
        if PARANOID and self.esem:
            for k in ("te", "ve", "sc", "gp"):
                s = self.esem[k]
                waits.append((s, self.val(s)))
            if inc is None and eng in self.esem_by_eng:
                inc = self.esem_by_eng[eng]
                inc_by = 1
        after = None
        if inc is not None:
            after = self.val(inc) + inc_by
            self.cnt[id(inc)] = after
        self.ops[eng].append((waits, builder, inc, inc_by))
        return after

    def replay(self, eng_name, e):
        for waits, builder, inc, inc_by in self.ops[eng_name]:
            for sem, v in waits:
                if v is None or v == 0:
                    continue
                e.wait_ge(sem, v)
            ins = builder(e)
            if inc is not None:
                if ins is None:
                    if PARANOID:
                        continue
                    assert False, "builder for inc op returned no instruction"
                ins.then_inc(inc, inc_by)


def sap(t, off, dims):
    return bass.AP(t, off, dims)


def build_nc(T, PP, dbg=False):
    NG = T // GT
    GPP = GT * PP
    GF = NCH * GPP
    TF = NCH * PP
    JC = 6 * D // 128
    MODW = JC * T

    nc = bass.Bass("TRN2", target_bir_lowering=False, detect_race_conditions=False)

    xh = nc.dram_tensor("xh", [NG, 128, GF], F32, kind="ExternalInput")
    wnames = ["WaT", "WxT", "wfT", "wiT", "wcT", "ufT", "uiT", "ucT", "woT", "uoT"]
    wd = {n: nc.dram_tensor(n, [D, D], F32, kind="ExternalInput") for n in wnames}
    wmodd = nc.dram_tensor("wmod", [D, 6 * D // NCORES], F32, kind="ExternalInput")
    sctd = nc.dram_tensor("scT", [128, NCH * T], F32, kind="ExternalInput")
    vecsd = nc.dram_tensor("vecs", [128, 64], F32, kind="ExternalInput")
    masksd = nc.dram_tensor("cmask", [128, 128], F32, kind="ExternalInput")
    outp = nc.dram_tensor("outp", [T, 128, TF], F32, kind="ExternalOutput")
    x2d = nc.dram_tensor("x2d", [T, 128, TF], F32)
    rowsd = nc.dram_tensor("rowsd", [NG, 2, GPP], F32)
    cind = nc.dram_tensor("cin", [T, 128, 16], F32)
    coutd = nc.dram_tensor("cout", [T, 8, 128, 16], F32)
    modind = nc.dram_tensor("modin", [128, 6 * NCH * T // NCORES], F32)
    modoutd = nc.dram_tensor("modout", [NCORES, 128, 6 * NCH * T // NCORES], F32)
    dbg_x2 = nc.dram_tensor("dbg_x2", [T, 128, NCH * PP], F32, kind="ExternalOutput") if dbg else None
    dbg_a = nc.dram_tensor("dbg_a", [T // GT, 128, NCH * GT * PP], F32, kind="ExternalOutput") if dbg else None
    dbg_b = nc.dram_tensor("dbg_b", [T // GT, 128, NCH * GT * PP], F32, kind="ExternalOutput") if dbg else None
    dbg_ci = nc.dram_tensor("dbg_ci", [T, 128, 16], F32, kind="ExternalOutput") if dbg else None
    dbg_h1 = nc.dram_tensor("dbg_h1", [T // GT, 128, NCH * GT * PP], F32, kind="ExternalOutput") if dbg else None
    dbg_dv = nc.dram_tensor("dbg_dv", [128, 128], F32, kind="ExternalOutput") if dbg else None
    dbg_cmb = nc.dram_tensor("dbg_cmb", [128, 512], F32, kind="ExternalOutput") if dbg else None
    dbg_mod = nc.dram_tensor("dbg_mod", [128, 6 * NCH * T], F32, kind="ExternalOutput") if dbg else None

    with ExitStack() as stack:
        e = stack.enter_context

        wslot = [e(nc.sbuf_tensor(f"wslot{i}", [128, 8192], BF16)) for i in range(6)]
        Bx = wslot[2].ap().bitcast(F32)
        Bh1 = wslot[3].ap().bitcast(F32)
        Ba = wslot[4].ap().bitcast(F32)
        Baa = wslot[5].ap().bitcast(F32)
        Bh1b = e(nc.sbuf_tensor("Bh1b", [128, 4096], BF16))
        XBIG = e(nc.sbuf_tensor("XBIG", [128, 8192], BF16))
        XBF = XBIG.ap().bitcast(F32)
        SCRA = e(nc.sbuf_tensor("SCRA", [128, 2 * GPP], F32))
        SCRX = e(nc.sbuf_tensor("SCRX", [128, 2 * GPP], F32))
        Y = e(nc.sbuf_tensor("Y", [128, TF], F32))
        MB = e(nc.sbuf_tensor("MB", [128, GPP], F32))
        RSTD = e(nc.sbuf_tensor("RSTD", [128, GPP], F32))
        MB2 = e(nc.sbuf_tensor("MB2", [128, GPP], F32))
        RSTD2 = e(nc.sbuf_tensor("RSTD2", [128, GPP], F32))
        VAR = e(nc.sbuf_tensor("VAR", [128, GPP], F32))
        VAR2 = e(nc.sbuf_tensor("VAR2", [128, GPP], F32))
        VT = e(nc.sbuf_tensor("VT", [128, GPP], F32))
        VTB = e(nc.sbuf_tensor("VTB", [128, GPP], F32))
        CARRY = e(nc.sbuf_tensor("CARRY", [128, 32], F32))
        AGB = e(nc.sbuf_tensor("AGB", [128, 256], F32))
        HIN = e(nc.sbuf_tensor("HIN", [128, 16], F32))
        CMB1 = e(nc.sbuf_tensor("CMB1", [128, 64], F32))
        CMB2 = e(nc.sbuf_tensor("CMB2", [128, 64], F32))
        CMB3 = e(nc.sbuf_tensor("CMB3", [128, 64], F32))
        ATOT = e(nc.sbuf_tensor("ATOT", [128, 16], F32))
        THS = e(nc.sbuf_tensor("THS", [128, 16], F32))
        MOD = e(nc.sbuf_tensor("MOD", [128, MODW], F32))
        SCT = e(nc.sbuf_tensor("SCT", [128, NCH * T], F32))
        SCT2 = e(nc.sbuf_tensor("SCT2", [128, NCH * T], F32))
        VEC = e(nc.sbuf_tensor("VEC", [128, 64], F32))
        DVEC = e(nc.sbuf_tensor("DVEC", [128, 64], F32))
        MASK = e(nc.sbuf_tensor("MASK", [128, 128], F32))
        ONESB = e(nc.sbuf_tensor("ONESB", [128, 128], BF16))
        ONESF = e(nc.sbuf_tensor("ONESF", [128, 128], F32))
        ONES5 = e(nc.sbuf_tensor("ONES5", [128, 512], F32))
        WST = e(nc.sbuf_tensor("WST", [128, 4096], BF16))
        WSTf = WST.ap().bitcast(F32)
        ROWSTAGE = e(nc.sbuf_tensor("ROWSTAGE", [128, 4 * PP], F32))
        C = e(nc.sbuf_tensor("C", [128, TF], F32))
        CB = e(nc.sbuf_tensor("CB", [128, 2 * 2048], BF16))
        H2B = e(nc.sbuf_tensor("H2B", [128, 2 * 2048], BF16))
        THF = e(nc.sbuf_tensor("THF", [128, 2 * PP], F32))
        THI = e(nc.sbuf_tensor("THI", [128, 2 * PP], F32))
        THG = e(nc.sbuf_tensor("THG", [128, 2 * PP], F32))
        U1s = e(nc.sbuf_tensor("U1s", [128, 2 * PP], F32))
        U2s = e(nc.sbuf_tensor("U2s", [128, 2 * PP], F32))
        Z3 = THF.ap()
        Z3B = U2s.ap()
        CONSTS = e(nc.sbuf_tensor("CONSTS", [128, 1], F32))

        ps_a = e(nc.psum_tensor("ps_a", [128, GPP], F32))
        ps_b = e(nc.psum_tensor("ps_b", [128, GPP], F32))
        ps_c = e(nc.psum_tensor("ps_c", [128, GPP], F32))
        ps_d = e(nc.psum_tensor("ps_d", [128, GPP], F32))
        ps_mod = e(nc.psum_tensor("ps_mod", [128, max(MODW, 3 * PP)], F32))
        ps_g = e(nc.psum_tensor("ps_g", [128, 3 * PP], F32))

        ldm = e(nc.semaphore("ldm"))
        ldw0 = e(nc.semaphore("ldw0"))
        ldw1 = e(nc.semaphore("ldw1"))
        ldv0 = e(nc.semaphore("ldv0"))
        ldv1 = e(nc.semaphore("ldv1"))
        ldx = e(nc.semaphore("ldx"))
        ldx2 = [e(nc.semaphore("ldx2_0")), e(nc.semaphore("ldx2_1"))]
        ldr = [e(nc.semaphore("ldr0")), e(nc.semaphore("ldr1"))]
        sqs = e(nc.semaphore("sqs"))
        sqr = e(nc.semaphore("sqr"))
        sqo = [e(nc.semaphore("sqo0")), e(nc.semaphore("sqo1"))]
        gqc = [e(nc.semaphore("gqc0")), e(nc.semaphore("gqc1"))]
        gqa = [e(nc.semaphore("gqa0")), e(nc.semaphore("gqa1"))]
        ccs = e(nc.semaphore("ccs"))
        s_te = e(nc.semaphore("s_te"))
        s_ve = e(nc.semaphore("s_ve"))
        s_sc = e(nc.semaphore("s_sc"))
        s_gp = e(nc.semaphore("s_gp"))

        MODL_BUF = VT
        P = Prog()
        P.esem_by_eng = {"tensor": s_te, "vector": s_ve, "scalar": s_sc, "gpsimd": s_gp}
        P.set_esems({"te": s_te, "ve": s_ve, "sc": s_sc, "gp": s_gp,
                     "ldm": ldm, "ldw0": ldw0, "ldw1": ldw1, "ldv0": ldv0, "ldv1": ldv1,
                     "ldx": ldx, "ldx20": ldx2[0], "ldx21": ldx2[1],
                     "ldr0": ldr[0], "ldr1": ldr[1], "sqs": sqs, "sqr": sqr,
                     "sqo0": sqo[0], "sqo1": sqo[1], "gqc0": gqc[0], "gqc1": gqc[1],
                     "gqa0": gqa[0], "gqa1": gqa[1], "ccs": ccs})

        VORD = ["ba", "bx", "bf", "bi", "bc", "bo", "inc", "lam"]
        vcol = {n: 8 * i for i, n in enumerate(VORD)}
        dcol = {"bah": 0, "bxh": 8, "bfh": 16, "bih": 24, "sh": 32, "s256": 40}

        def vc(name, c):
            return VEC[:, vcol[name] + c: vcol[name] + c + 1]

        def dc(name, c):
            return DVEC[:, dcol[name] + c: dcol[name] + c + 1]

        def modcol(g, c, t):
            return MOD[:, g * NCH * T + c * T + t: g * NCH * T + c * T + t + 1]

        def modcol_bc(g, c, t, n):
            off = g * NCH * T + c * T + t
            return sap(MOD, off, [[MODW, 128], [0, n]])

        def gsl(ap, c, t2=None):
            if t2 is None:
                return ap[:, c * GPP:(c + 1) * GPP]
            return ap[:, c * GPP + t2 * PP: c * GPP + (t2 + 1) * PP]

        def b(f):
            return f

        P.emit("sync", lambda s: s.dma_start(out=VEC[:, :], in_=vecsd[:, :]), inc=ldm, inc_by=16)
        P.emit("sync", lambda s: s.dma_start(out=SCT[:, :], in_=sctd[:, :]), inc=ldm, inc_by=16)
        ld_msk = P.emit("sync", lambda s: s.dma_start(out=MASK[:, :], in_=masksd[:, :]), inc=ldm, inc_by=16)
        if dbg:
            P.emit("sync", lambda s: s.dma_start(out=dbg_dv[:, 112:128], in_=masksd[:, 0:16]), inc=ldm, inc_by=16)

        gp_ones = P.emit("gpsimd", lambda g: (g.memset(ONESB[:, :], 1.0),
                                              g.memset(ONESF[:, :], 1.0),
                                              g.memset(ONES5[:, :], 1.0),
                                              g.memset(CONSTS[:, :], 0.25),
                                              g.memset(ROWSTAGE[:, :], 0.0))[-1],
                         inc=s_gp)

        def deriv_a(s):
            s.mul(DVEC[:, 0:8], VEC[:, vcol["ba"]:vcol["ba"] + 8], 0.5)
            s.mul(DVEC[:, 8:16], VEC[:, vcol["bx"]:vcol["bx"] + 8], 0.5)
            s.mul(DVEC[:, 16:24], VEC[:, vcol["bf"]:vcol["bf"] + 8], 0.5)
            s.mul(DVEC[:, 24:32], VEC[:, vcol["bi"]:vcol["bi"] + 8], 0.5)
            return s.activation(VT[:, 0:8], VEC[:, vcol["lam"]:vcol["lam"] + 8], AF.Exp, scale=-1.0)
        d1v = P.emit("scalar", deriv_a, waits=[(ldm, ld_msk)], inc=s_sc)
        d2v = P.emit("scalar", lambda s: s.activation(VT[:, 8:16], VT[:, 0:8], AF.Ln, bias=1.0),
                     waits=[(s_sc, d1v)], inc=s_sc)
        d3v = P.emit("scalar", lambda s: s.mul(DVEC[:, 32:40], VT[:, 8:16], -4.0),
                     waits=[(s_sc, d2v)], inc=s_sc)
        d4v = P.emit("scalar", lambda s: s.mul(DVEC[:, 40:48], DVEC[:, 32:40], float(PP)),
                     waits=[(s_sc, d3v)], inc=s_sc)
        sc_deriv = P.emit("scalar", lambda s: s.activation(SCT2[:, :], SCT[:, :], AF.Silu),
                          waits=[(s_sc, d4v)], inc=s_sc)

        JCL = JC // NCORES
        MODL = JCL * T
        halfW = JCL * 128
        stage_slots = [Bx, Bh1]
        band_ld = []
        mm_hist = []
        acc_hist = []
        for k in range(NCH):
            s0 = stage_slots[k % 2]
            w = [(s_te, mm_hist[k - 2] if k >= 2 else None)]
            lw = ldw0 if k % 2 == 0 else ldw1
            a1 = P.emit("sync", (lambda s, k=k, s0=s0: s.dma_start(out=s0[:, 0:halfW], in_=wmodd[k * 128:(k + 1) * 128, :])),
                        waits=w, inc=lw, inc_by=16)

            def mmb(t_, k=k, s0=s0):
                last = None
                for jc in range(JCL):
                    last = t_.matmul(ps_mod[:, jc * T:(jc + 1) * T],
                                     s0[:, jc * 128:(jc + 1) * 128],
                                     SCT2[:, k * T:(k + 1) * T],
                                     start=True, stop=True)
                return last
            w_mm = [(lw, a1), (s_sc, sc_deriv)]
            if k > 0:
                w_mm.append((s_ve, acc_hist[k - 1]))
            mm_hist.append(P.emit("tensor", mmb, waits=w_mm, inc=s_te))

            def accb(v, k=k):
                sv = Y[:, 0:MODL]
                if k == 0:
                    return v.tensor_copy(sv, ps_mod[:, 0:MODL])
                src_, dst_ = (sv, MODL_BUF[:, 0:MODL]) if k % 2 == 1 else (MODL_BUF[:, 0:MODL], sv)
                return v.tensor_tensor(dst_, src_, ps_mod[:, 0:MODL], OP.add)
            acc_hist.append(P.emit("vector", accb, waits=[(s_te, mm_hist[k])], inc=s_ve))
        mm_after = mm_hist[-1]

        md1 = P.emit("gpsimd", lambda g_: g_.dma_start(out=modind[:, :], in_=MODL_BUF[:, 0:MODL]),
                     waits=[(s_ve, acc_hist[-1])], inc=gqc[0], inc_by=16)
        mag = P.emit("gpsimd", lambda g_: g_.collective_compute(
            "AllGather", OP.bypass, replica_groups=[list(range(NCORES))],
            ins=[modind.ap().opt()], outs=[modoutd.ap().opt()]),
            waits=[(gqc[0], md1)], inc=ccs)
        md2 = P.emit("gpsimd", lambda g_: g_.dma_start(
            out=MOD[:, :],
            in_=sap(modoutd, 0, [[MODL, 128], [128 * MODL, NCORES], [1, MODL]])),
            waits=[(ccs, mag)], inc=gqc[0], inc_by=16)
        acc_hist.append(P.emit("vector", lambda v: v.engine_nop(), waits=[(gqc[0], md2)], inc=s_ve))

        NT = NCH * T
        e1 = P.emit("vector", lambda v: (v.tensor_scalar_add(VT[:, 0:NT], MOD[:, 1 * NT:2 * NT], 1.0),
                                         v.tensor_scalar_add(VTB[:, 0:NT], MOD[:, 4 * NT:5 * NT], 1.0))[-1],
                    waits=[(s_ve, acc_hist[-1])], inc=s_ve)
        ve_add1 = P.emit("vector", lambda v: (v.tensor_copy(MOD[:, 1 * NT:2 * NT], VT[:, 0:NT]),
                                              v.tensor_copy(MOD[:, 4 * NT:5 * NT], VTB[:, 0:NT]))[-1],
                         waits=[(s_ve, e1)], inc=s_ve)
        ve_modcl = ve_add1
        for gsec in (2, 5):
            lo = gsec * NT
            sA = P.emit("scalar", (lambda s, lo=lo: (
                s.sign(VT[:, 0:NT], MOD[:, lo:lo + NT]),
                s.activation(VTB[:, 0:NT], MOD[:, lo:lo + NT], AF.Abs))[-1]),
                waits=[(s_ve, ve_modcl)], inc=s_sc)
            eM = P.emit("vector", lambda v: v.tensor_scalar_max(SCRA[:, 0:NT], VTB[:, 0:NT], 0.25),
                        waits=[(s_sc, sA), (s_ve, ve_modcl)], inc=s_ve)
            ve_modcl = P.emit("vector", (lambda v, lo=lo:
                              v.tensor_tensor(MOD[:, lo:lo + NT], VT[:, 0:NT], SCRA[:, 0:NT], OP.mult)),
                              waits=[(s_ve, eM)], inc=s_ve)
        if dbg:
            ve_modcl = P.emit("vector", lambda v: v.engine_nop(), waits=[(s_ve, ve_modcl)], inc=s_ve)
            dmod_e = P.emit("sync", lambda s: s.dma_start(out=dbg_mod[:, :], in_=MOD[:, :]),
                            waits=[(s_ve, ve_modcl)], inc=ldm, inc_by=16)
            ve_modcl = P.emit("vector", lambda v: v.engine_nop(), waits=[(ldm, dmod_e)], inc=s_ve)
        mod_ready_v = ve_modcl

        def load_weight(slot_idx, dram, waits_extra=()):
            done = None
            for k in range(NCH):
                half = load_weight.seq % 2
                load_weight.seq += 1
                w = [(s_sc, load_weight.cast_hist.get(load_weight.seq - 2))]
                if k < 2:
                    w.extend(waits_extra)
                lv = ldv0 if half == 0 else ldv1
                a = P.emit("sync", (lambda s, k=k, half=half, dram=dram:
                                    s.dma_start(out=WSTf[:, half * 1024:(half + 1) * 1024],
                                                in_=dram[k * 128:(k + 1) * 128, :])),
                           waits=w, inc=lv, inc_by=16)
                done = P.emit("scalar", (lambda s, k=k, half=half, si=slot_idx:
                                         s.copy(wslot[si][:, k * 1024:(k + 1) * 1024],
                                                WSTf[:, half * 1024:(half + 1) * 1024])),
                              waits=[(lv, a)], inc=s_sc)
                load_weight.cast_hist[load_weight.seq] = done
            return done
        load_weight.cast_hist = {}
        load_weight.seq = 0

        wa_done = load_weight(0, wd["WaT"])
        wx_done = load_weight(1, wd["WxT"])

        CBf = CB.ap().bitcast(F32)
        NGq = NG

        def bx_pieces(g):
            if g % 2 == 0:
                return [(Bx, 0, 8)]
            return [(C.ap(), 0, 4), (CBf, 4, 4)]

        def bxsl(g, c, t2=None):
            for ap, base, n in bx_pieces(g):
                if base <= c < base + n:
                    off = (c - base) * GPP
                    if t2 is None:
                        return ap[:, off: off + GPP]
                    return ap[:, off + t2 * PP: off + (t2 + 1) * PP]

        def bxhalf(g, h):
            for ap, base, n in bx_pieces(g):
                if base <= 4 * h < base + n:
                    off = (4 * h - base) * GPP
                    return ap[:, off: off + 4 * GPP]

        def hsl(c, t2=None):
            if t2 is None:
                return XBF[:, c * GPP:(c + 1) * GPP]
            return XBF[:, c * GPP + t2 * PP: c * GPP + (t2 + 1) * PP]

        def lanex(g):
            return ldx if g % 2 == 0 else ldx2[1]

        x_ld = [None] * NGq
        x_ld_half = [None] * NGq
        pcast = [None] * NGq
        ln2cast_h = [None] * NGq
        spill_done = [None] * NGq
        ln2mm = [None] * NGq
        ln2ev = [None] * NGq
        ln2sc = [None] * NGq
        ln2cast = [None] * NGq
        rows_done = [None] * NGq
        stats_of = [None] * NGq
        var_of = [None] * NGq
        rstd_of = [None] * NGq
        last_rescan = [None, None]
        prev_x2 = None
        prev_b = None
        b_hi_ve = None
        prev_u_last = None
        prev_carry_dma = [None, None]
        prev_agb_cmb = [None, None]
        prev_zx_te = None
        prev_zx_thx_last = None
        prev_extract_gp = None
        prev_dbgb = None
        prev_dbgh1 = None
        sc_sr2 = None
        sc_atot = None
        ve_h1 = None

        def emit_xload(g):
            halves = []
            for h in range(2):
                w = []
                if g == 0 and h == 0:
                    w.append((s_te, mm_after))
                if g >= 2:
                    w.append((s_gp, ln2cast_h[g - 2][h]))
                    if h == 0:
                        w.append((sqs, spill_done[g - 2]))
                v = P.emit("sync", (lambda s, g=g, h=h:
                                    s.dma_start(out=bxhalf(g, h),
                                                in_=xh[g][:, h * 4 * GPP:(h + 1) * 4 * GPP])),
                           waits=w, inc=lanex(g), inc_by=16)
                halves.append(v)
            x_ld_half[g] = halves
            x_ld[g] = halves[-1]

        def emit_prelude_casts(g):
            w_free = []
            if g == 1:
                w_free.append((s_te, stats_of[0]))
            elif g >= 2:
                w_free.append((s_te, ln2mm[g - 2]))
            for c in range(NCH):
                w = list(w_free) if c == 0 else []
                if c % 4 == 0:
                    w.append((lanex(g), x_ld_half[g][c // 4]))
                P.emit("gpsimd", (lambda g_, g2=g, c=c:
                                  g_.tensor_tensor(WST[:, c * GPP:(c + 1) * GPP], bxsl(g2, c),
                                                   ONES5[:, 0:GPP], OP.mult)),
                       waits=w, inc=s_gp)
            v = None
            for h in range(2):
                v = P.emit("gpsimd", (lambda g_, g2=g, h=h:
                                      g_.tensor_tensor(H2B[:, h * 4 * GPP:(h + 1) * 4 * GPP],
                                                       bxhalf(g2, h), bxhalf(g2, h), OP.mult)),
                           inc=s_gp)
            pcast[g] = v

        def emit_prelude_stats(g):
            def stats_mm(t_):
                last = None
                for c in range(NCH):
                    t_.matmul(ps_c[:, :], ONESB[:, :], WST[:, c * GPP:(c + 1) * GPP],
                              start=(c == 0), stop=(c == NCH - 1))
                    last = t_.matmul(ps_d[:, :], ONESB[:, :], H2B[:, c * GPP:(c + 1) * GPP],
                                     start=(c == 0), stop=(c == NCH - 1))
                return last
            w_st = [(s_gp, pcast[g]), (s_gp, gp_ones)]
            if g > 0:
                w_st.append((s_ve, var_of[g - 1]))
            stats_of[g] = P.emit("tensor", stats_mm, waits=w_st, inc=s_te)

            sc_mb = P.emit("scalar", lambda s: s.activation(MB[:, :], ps_c[:, :], AF.Copy, scale=1.0 / D),
                           waits=[(s_te, stats_of[g])], inc=s_sc)
            v1 = P.emit("vector", lambda v: v.tensor_tensor(VT[:, :], MB[:, :], MB[:, :], OP.mult),
                        waits=[(s_sc, sc_mb)], inc=s_ve)
            v2 = P.emit("vector", lambda v: v.tensor_scalar_sub(VTB[:, :], VT[:, :], 1e-5),
                        waits=[(s_ve, v1)], inc=s_ve)
            var_of[g] = P.emit("vector", lambda v: v.scalar_tensor_tensor(VAR[:, :], ps_d[:, :], 1.0 / D,
                                                                          VTB[:, :], OP.mult, OP.subtract),
                               waits=[(s_ve, v2)], inc=s_ve)

        def emit_nlw2(g):
            nv = P.emit("scalar", lambda s: s.activation(VT[:, :], VAR[:, :], AF.Ln),
                        waits=[(s_ve, var_of[g])], inc=s_sc)
            rstd_of[g] = P.emit("scalar", lambda s: s.activation(RSTD[:, :], VT[:, :], AF.Exp, scale=-0.5),
                                waits=[(s_sc, nv)], inc=s_sc)

        def emit_lnblock(g):
            nonlocal ve_h1
            lt1 = None
            for ap, base, n in bx_pieces(g):
                def ln_t1(v, ap=ap, base=base, n=n):
                    bx3 = ap[:, 0:n * GPP].rearrange("p (c q) -> p c q", c=n)
                    bh3 = XBF[:, base * GPP:(base + n) * GPP].rearrange("p (c q) -> p c q", c=n)
                    mb_bc = sap(MB, 0, [[GPP, 128], [0, n], [1, GPP]])
                    return v.tensor_tensor(bh3, bx3, mb_bc, OP.subtract)
                w_t1 = []
                if base == 0:
                    w_t1 = [(s_sc, rstd_of[g]), (lanex(g), x_ld[g])]
                    if dbg and prev_dbgh1 is not None:
                        w_t1.append((ldm, prev_dbgh1))
                lt1 = P.emit("vector", ln_t1, waits=w_t1, inc=s_ve)

            def ln_t2(v):
                bh3 = XBF[:, 0:GF].rearrange("p (c q) -> p c q", c=NCH)
                rs_bc = sap(RSTD, 0, [[GPP, 128], [0, NCH], [1, GPP]])
                return v.tensor_tensor(bh3, bh3, rs_bc, OP.mult)
            lt2 = P.emit("vector", ln_t2, waits=[(s_ve, lt1)], inc=s_ve)

            def h1_build(v, g=g):
                last = None
                for c in range(NCH):
                    for t2 in range(GT):
                        t = GT * g + t2
                        last = v.scalar_tensor_tensor(
                            hsl(c, t2), hsl(c, t2),
                            modcol(1, c, t), modcol_bc(0, c, t, PP),
                            OP.mult, OP.add)
                return last
            ve_h1 = P.emit("vector", h1_build, waits=[(s_ve, mod_ready_v), (s_ve, lt2)], inc=s_ve)

        def emit_tail_scanA(g):
            nonlocal prev_extract_gp
            hbuf = [Y, Baa]
            ag_done = {}
            agv_of = {}
            for t2 in range(GT):
                t = GT * g + t2
                par = t % 2
                hb = hbuf[t2]

                def scan1(v, t2=t2, hb=hb):
                    last = None
                    for c in range(NCH):
                        last = v.tensor_tensor_scan(hb[:, c * PP:(c + 1) * PP],
                                                    gsl(Ba, c, t2), gsl(Bh1, c, t2),
                                                    0.0, OP.mult, OP.add)
                    return last
                ve_s1 = P.emit("vector", scan1, waits=[(s_gp, prev_b), (s_ve, b_hi_ve)], inc=s_ve)

                def extract(g_, t2=t2, par=par, hb=hb):
                    for c in range(NCH):
                        g_.tensor_copy(CARRY[:, par * 16 + c: par * 16 + c + 1],
                                       hb[:, c * PP + PP - 1: c * PP + PP])
                    return g_.tensor_copy(sap(CARRY, par * 16 + 8, [[32, 128], [1, 8]]),
                                          sap(ATOT, t2, [[16, 128], [GT, 8]]))
                w_ex = [(s_ve, ve_s1), (s_sc, sc_atot)]
                if prev_carry_dma[par] is not None:
                    w_ex.append((gqc[par], prev_carry_dma[par]))
                gp_ex = P.emit("gpsimd", extract, waits=w_ex, inc=s_gp)
                if t2 == GT - 1:
                    prev_extract_gp = gp_ex

                d1 = P.emit("gpsimd", (lambda g_, t=t, par=par:
                                       g_.dma_start(out=cind[t], in_=CARRY[:, par * 16:(par + 1) * 16])),
                            waits=[(s_gp, gp_ex)], inc=gqc[par], inc_by=16)

                agv_of[t2] = P.emit("gpsimd", (lambda g_, t=t: g_.collective_compute(
                    "AllGather", OP.bypass,
                    replica_groups=[list(range(NCORES))],
                    ins=[cind[t].opt()],
                    outs=[coutd[t].opt()])),
                    waits=[(gqc[par], d1)], inc=ccs)
                prev_carry_dma[par] = d1

            for t2 in range(GT):
                t = GT * g + t2
                par = t % 2
                w_ab = [(ccs, agv_of[t2])]
                if prev_agb_cmb[par] is not None:
                    w_ab.append((s_ve, prev_agb_cmb[par]))
                d2 = P.emit("gpsimd", (lambda g_, t=t, par=par:
                                       g_.dma_start(out=AGB[:, par * 128:(par + 1) * 128],
                                                    in_=sap(coutd, t * 8 * 128 * 16, [[16, 128], [2048, 8], [1, 16]]))),
                            waits=w_ab, inc=gqa[par], inc_by=16)
                ag_done[t2] = d2
            return hbuf, ag_done

        def emit_tail_scanB(g, hbuf, ag_done):
            nonlocal prev_x2
            for t2 in range(GT):
                t = GT * g + t2
                par = t % 2
                hb = hbuf[t2]

                def cmb_a(v, par=par):
                    at_ap = sap(AGB, par * 128 + 8, [[256, 128], [1, 8], [16, 8]])
                    he_ap = sap(AGB, par * 128 + 0, [[256, 128], [1, 8], [16, 8]])
                    am = MASK[:, 0:64].rearrange("p (c j) -> p c j", c=8)
                    c1 = CMB1[:, :].rearrange("p (c j) -> p c j", c=8)
                    c2 = CMB2[:, :].rearrange("p (c j) -> p c j", c=8)
                    v.tensor_tensor(c2, he_ap, am, OP.mult)
                    return v.tensor_tensor(c1, at_ap, am, OP.mult)

                def cmb_b(v):
                    c1 = CMB1[:, :].rearrange("p (c j) -> p c j", c=8)
                    bm = MASK[:, 64:128].rearrange("p (c j) -> p c j", c=8)
                    return v.tensor_tensor(c1, c1, bm, OP.add)

                def cmb_c(v):
                    last = None
                    for c in range(NCH):
                        last = v.tensor_tensor_scan(CMB3[:, c * 8:(c + 1) * 8],
                                                    CMB1[:, c * 8:(c + 1) * 8],
                                                    CMB2[:, c * 8:(c + 1) * 8],
                                                    0.0, OP.mult, OP.add)
                    return last
                w_cmb = [(gqa[par], ag_done[t2])]
                if t < 2:
                    w_cmb.append((ldm, ld_msk))
                ca_ = P.emit("vector", cmb_a, waits=w_cmb, inc=s_ve)
                cb_ = P.emit("vector", cmb_b, waits=[(s_ve, ca_)], inc=s_ve)
                cc_ = P.emit("vector", cmb_c, waits=[(s_ve, cb_)], inc=s_ve)
                cmb = P.emit("vector", (lambda v, par=par:
                                        v.tensor_copy(HIN[:, par * 8:(par + 1) * 8],
                                                      sap(CMB3, 7, [[64, 128], [8, 8]]))),
                             waits=[(s_ve, cc_)], inc=s_ve)
                prev_agb_cmb[par] = cmb

                def rescan(v, t2=t2, par=par, hb=hb):
                    last = None
                    for c in range(NCH):
                        last = v.tensor_tensor_scan(hb[:, c * PP:(c + 1) * PP],
                                                    gsl(Ba, c, t2), gsl(Bh1, c, t2),
                                                    HIN[:, par * 8 + c: par * 8 + c + 1],
                                                    OP.mult, OP.add)
                    return last
                ve_rs = P.emit("vector", rescan, waits=[(s_ve, cmb)], inc=s_ve)
                last_rescan[t2] = ve_rs

                def x2b(v, t2=t2, g=g, hb=hb):
                    t = GT * g + t2
                    last = None
                    for c in range(NCH):
                        last = v.scalar_tensor_tensor(bxsl(g, c, t2), hb[:, c * PP:(c + 1) * PP],
                                                      modcol(2, c, t), bxsl(g, c, t2),
                                                      OP.mult, OP.add)
                    return last
                prev_x2 = P.emit("vector", x2b, waits=[(s_ve, ve_rs)], inc=s_ve)

                sp = None
                for pi, (ap, base, n) in enumerate(bx_pieces(g)):
                    w_sp = []
                    if pi == 0:
                        w_sp.append((s_ve, prev_x2))
                        if t2 == 0 and g > 0:
                            w_sp.append((sqs, spill_done[g - 1]))
                    def spl(s, ap=ap, base=base, n=n, t2=t2, t=t):
                        src = ap[:, 0:n * GPP].rearrange("p (c u q) -> p c u q", c=n, u=GT)[:, :, t2, :]
                        return s.dma_start(out=x2d[t][:, base * PP:(base + n) * PP], in_=src)
                    sp = P.emit("sync", spl, waits=w_sp, inc=sqs, inc_by=16)
                if t2 == GT - 1:
                    spill_done[g] = sp

        def emit_main(g):
            nonlocal prev_b, b_hi_ve, prev_u_last, prev_zx_te, prev_zx_thx_last
            nonlocal sc_sr2, sc_atot, prev_dbgb, prev_dbgh1
            if dbg:
                prev_dbgh1 = P.emit("sync", (lambda s, g=g: s.dma_start(out=dbg_h1[g], in_=XBF[:, 0:GF])),
                                    waits=[(s_ve, ve_h1)], inc=ldm, inc_by=16)
            w_h1b = [(s_ve, ve_h1)]
            if prev_zx_te is not None:
                w_h1b.append((s_te, prev_zx_te))
            gp_h1b = P.emit("scalar", lambda s: s.copy(Bh1b[:, 0:GF], XBF[:, 0:GF]),
                            waits=w_h1b, inc=s_sc)

            th_done = {}
            red_done = {}
            a_done = {}
            aa_done = {}
            z_te = {}
            for c in range(NCH):
                psz = ps_a if c % 2 == 0 else ps_b

                def zmm(t_, c=c, psz=psz):
                    last = None
                    for k in range(NCH):
                        last = t_.matmul(psz[:, :],
                                         wslot[0][:, k * 1024 + c * 128: k * 1024 + (c + 1) * 128],
                                         Bh1b[:, k * GPP:(k + 1) * GPP],
                                         start=(k == 0), stop=(k == NCH - 1))
                    return last
                w_z = [(s_sc, gp_h1b)] if c == 0 else []
                if c == 0:
                    w_z.append((s_sc, wa_done))
                    if g > 0:
                        w_z.append((s_sc, prev_zx_thx_last))
                if c >= 2:
                    w_z.append((s_sc, th_done[c - 2]))
                z_te[c] = P.emit("tensor", zmm, waits=w_z, inc=s_te)

                def tha(s, c=c, psz=psz):
                    return s.activation(SCRA[:, (c % 2) * GPP:(c % 2 + 1) * GPP], psz[:, :],
                                        AF.Tanh, scale=0.5, bias=dc("bah", c))
                w_th = [(s_te, z_te[c])]
                if c >= 2:
                    w_th.append((s_ve, red_done[c - 2]))
                    w_th.append((s_sc, a_done[c - 2]))
                th_done[c] = P.emit("scalar", tha, waits=w_th, inc=s_sc)

                w_a = [(s_sc, th_done[c])]
                if c == 0:
                    w_a.append((s_ve, last_rescan[1]))
                a_done[c] = P.emit("scalar",
                                   (lambda s, c=c: s.activation(gsl(Ba, c), SCRA[:, (c % 2) * GPP:(c % 2 + 1) * GPP],
                                                                AF.Exp, scale=dc("sh", c), bias=dc("sh", c))),
                                   waits=w_a, inc=s_sc)

                def red(v, c=c):
                    src = SCRA[:, (c % 2) * GPP:(c % 2 + 1) * GPP].rearrange("p (t q) -> p t q", t=GT)
                    return v.tensor_reduce(THS[:, c * GT:(c + 1) * GT], src, AX, OP.add)
                red_done[c] = P.emit("vector", red, waits=[(s_sc, th_done[c])], inc=s_ve)

                w_aa = [(s_sc, a_done[c])]
                if c == 0 and prev_x2 is not None:
                    w_aa.append((s_ve, prev_x2))
                aa_done[c] = P.emit("gpsimd",
                                    (lambda g_, c=c: g_.tensor_tensor(gsl(Baa, c), gsl(Ba, c), gsl(Ba, c), OP.mult)),
                                    waits=w_aa, inc=s_gp)

            u_done = {}
            thx_done = {}
            for c in range(NCH):
                psz = ps_a if c % 2 == 0 else ps_b

                def zmm(t_, c=c, psz=psz):
                    last = None
                    for k in range(NCH):
                        last = t_.matmul(psz[:, :],
                                         wslot[1][:, k * 1024 + c * 128: k * 1024 + (c + 1) * 128],
                                         Bh1b[:, k * GPP:(k + 1) * GPP],
                                         start=(k == 0), stop=(k == NCH - 1))
                    return last
                w_z = [(s_sc, th_done[NCH - 2 + (c % 2)] if c < 2 else thx_done[c - 2])]
                if c == 0:
                    w_z.append((s_sc, wx_done))
                zx_te = P.emit("tensor", zmm, waits=w_z, inc=s_te)
                if c == NCH - 1:
                    prev_zx_te = zx_te

                def thx(s, c=c, psz=psz):
                    return s.activation(SCRX[:, (c % 2) * GPP:(c % 2 + 1) * GPP], psz[:, :],
                                        AF.Tanh, scale=0.5, bias=dc("bxh", c))
                w_th = [(s_te, zx_te)]
                if c >= 2:
                    w_th.append((s_ve, u_done[c - 2]))
                thx_done[c] = P.emit("scalar", thx, waits=w_th, inc=s_sc)

                def ub(v, c=c):
                    return v.scalar_tensor_tensor(gsl(Bh1, c), SCRX[:, (c % 2) * GPP:(c % 2 + 1) * GPP],
                                                  1.0, hsl(c), OP.add, OP.mult)
                u_done[c] = P.emit("vector", ub, waits=[(s_sc, thx_done[c])], inc=s_ve)
            prev_zx_thx_last = thx_done[NCH - 1]
            prev_u_last = u_done[NCH - 1]

            def atot(s, g=g):
                last = None
                for c in range(NCH):
                    last = s.activation(ATOT[:, c * GT:(c + 1) * GT], THS[:, c * GT:(c + 1) * GT],
                                        AF.Exp, scale=dc("sh", c), bias=dc("s256", c))
                return last
            w_at = [(s_ve, red_done[NCH - 1])]
            if prev_extract_gp is not None:
                w_at.append((s_gp, prev_extract_gp))
            sc_atot = P.emit("scalar", atot, waits=w_at, inc=s_sc)

            H2 = GF // 2
            w_nl4a = [(s_gp, aa_done[NCH // 2 - 1]), (s_ve, u_done[NCH // 2 - 1]),
                      (s_gp, gp_ones)]
            if dbg:
                w_nl4a.append((ldm, prev_dbgh1))
            l1a = P.emit("scalar",
                         lambda s: s.activation(XBF[:, 0:H2], Baa[:, 0:H2], AF.Ln, scale=-0.25, bias=CONSTS[:, 0:1]),
                         waits=w_nl4a, inc=s_sc)
            l1b = P.emit("scalar",
                         lambda s: s.activation(XBF[:, H2:GF], Baa[:, H2:GF], AF.Ln, scale=-0.25, bias=CONSTS[:, 0:1]),
                         waits=[(s_gp, aa_done[NCH - 1]), (s_ve, u_done[NCH - 1])], inc=s_sc)
            P.emit("scalar",
                   lambda s: s.activation(Baa[:, 0:H2], XBF[:, 0:H2], AF.Exp, scale=0.5),
                   waits=[(s_sc, l1a)], inc=s_sc)
            sc_sr2 = P.emit("scalar",
                            lambda s: s.activation(Baa[:, H2:GF], XBF[:, H2:GF], AF.Exp, scale=0.5),
                            waits=[(s_sc, l1b)], inc=s_sc)

            if g >= 2:
                w_n1 = [(s_ve, ln2ev[g - 2])]
                if g >= 3 and rows_done[g - 3] is not None:
                    w_n1.append((sqr, rows_done[g - 3]))
                nb1 = P.emit("scalar", lambda s: s.activation(VTB[:, :], VAR2[:, :], AF.Ln),
                             waits=w_n1, inc=s_sc)
                nb2 = P.emit("scalar", lambda s: s.activation(RSTD2[:, :], VTB[:, :], AF.Exp, scale=-0.5),
                             waits=[(s_sc, nb1)], inc=s_sc)
                P.emit(SCQ, (lambda s, g=g: s.dma_start(out=rowsd[g - 2, 0], in_=MB2[0:1, :])),
                       waits=[(s_sc, nb2)], inc=sqr, inc_by=16)
                rows_done[g - 2] = P.emit(SCQ, (lambda s, g=g: s.dma_start(out=rowsd[g - 2, 1], in_=RSTD2[0:1, :])),
                                          inc=sqr, inc_by=16)

            H = GF // 2
            P.emit("vector",
                   lambda v: v.tensor_tensor(Bh1[:, H:GF], Bh1[:, H:GF], Baa[:, H:GF], OP.mult),
                   waits=[(s_sc, sc_sr2), (s_ve, u_done[NCH - 1])], inc=s_ve)
            b_hi_ve = P.val(s_ve)
            prev_b = P.emit("gpsimd",
                            lambda g_: g_.tensor_tensor(Bh1[:, 0:H], Bh1[:, 0:H], Baa[:, 0:H], OP.mult),
                            waits=[(s_sc, sc_sr2), (s_ve, u_done[NCH - 1])], inc=s_gp)

            if dbg:
                P.emit("sync", (lambda s, g=g: s.dma_start(out=dbg_a[g], in_=Ba[:, 0:GF])),
                       waits=[(s_gp, prev_b)], inc=ldm, inc_by=16)
                prev_dbgb = P.emit("sync", (lambda s, g=g: s.dma_start(out=dbg_b[g], in_=Bh1[:, 0:GF])),
                                   inc=ldm, inc_by=16)
                P.emit("vector", lambda v: v.engine_nop(), waits=[(ldm, prev_dbgb)])

        def emit_ln2_casts(g):
            w_c1 = [(s_ve, prev_x2)]
            if g + 1 < NGq:
                w_c1.append((s_te, stats_of[g + 1]))
            hv = []
            v = None
            for h in range(2):
                for c in range(4 * h, 4 * h + 4):
                    P.emit("gpsimd", (lambda g_, g2=g, c=c:
                                      g_.tensor_tensor(WST[:, c * GPP:(c + 1) * GPP], bxsl(g2, c),
                                                       ONES5[:, 0:GPP], OP.mult)),
                           waits=(w_c1 if (h == 0 and c == 0) else ()), inc=s_gp)
                v = P.emit("gpsimd", (lambda g_, g2=g, h=h:
                                      g_.tensor_tensor(H2B[:, h * 4 * GPP:(h + 1) * 4 * GPP],
                                                       bxhalf(g2, h), bxhalf(g2, h), OP.mult)),
                           inc=s_gp)
                hv.append(v)
            ln2cast_h[g] = hv
            ln2cast[g] = v

        def emit_ln2_mm(g):
            def stats2_mm(t_):
                last = None
                for c in range(NCH):
                    t_.matmul(ps_mod[:, 0:GPP], ONESB[:, :], WST[:, c * GPP:(c + 1) * GPP],
                              start=(c == 0), stop=(c == NCH - 1))
                    last = t_.matmul(ps_g[:, 0:GPP], ONESB[:, :], H2B[:, c * GPP:(c + 1) * GPP],
                                     start=(c == 0), stop=(c == NCH - 1))
                return last
            w_l2 = [(s_gp, ln2cast[g])]
            if g == 0:
                w_l2.append((s_ve, acc_hist[-1]))
            else:
                w_l2.append((s_ve, ln2ev[g - 1]))
                w_l2.append((s_sc, ln2sc[g - 1]))
            ln2mm[g] = P.emit("tensor", stats2_mm, waits=w_l2, inc=s_te)
            w_ls = [(s_te, ln2mm[g])]
            if P.val(sqr):
                w_ls.append((sqr, P.val(sqr)))
            ln2sc[g] = P.emit("scalar", lambda s: s.activation(MB2[:, :], ps_mod[:, 0:GPP], AF.Copy, scale=1.0 / D),
                              waits=w_ls, inc=s_sc)
            w1 = P.emit("vector", lambda v: v.tensor_tensor(VT[:, :], MB2[:, :], MB2[:, :], OP.mult),
                        waits=[(s_sc, ln2sc[g])], inc=s_ve)
            w2 = P.emit("vector", lambda v: v.tensor_scalar_sub(VTB[:, :], VT[:, :], 1e-5),
                        waits=[(s_ve, w1)], inc=s_ve)
            ln2ev[g] = P.emit("vector", lambda v: v.scalar_tensor_tensor(VAR2[:, :], ps_g[:, 0:GPP], 1.0 / D,
                                                                         VTB[:, :], OP.mult, OP.subtract),
                              waits=[(s_ve, w2)], inc=s_ve)

        emit_xload(0)
        emit_xload(1)
        tailst = None
        for g in range(NG):
            emit_prelude_casts(g)
            if g > 0:
                tailst = emit_tail_scanA(g - 1)
            emit_prelude_stats(g)
            emit_nlw2(g)
            emit_lnblock(g)
            if g > 0:
                emit_tail_scanB(g - 1, *tailst)
                emit_ln2_casts(g - 1)
            emit_main(g)
            if g > 0:
                emit_ln2_mm(g - 1)
            if 2 <= g + 1 < NG:
                emit_xload(g + 1)

        tailst = emit_tail_scanA(NG - 1)
        emit_tail_scanB(NG - 1, *tailst)
        emit_ln2_casts(NG - 1)

        def emit_rows_fin(gg):
            w_n1 = [(s_ve, ln2ev[gg])]
            if rows_done[gg - 1] is not None:
                w_n1.append((sqr, rows_done[gg - 1]))
            nf1 = P.emit("scalar", lambda s: s.activation(VTB[:, :], VAR2[:, :], AF.Ln),
                         waits=w_n1, inc=s_sc)
            sc_fin = P.emit("scalar", lambda s: s.activation(RSTD2[:, :], VTB[:, :], AF.Exp, scale=-0.5),
                            waits=[(s_sc, nf1)], inc=s_sc)
            P.emit(SCQ, (lambda s, gg=gg: s.dma_start(out=rowsd[gg, 0], in_=MB2[0:1, :])),
                   waits=[(s_sc, sc_fin)], inc=sqr, inc_by=16)
            rows_done[gg] = P.emit(SCQ, (lambda s, gg=gg: s.dma_start(out=rowsd[gg, 1], in_=RSTD2[0:1, :])),
                                   inc=sqr, inc_by=16)

        emit_rows_fin(NG - 2)
        emit_ln2_mm(NG - 1)
        emit_rows_fin(NG - 1)


        attn_done = [(s_ve, P.val(s_ve)), (s_gp, P.val(s_gp)),
                     (s_sc, P.val(s_sc)), (s_te, P.val(s_te))]
        attn_te_total = P.val(s_te)

        ffn_w = {}
        for i, n in enumerate(["wfT", "wiT", "wcT", "ufT", "uiT", "ucT"]):
            ffn_w[n] = load_weight(i, wd[n],
                                   waits_extra=(attn_done if i == 0 else ()))
        ffn_w_done = ffn_w["ucT"]

        w_c0 = list(attn_done) + [(sqs, P.val(sqs))]
        cz = P.emit("vector", lambda v: v.memset(Z3[:, 0:PP], 0.0), waits=w_c0, inc=s_ve)

        def c0_build(v):
            last = None
            for c in range(NCH):
                last = v.tensor_scalar_add(C[:, c * PP:(c + 1) * PP], Z3[:, 0:PP], vc("inc", c))
            return last
        c0b = P.emit("vector", c0_build, waits=[(s_ve, cz)], inc=s_ve)
        ve_c0 = P.emit("vector", lambda v: v.tensor_copy(CB[:, 0:TF], C[:, :]),
                       waits=[(s_ve, c0b)], inc=s_ve)

        sqs_total = P.val(sqs)
        sqr_total = P.val(sqr)
        x2_ld = [None, None]
        row_ld = [None, None]
        h2prep = [None, None]
        h2cast = [None, None]
        bcast_te = [None, None]
        gate_te_last = [None, None]
        ew_last = [None, None]
        prev_ew_all = ve_c0
        last_h2cast = None

        W3 = {0: "wfT", 1: "wiT", 2: "wcT"}
        U3 = {0: "ufT", 1: "uiT", 2: "ucT"}
        WIDX = {"wfT": 0, "wiT": 1, "wcT": 2, "ufT": 3, "uiT": 4, "ucT": 5}

        def prep_step(t):
            par = t % 2
            g = t // GT
            t2 = t % GT
            w_x = [(sqs, sqs_total)]
            if t < 2:
                w_x.append((s_te, attn_te_total))
            if h2prep[par] is not None:
                w_x.append((s_ve, h2prep[par]))
            x2_ld[par] = P.emit("sync", (lambda s, t=t, par=par:
                                         s.dma_start(out=XBF[:, par * 2048: par * 2048 + TF], in_=x2d[t])),
                                waits=w_x, inc=ldx2[par], inc_by=16)

            w_r = [(sqr, sqr_total)]
            if bcast_te[par] is not None:
                w_r.append((s_te, bcast_te[par]))
            P.emit("sync", (lambda s, g=g, t2=t2, par=par:
                            s.dma_start(out=ROWSTAGE[0:1, par * 2 * PP: par * 2 * PP + PP],
                                        in_=rowsd[g, 0, t2 * PP:(t2 + 1) * PP])),
                   waits=w_r, inc=ldr[par], inc_by=16)
            row_ld[par] = P.emit("sync", (lambda s, g=g, t2=t2, par=par:
                                          s.dma_start(out=ROWSTAGE[0:1, par * 2 * PP + PP: par * 2 * PP + 2 * PP],
                                                      in_=rowsd[g, 1, t2 * PP:(t2 + 1) * PP])),
                                 inc=ldr[par], inc_by=16)

            psbc = ps_c if par == 0 else ps_d
            w_bc = [(ldr[par], row_ld[par]), (s_sc, ffn_w_done)]
            if h2prep[par] is not None:
                w_bc.append((s_ve, h2prep[par]))
            if t == 0:
                w_bc.append((s_te, P.val(s_te)))
            bcast_te[par] = P.emit("tensor", (lambda t_, par=par, psbc=psbc:
                                              t_.matmul(psbc[:, 0:2 * PP], ONESF[:, :],
                                                        ROWSTAGE[:, par * 2 * PP:(par + 1) * 2 * PP],
                                                        start=True, stop=True)),
                                   waits=w_bc, inc=s_te)

            def h2p_a(v, par=par, psbc=psbc):
                xv = XBF[:, par * 2048: par * 2048 + TF].rearrange("p (c q) -> p c q", c=NCH)
                yv = Y[:, :].rearrange("p (c q) -> p c q", c=NCH)
                mb_bc = sap(psbc, 0, [[2 * PP, 128], [0, NCH], [1, PP]])
                return v.tensor_tensor(yv, xv, mb_bc, OP.subtract)

            def h2p_b(v, psbc=psbc):
                yv = Y[:, :].rearrange("p (c q) -> p c q", c=NCH)
                rs_bc = sap(psbc, PP, [[2 * PP, 128], [0, NCH], [1, PP]])
                return v.tensor_tensor(yv, yv, rs_bc, OP.mult)

            def h2p_c(v, t=t):
                last = None
                for c in range(NCH):
                    last = v.scalar_tensor_tensor(Y[:, c * PP:(c + 1) * PP], Y[:, c * PP:(c + 1) * PP],
                                                  modcol(4, c, t), modcol_bc(3, c, t, PP),
                                                  OP.mult, OP.add)
                return last
            w_h2 = [(s_te, bcast_te[par]), (ldx2[par], x2_ld[par])]
            if last_h2cast is not None:
                w_h2.append((s_sc, last_h2cast))
            if t == 0:
                w_h2.append((s_ve, ve_c0))
            hpa = P.emit("vector", h2p_a, waits=w_h2, inc=s_ve)
            hpb = P.emit("vector", h2p_b, waits=[(s_ve, hpa)], inc=s_ve)
            h2prep[par] = P.emit("vector", h2p_c, waits=[(s_ve, hpb)], inc=s_ve)

        def cast_step(t):
            nonlocal last_h2cast
            par = t % 2
            w_hc = [(s_ve, h2prep[par])]
            if gate_te_last[par] is not None:
                w_hc.append((s_te, gate_te_last[par]))
            h2cast[par] = P.emit("scalar", (lambda s, par=par:
                                            s.copy(H2B[:, par * 2048: par * 2048 + TF], Y[:, :])),
                                 waits=w_hc, inc=s_sc)
            last_h2cast = h2cast[par]

        prep_step(0)
        cast_step(0)

        for t in range(T):
            par = t % 2
            if t + 1 < T:
                prep_step(t + 1)

            ew_mc = {}
            sc_ev_mc = {}
            if t == 0:
                prev_psg_ev = {0: None, 1: None}
            for mc in range(NCH):
                psg = ps_g if mc % 2 == 0 else ps_mod
                def gmm_w(t_, gate, mc=mc, par=par, psg=psg):
                    last = None
                    for k in range(NCH):
                        mov = H2B[:, par * 2048 + k * PP: par * 2048 + (k + 1) * PP]
                        last = t_.matmul(psg[:, gate * PP:(gate + 1) * PP],
                                         wslot[WIDX[W3[gate]]][:, k * 1024 + mc * 128: k * 1024 + (mc + 1) * 128],
                                         mov, start=(k == 0), stop=False)
                    return last

                def gmm_u(t_, gate, mc=mc, par=par, psg=psg):
                    last = None
                    for k in range(NCH):
                        mov = CB[:, par * 2048 + k * PP: par * 2048 + (k + 1) * PP]
                        last = t_.matmul(psg[:, gate * PP:(gate + 1) * PP],
                                         wslot[WIDX[U3[gate]]][:, k * 1024 + mc * 128: k * 1024 + (mc + 1) * 128],
                                         mov, start=False, stop=(k == NCH - 1))
                    return last
                w_g = []
                if mc == 0:
                    w_g = [(s_sc, h2cast[par]), (s_sc, ffn_w_done)]
                    if t == 0:
                        w_g.append((s_ve, ln2ev[NG - 1]))
                        w_g.append((s_sc, ln2sc[NG - 1]))
                    if ew_last[par] is not None:
                        w_g.append((s_ve, ew_last[par]))
                if mc >= 2:
                    w_g.append((s_sc, sc_ev_mc[mc - 2]))
                    w_g.append((s_ve, ew_mc.get(mc - 2)))
                else:
                    w_g.append((s_sc, prev_psg_ev[mc % 2]))
                import functools as _ft
                P.emit("tensor", _ft.partial(gmm_w, gate=0), waits=w_g, inc=s_te)
                w_gu = []
                if mc == 0:
                    w_gu = [(s_ve, prev_ew_all)]
                gt = P.emit("tensor", _ft.partial(gmm_u, gate=0), waits=w_gu, inc=s_te)
                for gate in range(1, 3):
                    P.emit("tensor", _ft.partial(gmm_w, gate=gate), inc=s_te)
                    gt = P.emit("tensor", _ft.partial(gmm_u, gate=gate), inc=s_te)
                if mc == NCH - 1:
                    gate_te_last[par] = gt

                mp = mc % 2

                def gevac(s, mc=mc, mp=mp, psg=psg):
                    s.activation(THF[:, mp * PP:(mp + 1) * PP], psg[:, 0:PP], AF.Tanh,
                                 scale=0.5, bias=dc("bfh", mc))
                    s.activation(THI[:, mp * PP:(mp + 1) * PP], psg[:, PP:2 * PP], AF.Tanh,
                                 scale=0.5, bias=dc("bih", mc))
                    return s.activation(THG[:, mp * PP:(mp + 1) * PP], psg[:, 2 * PP:3 * PP], AF.Tanh,
                                        bias=vc("bc", mc))
                w_e = [(s_te, gt)]
                if mc >= 2:
                    w_e.append((s_ve, ew_mc[mc - 2]))
                sc_ev = P.emit("scalar", gevac, waits=w_e, inc=s_sc)
                sc_ev_mc[mc] = sc_ev
                if mc >= NCH - 2:
                    prev_psg_ev[mc % 2] = sc_ev

                def gew_a(v, mc=mc, mp=mp):
                    v.scalar_tensor_tensor(U1s[:, mp * PP:(mp + 1) * PP], THF[:, mp * PP:(mp + 1) * PP],
                                           1.0, C[:, mc * PP:(mc + 1) * PP], OP.add, OP.mult)
                    return v.scalar_tensor_tensor(U2s[:, mp * PP:(mp + 1) * PP], THI[:, mp * PP:(mp + 1) * PP],
                                                  1.0, THG[:, mp * PP:(mp + 1) * PP], OP.add, OP.mult)

                def gew_b(v, mc=mc, mp=mp):
                    return v.tensor_tensor(U1s[:, mp * PP:(mp + 1) * PP], U1s[:, mp * PP:(mp + 1) * PP],
                                           U2s[:, mp * PP:(mp + 1) * PP], OP.add)

                def gew_c(v, mc=mc, mp=mp, t=t):
                    v.tensor_scalar_mul(C[:, mc * PP:(mc + 1) * PP], U1s[:, mp * PP:(mp + 1) * PP], 0.5)
                    return v.tensor_scalar_mul(CB[:, (1 - t % 2) * 2048 + mc * PP: (1 - t % 2) * 2048 + (mc + 1) * PP],
                                               U1s[:, mp * PP:(mp + 1) * PP], 0.5)
                ga_ = P.emit("vector", gew_a, waits=[(s_sc, sc_ev)], inc=s_ve)
                gb_ = P.emit("vector", gew_b, waits=[(s_ve, ga_)], inc=s_ve)
                ew_mc[mc] = P.emit("vector", gew_c, waits=[(s_ve, gb_)], inc=s_ve)
            ew_last[par] = ew_mc[NCH - 1]
            prev_ew_all = ew_mc[NCH - 1]
            if t + 1 < T:
                cast_step(t + 1)

        uo_done = load_weight(1, wd["uoT"], waits_extra=[(s_te, gate_te_last[(T - 1) % 2])])
        wo_done = load_weight(0, wd["woT"], waits_extra=[(s_te, gate_te_last[(T - 1) % 2])])
        cfin_par = T % 2

        u_ev = {}
        for oc in range(NCH):
            pu = ps_c if oc % 2 == 0 else ps_d

            def umm(t_, oc=oc, pu=pu):
                last = None
                for k in range(NCH):
                    last = t_.matmul(pu[:, 0:PP],
                                     wslot[1][:, k * 1024 + oc * 128: k * 1024 + (oc + 1) * 128],
                                     CB[:, cfin_par * 2048 + k * PP: cfin_par * 2048 + (k + 1) * PP],
                                     start=(k == 0), stop=(k == NCH - 1))
                return last
            w_u = [(s_ve, prev_ew_all), (s_sc, uo_done)]
            if oc == 0:
                w_u.append((s_te, bcast_te[0]))
                w_u.append((s_te, bcast_te[1]))
            if oc >= 2:
                w_u.append((s_sc, u_ev[oc - 2]))
            ut = P.emit("tensor", umm, waits=w_u, inc=s_te)
            u_ev[oc] = P.emit("scalar", (lambda s, oc=oc, pu=pu:
                                         s.activation(C[:, oc * PP:(oc + 1) * PP], pu[:, 0:PP],
                                                      AF.Identity, bias=vc("bo", oc))),
                              waits=[(s_te, ut), (s_ve, prev_ew_all)], inc=s_sc)
        u_done_all = u_ev[NCH - 1]

        out_dma_cnt = 0
        prev_outdma = {}
        prev_h2o = [None, None]
        prev_h2co = [None, None]
        prev_bc_o = [None, None]
        prev_mm_o = {}
        prev_stt_o = {}
        for gp_ in range(NG):
            t0o = GT * gp_
            for t2 in range(GT):
                t = t0o + t2
                par = t2

                w_x = []
                if gp_ == 0:
                    w_x.append((s_sc, h2cast[par]))
                    w_x.append((s_ve, h2prep[par]))
                else:
                    w_x.append((s_ve, prev_h2o[par]))
                    w_x.append((s_ve, prev_stt_o[(gp_ - 1, NCH - 1, par)]))
                x2o = P.emit("sync", (lambda s, t=t, par=par:
                                      s.dma_start(out=XBF[:, par * 2048: par * 2048 + TF], in_=x2d[t])),
                             waits=w_x, inc=ldx2[par], inc_by=16)

                w_r = []
                if prev_bc_o[par] is not None:
                    w_r.append((s_te, prev_bc_o[par]))
                elif gp_ == 0:
                    w_r.append((s_te, bcast_te[par]))

                P.emit("sync", (lambda s, g=gp_, t2=t2, par=par:
                                s.dma_start(out=ROWSTAGE[0:1, par * 2 * PP: par * 2 * PP + PP],
                                            in_=rowsd[g, 0, t2 * PP:(t2 + 1) * PP])),
                       waits=w_r, inc=ldr[par], inc_by=16)
                rowo = P.emit("sync", (lambda s, g=gp_, t2=t2, par=par:
                                       s.dma_start(out=ROWSTAGE[0:1, par * 2 * PP + PP: par * 2 * PP + 2 * PP],
                                                   in_=rowsd[g, 1, t2 * PP:(t2 + 1) * PP])),
                              inc=ldr[par], inc_by=16)

                psbc = ps_c if par == 0 else ps_d
                w_bc = [(ldr[par], rowo), (s_sc, u_done_all)]
                if prev_h2o[par] is not None:
                    w_bc.append((s_ve, prev_h2o[par]))
                bco = P.emit("tensor", (lambda t_, par=par, psbc=psbc:
                                        t_.matmul(psbc[:, 0:2 * PP], ONESF[:, :],
                                                  ROWSTAGE[:, par * 2 * PP:(par + 1) * 2 * PP],
                                                  start=True, stop=True)),
                             waits=w_bc, inc=s_te)
                prev_bc_o[par] = bco

                def h2po_a(v, par=par, psbc=psbc):
                    xv = XBF[:, par * 2048: par * 2048 + TF].rearrange("p (c q) -> p c q", c=NCH)
                    yv = Y[:, :].rearrange("p (c q) -> p c q", c=NCH)
                    mb_bc = sap(psbc, 0, [[2 * PP, 128], [0, NCH], [1, PP]])
                    return v.tensor_tensor(yv, xv, mb_bc, OP.subtract)

                def h2po_b(v, psbc=psbc):
                    yv = Y[:, :].rearrange("p (c q) -> p c q", c=NCH)
                    rs_bc = sap(psbc, PP, [[2 * PP, 128], [0, NCH], [1, PP]])
                    return v.tensor_tensor(yv, yv, rs_bc, OP.mult)

                def h2po_c(v, t=t):
                    last = None
                    for c in range(NCH):
                        last = v.scalar_tensor_tensor(Y[:, c * PP:(c + 1) * PP], Y[:, c * PP:(c + 1) * PP],
                                                      modcol(4, c, t), modcol_bc(3, c, t, PP),
                                                      OP.mult, OP.add)
                    return last
                w_h2 = [(s_te, bco), (ldx2[par], x2o)]
                w_h2.append((s_sc, last_h2cast))
                if gp_ == 0 and t2 == 0:
                    w_h2.append((s_ve, prev_ew_all))
                hoa = P.emit("vector", h2po_a, waits=w_h2, inc=s_ve)
                hob = P.emit("vector", h2po_b, waits=[(s_ve, hoa)], inc=s_ve)
                h2o = P.emit("vector", h2po_c, waits=[(s_ve, hob)], inc=s_ve)
                prev_h2o[par] = h2o

                w_hc = [(s_ve, h2o)]
                if prev_mm_o.get(par) is not None:
                    w_hc.append((s_te, prev_mm_o[par]))
                h2co = P.emit("scalar", (lambda s, par=par:
                                         s.copy(H2B[:, par * 2048: par * 2048 + TF], Y[:, :])),
                              waits=w_hc, inc=s_sc)
                prev_h2co[par] = h2co
                last_h2cast = h2co

            def psout(t2, op_):
                if t2 == 0:
                    return (ps_a if op_ == 0 else ps_b)[:, 0:PP]
                return (ps_g if op_ == 0 else ps_mod)[:, 0:PP]
            mm_oc = {}
            for oc in range(NCH):
                def omm(t_, oc=oc):
                    last = None
                    for k in range(NCH):
                        for t2 in range(GT):
                            last = t_.matmul(psout(t2, oc % 2),
                                             wslot[0][:, k * 1024 + oc * 128: k * 1024 + (oc + 1) * 128],
                                             H2B[:, t2 * 2048 + k * PP: t2 * 2048 + (k + 1) * PP],
                                             start=(k == 0), stop=(k == NCH - 1))
                    return last
                w_m = [(s_sc, prev_h2co[1]), (s_sc, wo_done)]
                if gp_ == 0 and oc == 0:
                    w_m.append((s_ve, prev_ew_all))
                    w_m.append((s_sc, u_done_all))
                if oc >= 2:
                    w_m.append((s_ve, prev_stt_o[(gp_, oc - 2, 1)]))
                mm_oc[oc] = P.emit("tensor", omm, waits=w_m, inc=s_te)

                op_ = oc % 2
                for t2 in range(GT):
                    t = t0o + t2

                    def oev(v, oc=oc, op_=op_, t2=t2):
                        return v.tensor_tensor(Z3[:, op_ * PP:(op_ + 1) * PP],
                                               psout(t2, op_),
                                               C[:, oc * PP:(oc + 1) * PP], OP.add)
                    w_e = [(s_te, mm_oc[oc]), (s_sc, u_done_all)]
                    if (gp_, oc - 2, t2) in prev_stt_o or t2 > 0 or oc > 0 or gp_ > 0:
                        w_e.append((s_sc, P.val(s_sc)))
                    vz = P.emit("vector", oev, waits=w_e, inc=s_ve)
                    sz = P.emit("scalar", (lambda s, op_=op_: s.activation(Z3B[:, op_ * PP:(op_ + 1) * PP],
                                                                           Z3[:, op_ * PP:(op_ + 1) * PP], AF.Tanh)),
                                waits=[(s_ve, vz)], inc=s_sc)

                    def ostt(v, oc=oc, op_=op_, t=t, par=t2):
                        return v.scalar_tensor_tensor(U1s[:, op_ * PP:(op_ + 1) * PP],
                                                      Z3B[:, op_ * PP:(op_ + 1) * PP],
                                                      modcol(5, oc, t),
                                                      XBF[:, par * 2048 + oc * PP: par * 2048 + (oc + 1) * PP],
                                                      OP.mult, OP.add)
                    w_s = [(s_sc, sz)]
                    if prev_outdma.get(op_) is not None:
                        w_s.append((sqo[op_], prev_outdma[op_]))
                    so = P.emit("vector", ostt, waits=w_s, inc=s_ve)
                    prev_stt_o[(gp_, oc, t2)] = so

                    od = P.emit(SCQ, (lambda s, t=t, oc=oc, op_=op_:
                                           s.dma_start(out=outp[t][:, oc * PP:(oc + 1) * PP],
                                                       in_=U1s[:, op_ * PP:(op_ + 1) * PP])),
                                waits=[(s_ve, so)], inc=sqo[op_], inc_by=16)
                    prev_outdma[op_] = od
                    out_dma_cnt += 1

        if dbg:
            def dvd(s):
                s.dma_start(out=dbg_dv[:, 0:64], in_=VEC[:, :]).then_inc(ldm, 16)
                return s.dma_start(out=dbg_dv[:, 64:112], in_=DVEC[:, 0:48])
            P.cnt[id(ldm)] = P.val(ldm) + 16
            P.emit("sync", dvd, waits=[(sqs, P.val(sqs))], inc=ldm, inc_by=16)

        if dbg:
            first_w = [(sqs, P.val(sqs))]
            for t in range(T):
                P.emit("sync", (lambda s, t=t: s.dma_start(out=dbg_x2[t], in_=x2d[t])),
                       waits=(first_w if t == 0 else ()), inc=ldm, inc_by=16)
                P.emit("sync", (lambda s, t=t: s.dma_start(out=dbg_ci[t], in_=cind[t])),
                       inc=ldm, inc_by=16)
            P.emit("sync", lambda s: None, waits=[(ldm, P.val(ldm))])

        P.emit("sync", lambda s: None,
               waits=[(sqo[0], P.val(sqo[0])), (sqo[1], P.val(sqo[1])),
                      (sqs, P.val(sqs)), (sqr, P.val(sqr)), (ccs, P.val(ccs)),
                      (gqa[0], P.val(gqa[0])), (gqa[1], P.val(gqa[1]))])

        block = e(nc.Block())

        @block.sync
        def _(s):
            P.replay("sync", s)

        @block.tensor
        def _(t_):
            P.replay("tensor", t_)

        @block.vector
        def _(v):
            P.replay("vector", v)

        @block.scalar
        def _(s):
            P.replay("scalar", s)

        @block.gpsimd
        def _(g_):
            P.replay("gpsimd", g_)

    return nc



def _pack_vec(v):
    return np.ascontiguousarray(v.reshape(8, 128).T)


def prep_inputs(inputs, T, PP):
    NG = T // GT
    GF = NCH * GT * PP
    x = np.asarray(inputs["L0_attn_x"], dtype=np.float32)
    Wa = np.asarray(inputs["L0_attn_Wa"], np.float32)
    Wx = np.asarray(inputs["L0_attn_Wx"], np.float32)
    weights = {
        "WaT": np.ascontiguousarray(Wa.T), "WxT": np.ascontiguousarray(Wx.T),
        "wfT": np.ascontiguousarray(np.asarray(inputs["L0_ffn_wf"], np.float32).T),
        "wiT": np.ascontiguousarray(np.asarray(inputs["L0_ffn_wi"], np.float32).T),
        "wcT": np.ascontiguousarray(np.asarray(inputs["L0_ffn_wc"], np.float32).T),
        "ufT": np.ascontiguousarray(np.asarray(inputs["L0_ffn_uf"], np.float32).T),
        "uiT": np.ascontiguousarray(np.asarray(inputs["L0_ffn_ui"], np.float32).T),
        "ucT": np.ascontiguousarray(np.asarray(inputs["L0_ffn_uc"], np.float32).T),
        "woT": np.ascontiguousarray(np.asarray(inputs["L0_ffn_wo"], np.float32).T),
        "uoT": np.ascontiguousarray(np.asarray(inputs["L0_ffn_uo"], np.float32).T),
    }
    wmod_full = np.asarray(inputs["L0_wmod"], np.float32)
    cc = np.asarray(inputs["L0_c"], np.float32)
    scT = np.ascontiguousarray(cc.reshape(T, 8, 128).transpose(2, 1, 0).reshape(128, 8 * T))
    vecs = np.zeros((128, 64), np.float32)
    order = ["L0_attn_ba", "L0_attn_bx", "L0_ffn_bf", "L0_ffn_bi", "L0_ffn_bc",
             "L0_ffn_bo", "L0_ffn_in_c", "L0_attn_Lam"]
    for i, n in enumerate(order):
        vecs[:, 8 * i:8 * (i + 1)] = _pack_vec(np.asarray(inputs[n], np.float32))

    in_maps = []
    for core in range(NCORES):
        s0 = core * PP
        xc = x[:, s0:s0 + PP, :]
        xr = xc.reshape(NG, GT, PP, NCH, 128)
        xhost = np.ascontiguousarray(xr.transpose(0, 4, 3, 1, 2).reshape(NG, 128, GF))
        masks = np.zeros((128, 128), np.float32)
        for c in range(8):
            for j in range(8):
                masks[:, c * 8 + j] = 1.0 if j < core else 0.0
                masks[:, 64 + c * 8 + j] = 0.0 if j < core else 1.0
        m = {"xh": xhost, "scT": scT, "vecs": vecs, "cmask": masks,
             "wmod": np.ascontiguousarray(wmod_full[:, core * 768:(core + 1) * 768])}
        m.update(weights)
        in_maps.append(m)
    return in_maps


def assemble(results, T, PP):
    S = NCORES * PP
    out = np.empty((T, S, D), np.float32)
    for core in range(NCORES):
        oc = results[core]["outp"]
        o = oc.reshape(T, 128, NCH, PP).transpose(0, 3, 2, 1).reshape(T, PP, D)
        out[:, core * PP:(core + 1) * PP, :] = o
    return out


def kernel(**inputs):
    T, PP = 16, 256
    nc = build_nc(T, PP)
    in_maps = prep_inputs(inputs, T, PP)
    res = run_bass_kernel_spmd(nc, in_maps, core_ids=list(range(NCORES)))
    return assemble(res.results, T, PP)


# version 13
# speedup vs baseline: 1.0613x; 1.0613x over previous
import os as _os
import numpy as np
import concourse.bass as bass
import concourse.mybir as mybir
from concourse.bass_utils import run_bass_kernel_spmd
from contextlib import ExitStack

F32 = mybir.dt.float32
BF16 = mybir.dt.bfloat16
AF = mybir.ActivationFunctionType
OP = mybir.AluOpType
AX = mybir.AxisListType.X

NCORES = 8
D = 1024
NCH = 8
GT = 2
LN2C = 0.6931471805599453

ENGS = ("sync", "tensor", "vector", "scalar", "gpsimd")


PARANOID = _os.environ.get("KPARANOID", "0") == "1"
SCQ = "sync" if _os.environ.get("KSYNCQ", "0") == "1" else "scalar"


class Prog:

    def __init__(self):
        self.ops = {e: [] for e in ENGS}
        self.cnt = {}
        self.esem = {}

    def set_esems(self, m):
        self.esem = m

    def val(self, sem):
        return self.cnt.get(id(sem), 0)

    def emit(self, eng, builder, waits=(), inc=None, inc_by=1):
        waits = list(waits)
        if PARANOID and self.esem:
            for k in ("te", "ve", "sc", "gp"):
                s = self.esem[k]
                waits.append((s, self.val(s)))
            if inc is None and eng in self.esem_by_eng:
                inc = self.esem_by_eng[eng]
                inc_by = 1
        after = None
        if inc is not None:
            after = self.val(inc) + inc_by
            self.cnt[id(inc)] = after
        self.ops[eng].append((waits, builder, inc, inc_by))
        return after

    def replay(self, eng_name, e):
        for waits, builder, inc, inc_by in self.ops[eng_name]:
            for sem, v in waits:
                if v is None or v == 0:
                    continue
                e.wait_ge(sem, v)
            ins = builder(e)
            if inc is not None:
                if ins is None:
                    if PARANOID:
                        continue
                    assert False, "builder for inc op returned no instruction"
                ins.then_inc(inc, inc_by)


def sap(t, off, dims):
    return bass.AP(t, off, dims)


def build_nc(T, PP, dbg=False):
    NG = T // GT
    GPP = GT * PP
    GF = NCH * GPP
    TF = NCH * PP
    JC = 6 * D // 128
    MODW = JC * T

    nc = bass.Bass("TRN2", target_bir_lowering=False, detect_race_conditions=False)

    xh = nc.dram_tensor("xh", [NG, 128, GF], F32, kind="ExternalInput")
    wnames = ["WaT", "WxT", "wfT", "wiT", "wcT", "ufT", "uiT", "ucT", "woT", "uoT"]
    wd = {n: nc.dram_tensor(n, [D, D], F32, kind="ExternalInput") for n in wnames}
    wmodd = nc.dram_tensor("wmod", [D, 6 * D // NCORES], F32, kind="ExternalInput")
    sctd = nc.dram_tensor("scT", [128, NCH * T], F32, kind="ExternalInput")
    vecsd = nc.dram_tensor("vecs", [128, 64], F32, kind="ExternalInput")
    masksd = nc.dram_tensor("cmask", [128, 128], F32, kind="ExternalInput")
    outp = nc.dram_tensor("outp", [T, 128, TF], F32, kind="ExternalOutput")
    x2d = nc.dram_tensor("x2d", [T, 128, TF], F32)
    rowsd = nc.dram_tensor("rowsd", [NG, 2, GPP], F32)
    cind = nc.dram_tensor("cin", [T, 128, 16], F32)
    coutd = nc.dram_tensor("cout", [T, 8, 128, 16], F32)
    modind = nc.dram_tensor("modin", [128, 6 * NCH * T // NCORES], F32)
    modoutd = nc.dram_tensor("modout", [NCORES, 128, 6 * NCH * T // NCORES], F32)
    dbg_x2 = nc.dram_tensor("dbg_x2", [T, 128, NCH * PP], F32, kind="ExternalOutput") if dbg else None
    dbg_a = nc.dram_tensor("dbg_a", [T // GT, 128, NCH * GT * PP], F32, kind="ExternalOutput") if dbg else None
    dbg_b = nc.dram_tensor("dbg_b", [T // GT, 128, NCH * GT * PP], F32, kind="ExternalOutput") if dbg else None
    dbg_ci = nc.dram_tensor("dbg_ci", [T, 128, 16], F32, kind="ExternalOutput") if dbg else None
    dbg_h1 = nc.dram_tensor("dbg_h1", [T // GT, 128, NCH * GT * PP], F32, kind="ExternalOutput") if dbg else None
    dbg_dv = nc.dram_tensor("dbg_dv", [128, 128], F32, kind="ExternalOutput") if dbg else None
    dbg_cmb = nc.dram_tensor("dbg_cmb", [128, 512], F32, kind="ExternalOutput") if dbg else None
    dbg_mod = nc.dram_tensor("dbg_mod", [128, 6 * NCH * T], F32, kind="ExternalOutput") if dbg else None

    with ExitStack() as stack:
        e = stack.enter_context

        wslot = [e(nc.sbuf_tensor(f"wslot{i}", [128, 8192], BF16)) for i in range(6)]
        Bx = wslot[2].ap().bitcast(F32)
        Bh1 = wslot[3].ap().bitcast(F32)
        Ba = wslot[4].ap().bitcast(F32)
        Baa = wslot[5].ap().bitcast(F32)
        Bh1b = e(nc.sbuf_tensor("Bh1b", [128, 4096], BF16))
        XBIG = e(nc.sbuf_tensor("XBIG", [128, 8192], BF16))
        XBF = XBIG.ap().bitcast(F32)
        SCRA = e(nc.sbuf_tensor("SCRA", [128, 2 * GPP], F32))
        SCRX = e(nc.sbuf_tensor("SCRX", [128, 2 * GPP], F32))
        Y = e(nc.sbuf_tensor("Y", [128, TF], F32))
        MB = e(nc.sbuf_tensor("MB", [128, GPP], F32))
        RSTD = e(nc.sbuf_tensor("RSTD", [128, GPP], F32))
        MB2 = e(nc.sbuf_tensor("MB2", [128, GPP], F32))
        RSTD2 = e(nc.sbuf_tensor("RSTD2", [128, GPP], F32))
        VAR = e(nc.sbuf_tensor("VAR", [128, GPP], F32))
        VAR2 = e(nc.sbuf_tensor("VAR2", [128, GPP], F32))
        VT = e(nc.sbuf_tensor("VT", [128, GPP], F32))
        VTB = e(nc.sbuf_tensor("VTB", [128, GPP], F32))
        CARRY = e(nc.sbuf_tensor("CARRY", [128, 32], F32))
        AGB = e(nc.sbuf_tensor("AGB", [128, 256], F32))
        HIN = e(nc.sbuf_tensor("HIN", [128, 16], F32))
        CMB1 = e(nc.sbuf_tensor("CMB1", [128, 64], F32))
        CMB2 = e(nc.sbuf_tensor("CMB2", [128, 64], F32))
        CMB3 = e(nc.sbuf_tensor("CMB3", [128, 64], F32))
        ATOT = e(nc.sbuf_tensor("ATOT", [128, 16], F32))
        THS = e(nc.sbuf_tensor("THS", [128, 16], F32))
        MOD = e(nc.sbuf_tensor("MOD", [128, MODW], F32))
        SCT = e(nc.sbuf_tensor("SCT", [128, NCH * T], F32))
        SCT2 = e(nc.sbuf_tensor("SCT2", [128, NCH * T], F32))
        VEC = e(nc.sbuf_tensor("VEC", [128, 64], F32))
        DVEC = e(nc.sbuf_tensor("DVEC", [128, 64], F32))
        MASK = e(nc.sbuf_tensor("MASK", [128, 128], F32))
        ONESB = e(nc.sbuf_tensor("ONESB", [128, 128], BF16))
        ONESF = e(nc.sbuf_tensor("ONESF", [128, 128], F32))
        ONES5 = e(nc.sbuf_tensor("ONES5", [128, 512], F32))
        WST = e(nc.sbuf_tensor("WST", [128, 4096], BF16))
        WSTf = WST.ap().bitcast(F32)
        ROWSTAGE = e(nc.sbuf_tensor("ROWSTAGE", [128, 4 * PP], F32))
        C = e(nc.sbuf_tensor("C", [128, TF], F32))
        CB = e(nc.sbuf_tensor("CB", [128, 2 * 2048], BF16))
        H2B = e(nc.sbuf_tensor("H2B", [128, 2 * 2048], BF16))
        THF = e(nc.sbuf_tensor("THF", [128, 2 * PP], F32))
        THI = e(nc.sbuf_tensor("THI", [128, 2 * PP], F32))
        THG = e(nc.sbuf_tensor("THG", [128, 2 * PP], F32))
        U1s = e(nc.sbuf_tensor("U1s", [128, 2 * PP], F32))
        U2s = e(nc.sbuf_tensor("U2s", [128, 2 * PP], F32))
        Z3 = THF.ap()
        Z3B = U2s.ap()
        CONSTS = e(nc.sbuf_tensor("CONSTS", [128, 1], F32))

        ps_a = e(nc.psum_tensor("ps_a", [128, GPP], F32))
        ps_b = e(nc.psum_tensor("ps_b", [128, GPP], F32))
        ps_c = e(nc.psum_tensor("ps_c", [128, GPP], F32))
        ps_d = e(nc.psum_tensor("ps_d", [128, GPP], F32))
        ps_mod = e(nc.psum_tensor("ps_mod", [128, max(MODW, 3 * PP)], F32))
        ps_g = e(nc.psum_tensor("ps_g", [128, 3 * PP], F32))

        ldm = e(nc.semaphore("ldm"))
        ldw0 = e(nc.semaphore("ldw0"))
        ldw1 = e(nc.semaphore("ldw1"))
        ldv0 = e(nc.semaphore("ldv0"))
        ldv1 = e(nc.semaphore("ldv1"))
        ldx = e(nc.semaphore("ldx"))
        ldx2 = [e(nc.semaphore("ldx2_0")), e(nc.semaphore("ldx2_1"))]
        ldr = [e(nc.semaphore("ldr0")), e(nc.semaphore("ldr1"))]
        sqs = e(nc.semaphore("sqs"))
        sqr = e(nc.semaphore("sqr"))
        sqo = [e(nc.semaphore("sqo0")), e(nc.semaphore("sqo1"))]
        gqc = [e(nc.semaphore("gqc0")), e(nc.semaphore("gqc1"))]
        gqa = [e(nc.semaphore("gqa0")), e(nc.semaphore("gqa1"))]
        ccs = e(nc.semaphore("ccs"))
        s_te = e(nc.semaphore("s_te"))
        s_ve = e(nc.semaphore("s_ve"))
        s_sc = e(nc.semaphore("s_sc"))
        s_gp = e(nc.semaphore("s_gp"))

        MODL_BUF = VT
        P = Prog()
        P.esem_by_eng = {"tensor": s_te, "vector": s_ve, "scalar": s_sc, "gpsimd": s_gp}
        P.set_esems({"te": s_te, "ve": s_ve, "sc": s_sc, "gp": s_gp,
                     "ldm": ldm, "ldw0": ldw0, "ldw1": ldw1, "ldv0": ldv0, "ldv1": ldv1,
                     "ldx": ldx, "ldx20": ldx2[0], "ldx21": ldx2[1],
                     "ldr0": ldr[0], "ldr1": ldr[1], "sqs": sqs, "sqr": sqr,
                     "sqo0": sqo[0], "sqo1": sqo[1], "gqc0": gqc[0], "gqc1": gqc[1],
                     "gqa0": gqa[0], "gqa1": gqa[1], "ccs": ccs})

        VORD = ["ba", "bx", "bf", "bi", "bc", "bo", "inc", "lam"]
        vcol = {n: 8 * i for i, n in enumerate(VORD)}
        dcol = {"bah": 0, "bxh": 8, "bfh": 16, "bih": 24, "sh": 32, "s256": 40}

        def vc(name, c):
            return VEC[:, vcol[name] + c: vcol[name] + c + 1]

        def dc(name, c):
            return DVEC[:, dcol[name] + c: dcol[name] + c + 1]

        def modcol(g, c, t):
            return MOD[:, g * NCH * T + c * T + t: g * NCH * T + c * T + t + 1]

        def modcol_bc(g, c, t, n):
            off = g * NCH * T + c * T + t
            return sap(MOD, off, [[MODW, 128], [0, n]])

        def gsl(ap, c, t2=None):
            if t2 is None:
                return ap[:, c * GPP:(c + 1) * GPP]
            return ap[:, c * GPP + t2 * PP: c * GPP + (t2 + 1) * PP]

        def b(f):
            return f

        P.emit("sync", lambda s: s.dma_start(out=VEC[:, :], in_=vecsd[:, :]), inc=ldm, inc_by=16)
        P.emit("sync", lambda s: s.dma_start(out=SCT[:, :], in_=sctd[:, :]), inc=ldm, inc_by=16)
        ld_msk = P.emit("sync", lambda s: s.dma_start(out=MASK[:, :], in_=masksd[:, :]), inc=ldm, inc_by=16)
        if dbg:
            P.emit("sync", lambda s: s.dma_start(out=dbg_dv[:, 112:128], in_=masksd[:, 0:16]), inc=ldm, inc_by=16)

        gp_ones = P.emit("gpsimd", lambda g: (g.memset(ONESB[:, :], 1.0),
                                              g.memset(ONESF[:, :], 1.0),
                                              g.memset(ONES5[:, :], 1.0),
                                              g.memset(CONSTS[:, :], 0.25),
                                              g.memset(ROWSTAGE[:, :], 0.0))[-1],
                         inc=s_gp)

        def deriv_a(s):
            s.mul(DVEC[:, 0:8], VEC[:, vcol["ba"]:vcol["ba"] + 8], 0.5)
            s.mul(DVEC[:, 8:16], VEC[:, vcol["bx"]:vcol["bx"] + 8], 0.5)
            s.mul(DVEC[:, 16:24], VEC[:, vcol["bf"]:vcol["bf"] + 8], 0.5)
            s.mul(DVEC[:, 24:32], VEC[:, vcol["bi"]:vcol["bi"] + 8], 0.5)
            return s.activation(VT[:, 0:8], VEC[:, vcol["lam"]:vcol["lam"] + 8], AF.Exp, scale=-1.0)
        d1v = P.emit("scalar", deriv_a, waits=[(ldm, ld_msk)], inc=s_sc)
        d2v = P.emit("scalar", lambda s: s.activation(VT[:, 8:16], VT[:, 0:8], AF.Ln, bias=1.0),
                     waits=[(s_sc, d1v)], inc=s_sc)
        d3v = P.emit("scalar", lambda s: s.mul(DVEC[:, 32:40], VT[:, 8:16], -4.0),
                     waits=[(s_sc, d2v)], inc=s_sc)
        d4v = P.emit("scalar", lambda s: s.mul(DVEC[:, 40:48], DVEC[:, 32:40], float(PP)),
                     waits=[(s_sc, d3v)], inc=s_sc)
        sc_deriv = P.emit("scalar", lambda s: s.activation(SCT2[:, :], SCT[:, :], AF.Silu),
                          waits=[(s_sc, d4v)], inc=s_sc)

        JCL = JC // NCORES
        MODL = JCL * T
        halfW = JCL * 128
        stage_slots = [Bx, Bh1]
        band_ld = []
        mm_hist = []
        acc_hist = []
        for k in range(NCH):
            s0 = stage_slots[k % 2]
            w = [(s_te, mm_hist[k - 2] if k >= 2 else None)]
            lw = ldw0 if k % 2 == 0 else ldw1
            a1 = P.emit("sync", (lambda s, k=k, s0=s0: s.dma_start(out=s0[:, 0:halfW], in_=wmodd[k * 128:(k + 1) * 128, :])),
                        waits=w, inc=lw, inc_by=16)

            def mmb(t_, k=k, s0=s0):
                last = None
                for jc in range(JCL):
                    last = t_.matmul(ps_mod[:, jc * T:(jc + 1) * T],
                                     s0[:, jc * 128:(jc + 1) * 128],
                                     SCT2[:, k * T:(k + 1) * T],
                                     start=True, stop=True)
                return last
            w_mm = [(lw, a1), (s_sc, sc_deriv)]
            if k > 0:
                w_mm.append((s_ve, acc_hist[k - 1]))
            mm_hist.append(P.emit("tensor", mmb, waits=w_mm, inc=s_te))

            def accb(v, k=k):
                sv = Y[:, 0:MODL]
                if k == 0:
                    return v.tensor_copy(sv, ps_mod[:, 0:MODL])
                src_, dst_ = (sv, MODL_BUF[:, 0:MODL]) if k % 2 == 1 else (MODL_BUF[:, 0:MODL], sv)
                return v.tensor_tensor(dst_, src_, ps_mod[:, 0:MODL], OP.add)
            acc_hist.append(P.emit("vector", accb, waits=[(s_te, mm_hist[k])], inc=s_ve))
        mm_after = mm_hist[-1]

        md1 = P.emit("gpsimd", lambda g_: g_.dma_start(out=modind[:, :], in_=MODL_BUF[:, 0:MODL]),
                     waits=[(s_ve, acc_hist[-1])], inc=gqc[0], inc_by=16)
        mag = P.emit("gpsimd", lambda g_: g_.collective_compute(
            "AllGather", OP.bypass, replica_groups=[list(range(NCORES))],
            ins=[modind.ap().opt()], outs=[modoutd.ap().opt()]),
            waits=[(gqc[0], md1)], inc=ccs)
        md2 = P.emit("gpsimd", lambda g_: g_.dma_start(
            out=MOD[:, :],
            in_=sap(modoutd, 0, [[MODL, 128], [128 * MODL, NCORES], [1, MODL]])),
            waits=[(ccs, mag)], inc=gqc[0], inc_by=16)
        acc_hist.append(P.emit("vector", lambda v: v.engine_nop(), waits=[(gqc[0], md2)], inc=s_ve))

        NT = NCH * T
        e1 = P.emit("vector", lambda v: (v.tensor_scalar_add(VT[:, 0:NT], MOD[:, 1 * NT:2 * NT], 1.0),
                                         v.tensor_scalar_add(VTB[:, 0:NT], MOD[:, 4 * NT:5 * NT], 1.0))[-1],
                    waits=[(s_ve, acc_hist[-1])], inc=s_ve)
        ve_add1 = P.emit("vector", lambda v: (v.tensor_copy(MOD[:, 1 * NT:2 * NT], VT[:, 0:NT]),
                                              v.tensor_copy(MOD[:, 4 * NT:5 * NT], VTB[:, 0:NT]))[-1],
                         waits=[(s_ve, e1)], inc=s_ve)
        ve_modcl = ve_add1
        for gsec in (2, 5):
            lo = gsec * NT
            sA = P.emit("scalar", (lambda s, lo=lo: (
                s.sign(VT[:, 0:NT], MOD[:, lo:lo + NT]),
                s.activation(VTB[:, 0:NT], MOD[:, lo:lo + NT], AF.Abs))[-1]),
                waits=[(s_ve, ve_modcl)], inc=s_sc)
            eM = P.emit("vector", lambda v: v.tensor_scalar_max(SCRA[:, 0:NT], VTB[:, 0:NT], 0.25),
                        waits=[(s_sc, sA), (s_ve, ve_modcl)], inc=s_ve)
            ve_modcl = P.emit("vector", (lambda v, lo=lo:
                              v.tensor_tensor(MOD[:, lo:lo + NT], VT[:, 0:NT], SCRA[:, 0:NT], OP.mult)),
                              waits=[(s_ve, eM)], inc=s_ve)
        if dbg:
            ve_modcl = P.emit("vector", lambda v: v.engine_nop(), waits=[(s_ve, ve_modcl)], inc=s_ve)
            dmod_e = P.emit("sync", lambda s: s.dma_start(out=dbg_mod[:, :], in_=MOD[:, :]),
                            waits=[(s_ve, ve_modcl)], inc=ldm, inc_by=16)
            ve_modcl = P.emit("vector", lambda v: v.engine_nop(), waits=[(ldm, dmod_e)], inc=s_ve)
        mod_ready_v = ve_modcl

        def load_weight(slot_idx, dram, waits_extra=()):
            done = None
            for k in range(NCH):
                half = load_weight.seq % 2
                load_weight.seq += 1
                w = [(s_sc, load_weight.cast_hist.get(load_weight.seq - 2))]
                if k < 2:
                    w.extend(waits_extra)
                lv = ldv0 if half == 0 else ldv1
                a = P.emit("sync", (lambda s, k=k, half=half, dram=dram:
                                    s.dma_start(out=WSTf[:, half * 1024:(half + 1) * 1024],
                                                in_=dram[k * 128:(k + 1) * 128, :])),
                           waits=w, inc=lv, inc_by=16)
                done = P.emit("scalar", (lambda s, k=k, half=half, si=slot_idx:
                                         s.copy(wslot[si][:, k * 1024:(k + 1) * 1024],
                                                WSTf[:, half * 1024:(half + 1) * 1024])),
                              waits=[(lv, a)], inc=s_sc)
                load_weight.cast_hist[load_weight.seq] = done
            return done
        load_weight.cast_hist = {}
        load_weight.seq = 0

        wa_done = load_weight(0, wd["WaT"])
        wx_done = load_weight(1, wd["WxT"])

        CBf = CB.ap().bitcast(F32)
        NGq = NG

        def bx_pieces(g):
            if g % 2 == 0:
                return [(Bx, 0, 8)]
            return [(C.ap(), 0, 4), (CBf, 4, 4)]

        def bxsl(g, c, t2=None):
            for ap, base, n in bx_pieces(g):
                if base <= c < base + n:
                    off = (c - base) * GPP
                    if t2 is None:
                        return ap[:, off: off + GPP]
                    return ap[:, off + t2 * PP: off + (t2 + 1) * PP]

        def bxhalf(g, h):
            for ap, base, n in bx_pieces(g):
                if base <= 4 * h < base + n:
                    off = (4 * h - base) * GPP
                    return ap[:, off: off + 4 * GPP]

        def hsl(c, t2=None):
            if t2 is None:
                return XBF[:, c * GPP:(c + 1) * GPP]
            return XBF[:, c * GPP + t2 * PP: c * GPP + (t2 + 1) * PP]

        def lanex(g):
            return ldx if g % 2 == 0 else ldx2[1]

        x_ld = [None] * NGq
        x_ld_half = [None] * NGq
        pcast = [None] * NGq
        ln2cast_h = [None] * NGq
        spill_done = [None] * NGq
        ln2mm = [None] * NGq
        ln2ev = [None] * NGq
        ln2sc = [None] * NGq
        ln2cast = [None] * NGq
        rows_done = [None] * NGq
        stats_of = [None] * NGq
        var_of = [None] * NGq
        rstd_of = [None] * NGq
        last_rescan = [None, None]
        prev_x2 = None
        prev_b = None
        b_hi_ve = None
        prev_u_last = None
        prev_carry_dma = [None, None]
        prev_agb_cmb = [None, None]
        prev_zx_te = None
        prev_zx_thx_last = None
        prev_extract_gp = None
        prev_dbgb = None
        prev_dbgh1 = None
        sc_sr2 = None
        sc_atot = None
        ve_h1 = None

        def emit_xload(g):
            halves = []
            for h in range(2):
                w = []
                if g == 0 and h == 0:
                    w.append((s_te, mm_after))
                if g >= 2:
                    w.append((s_gp, ln2cast_h[g - 2][h]))
                    if h == 0:
                        w.append((sqs, spill_done[g - 2]))
                v = P.emit("sync", (lambda s, g=g, h=h:
                                    s.dma_start(out=bxhalf(g, h),
                                                in_=xh[g][:, h * 4 * GPP:(h + 1) * 4 * GPP])),
                           waits=w, inc=lanex(g), inc_by=16)
                halves.append(v)
            x_ld_half[g] = halves
            x_ld[g] = halves[-1]

        def emit_prelude_casts(g):
            w_free = []
            if g == 1:
                w_free.append((s_te, stats_of[0]))
            elif g >= 2:
                w_free.append((s_te, ln2mm[g - 2]))
            v = None
            for h in range(2):
                P.emit("scalar", (lambda s, g2=g, h=h:
                                  s.copy(WST[:, h * 4 * GPP:(h + 1) * 4 * GPP], bxhalf(g2, h))),
                       waits=[(lanex(g), x_ld_half[g][h])] + (w_free if h == 0 else []), inc=s_sc)
                v = P.emit("scalar", (lambda s, g2=g, h=h:
                                      s.activation(H2B[:, h * 4 * GPP:(h + 1) * 4 * GPP],
                                                   bxhalf(g2, h), AF.Square)),
                           inc=s_sc)
            pcast[g] = v

        def emit_prelude_stats(g):
            def stats_mm(t_):
                last = None
                for c in range(NCH):
                    t_.matmul(ps_c[:, :], ONESB[:, :], WST[:, c * GPP:(c + 1) * GPP],
                              start=(c == 0), stop=(c == NCH - 1))
                    last = t_.matmul(ps_d[:, :], ONESB[:, :], H2B[:, c * GPP:(c + 1) * GPP],
                                     start=(c == 0), stop=(c == NCH - 1))
                return last
            w_st = [(s_sc, pcast[g]), (s_gp, gp_ones)]
            if g > 0:
                w_st.append((s_ve, var_of[g - 1]))
            stats_of[g] = P.emit("tensor", stats_mm, waits=w_st, inc=s_te)

            sc_mb = P.emit("scalar", lambda s: s.activation(MB[:, :], ps_c[:, :], AF.Copy, scale=1.0 / D),
                           waits=[(s_te, stats_of[g])], inc=s_sc)
            v1 = P.emit("vector", lambda v: v.tensor_tensor(VT[:, :], MB[:, :], MB[:, :], OP.mult),
                        waits=[(s_sc, sc_mb)], inc=s_ve)
            v2 = P.emit("vector", lambda v: v.tensor_scalar_sub(VTB[:, :], VT[:, :], 1e-5),
                        waits=[(s_ve, v1)], inc=s_ve)
            var_of[g] = P.emit("vector", lambda v: v.scalar_tensor_tensor(VAR[:, :], ps_d[:, :], 1.0 / D,
                                                                          VTB[:, :], OP.mult, OP.subtract),
                               waits=[(s_ve, v2)], inc=s_ve)

        def emit_nlw2(g):
            nv = P.emit("scalar", lambda s: s.activation(VT[:, :], VAR[:, :], AF.Ln),
                        waits=[(s_ve, var_of[g])], inc=s_sc)
            rstd_of[g] = P.emit("scalar", lambda s: s.activation(RSTD[:, :], VT[:, :], AF.Exp, scale=-0.5),
                                waits=[(s_sc, nv)], inc=s_sc)

        def emit_lnblock(g):
            nonlocal ve_h1
            lt1 = None
            for ap, base, n in bx_pieces(g):
                def ln_t1(v, ap=ap, base=base, n=n):
                    bx3 = ap[:, 0:n * GPP].rearrange("p (c q) -> p c q", c=n)
                    bh3 = XBF[:, base * GPP:(base + n) * GPP].rearrange("p (c q) -> p c q", c=n)
                    mb_bc = sap(MB, 0, [[GPP, 128], [0, n], [1, GPP]])
                    return v.tensor_tensor(bh3, bx3, mb_bc, OP.subtract)
                w_t1 = []
                if base == 0:
                    w_t1 = [(s_sc, rstd_of[g]), (lanex(g), x_ld[g])]
                    if dbg and prev_dbgh1 is not None:
                        w_t1.append((ldm, prev_dbgh1))
                lt1 = P.emit("vector", ln_t1, waits=w_t1, inc=s_ve)

            def ln_t2(v):
                bh3 = XBF[:, 0:GF].rearrange("p (c q) -> p c q", c=NCH)
                rs_bc = sap(RSTD, 0, [[GPP, 128], [0, NCH], [1, GPP]])
                return v.tensor_tensor(bh3, bh3, rs_bc, OP.mult)
            lt2 = P.emit("vector", ln_t2, waits=[(s_ve, lt1)], inc=s_ve)

            def h1_build(v, g=g):
                last = None
                for c in range(NCH):
                    for t2 in range(GT):
                        t = GT * g + t2
                        last = v.scalar_tensor_tensor(
                            hsl(c, t2), hsl(c, t2),
                            modcol(1, c, t), modcol_bc(0, c, t, PP),
                            OP.mult, OP.add)
                return last
            ve_h1 = P.emit("vector", h1_build, waits=[(s_ve, mod_ready_v), (s_ve, lt2)], inc=s_ve)

        def emit_tail_scanA(g):
            nonlocal prev_extract_gp
            hbuf = [Y, Baa]
            ag_done = {}
            agv_of = {}
            for t2 in range(GT):
                t = GT * g + t2
                par = t % 2
                hb = hbuf[t2]

                def scan1(v, t2=t2, hb=hb):
                    last = None
                    for c in range(NCH):
                        last = v.tensor_tensor_scan(hb[:, c * PP:(c + 1) * PP],
                                                    gsl(Ba, c, t2), gsl(Bh1, c, t2),
                                                    0.0, OP.mult, OP.add)
                    return last
                ve_s1 = P.emit("vector", scan1, waits=[(s_gp, prev_b), (s_ve, b_hi_ve)], inc=s_ve)

                def extract(g_, t2=t2, par=par, hb=hb):
                    for c in range(NCH):
                        g_.tensor_copy(CARRY[:, par * 16 + c: par * 16 + c + 1],
                                       hb[:, c * PP + PP - 1: c * PP + PP])
                    return g_.tensor_copy(sap(CARRY, par * 16 + 8, [[32, 128], [1, 8]]),
                                          sap(ATOT, t2, [[16, 128], [GT, 8]]))
                w_ex = [(s_ve, ve_s1), (s_sc, sc_atot)]
                if prev_carry_dma[par] is not None:
                    w_ex.append((gqc[par], prev_carry_dma[par]))
                gp_ex = P.emit("gpsimd", extract, waits=w_ex, inc=s_gp)
                if t2 == GT - 1:
                    prev_extract_gp = gp_ex

                d1 = P.emit("gpsimd", (lambda g_, t=t, par=par:
                                       g_.dma_start(out=cind[t], in_=CARRY[:, par * 16:(par + 1) * 16])),
                            waits=[(s_gp, gp_ex)], inc=gqc[par], inc_by=16)

                agv_of[t2] = P.emit("gpsimd", (lambda g_, t=t: g_.collective_compute(
                    "AllGather", OP.bypass,
                    replica_groups=[list(range(NCORES))],
                    ins=[cind[t].opt()],
                    outs=[coutd[t].opt()])),
                    waits=[(gqc[par], d1)], inc=ccs)
                prev_carry_dma[par] = d1

            for t2 in range(GT):
                t = GT * g + t2
                par = t % 2
                w_ab = [(ccs, agv_of[t2])]
                if prev_agb_cmb[par] is not None:
                    w_ab.append((s_ve, prev_agb_cmb[par]))
                d2 = P.emit("gpsimd", (lambda g_, t=t, par=par:
                                       g_.dma_start(out=AGB[:, par * 128:(par + 1) * 128],
                                                    in_=sap(coutd, t * 8 * 128 * 16, [[16, 128], [2048, 8], [1, 16]]))),
                            waits=w_ab, inc=gqa[par], inc_by=16)
                ag_done[t2] = d2
            return hbuf, ag_done

        def emit_tail_scanB(g, hbuf, ag_done):
            nonlocal prev_x2
            for t2 in range(GT):
                t = GT * g + t2
                par = t % 2
                hb = hbuf[t2]

                def cmb_a(v, par=par):
                    at_ap = sap(AGB, par * 128 + 8, [[256, 128], [1, 8], [16, 8]])
                    he_ap = sap(AGB, par * 128 + 0, [[256, 128], [1, 8], [16, 8]])
                    am = MASK[:, 0:64].rearrange("p (c j) -> p c j", c=8)
                    c1 = CMB1[:, :].rearrange("p (c j) -> p c j", c=8)
                    c2 = CMB2[:, :].rearrange("p (c j) -> p c j", c=8)
                    v.tensor_tensor(c2, he_ap, am, OP.mult)
                    return v.tensor_tensor(c1, at_ap, am, OP.mult)

                def cmb_b(v):
                    c1 = CMB1[:, :].rearrange("p (c j) -> p c j", c=8)
                    bm = MASK[:, 64:128].rearrange("p (c j) -> p c j", c=8)
                    return v.tensor_tensor(c1, c1, bm, OP.add)

                def cmb_c(v):
                    last = None
                    for c in range(NCH):
                        last = v.tensor_tensor_scan(CMB3[:, c * 8:(c + 1) * 8],
                                                    CMB1[:, c * 8:(c + 1) * 8],
                                                    CMB2[:, c * 8:(c + 1) * 8],
                                                    0.0, OP.mult, OP.add)
                    return last
                w_cmb = [(gqa[par], ag_done[t2])]
                if t < 2:
                    w_cmb.append((ldm, ld_msk))
                ca_ = P.emit("vector", cmb_a, waits=w_cmb, inc=s_ve)
                cb_ = P.emit("vector", cmb_b, waits=[(s_ve, ca_)], inc=s_ve)
                cc_ = P.emit("vector", cmb_c, waits=[(s_ve, cb_)], inc=s_ve)
                cmb = P.emit("vector", (lambda v, par=par:
                                        v.tensor_copy(HIN[:, par * 8:(par + 1) * 8],
                                                      sap(CMB3, 7, [[64, 128], [8, 8]]))),
                             waits=[(s_ve, cc_)], inc=s_ve)
                prev_agb_cmb[par] = cmb

                def rescan(v, t2=t2, par=par, hb=hb):
                    last = None
                    for c in range(NCH):
                        last = v.tensor_tensor_scan(hb[:, c * PP:(c + 1) * PP],
                                                    gsl(Ba, c, t2), gsl(Bh1, c, t2),
                                                    HIN[:, par * 8 + c: par * 8 + c + 1],
                                                    OP.mult, OP.add)
                    return last
                ve_rs = P.emit("vector", rescan, waits=[(s_ve, cmb)], inc=s_ve)
                last_rescan[t2] = ve_rs

                def x2b(v, t2=t2, g=g, hb=hb):
                    t = GT * g + t2
                    last = None
                    for c in range(NCH):
                        last = v.scalar_tensor_tensor(bxsl(g, c, t2), hb[:, c * PP:(c + 1) * PP],
                                                      modcol(2, c, t), bxsl(g, c, t2),
                                                      OP.mult, OP.add)
                    return last
                prev_x2 = P.emit("vector", x2b, waits=[(s_ve, ve_rs)], inc=s_ve)

                sp = None
                for pi, (ap, base, n) in enumerate(bx_pieces(g)):
                    w_sp = []
                    if pi == 0:
                        w_sp.append((s_ve, prev_x2))
                        if t2 == 0 and g > 0:
                            w_sp.append((sqs, spill_done[g - 1]))
                    def spl(s, ap=ap, base=base, n=n, t2=t2, t=t):
                        src = ap[:, 0:n * GPP].rearrange("p (c u q) -> p c u q", c=n, u=GT)[:, :, t2, :]
                        return s.dma_start(out=x2d[t][:, base * PP:(base + n) * PP], in_=src)
                    sp = P.emit("sync", spl, waits=w_sp, inc=sqs, inc_by=16)
                if t2 == GT - 1:
                    spill_done[g] = sp

        def emit_main(g):
            nonlocal prev_b, b_hi_ve, prev_u_last, prev_zx_te, prev_zx_thx_last
            nonlocal sc_sr2, sc_atot, prev_dbgb, prev_dbgh1
            if dbg:
                prev_dbgh1 = P.emit("sync", (lambda s, g=g: s.dma_start(out=dbg_h1[g], in_=XBF[:, 0:GF])),
                                    waits=[(s_ve, ve_h1)], inc=ldm, inc_by=16)
            w_h1b = [(s_ve, ve_h1)]
            if prev_zx_te is not None:
                w_h1b.append((s_te, prev_zx_te))
            gp_h1b = P.emit("scalar", lambda s: s.copy(Bh1b[:, 0:GF], XBF[:, 0:GF]),
                            waits=w_h1b, inc=s_sc)

            th_done = {}
            red_done = {}
            a_done = {}
            aa_done = {}
            z_te = {}
            for c in range(NCH):
                psz = ps_a if c % 2 == 0 else ps_b

                def zmm(t_, c=c, psz=psz):
                    last = None
                    for k in range(NCH):
                        last = t_.matmul(psz[:, :],
                                         wslot[0][:, k * 1024 + c * 128: k * 1024 + (c + 1) * 128],
                                         Bh1b[:, k * GPP:(k + 1) * GPP],
                                         start=(k == 0), stop=(k == NCH - 1))
                    return last
                w_z = [(s_sc, gp_h1b)] if c == 0 else []
                if c == 0:
                    w_z.append((s_sc, wa_done))
                    if g > 0:
                        w_z.append((s_sc, prev_zx_thx_last))
                if c >= 2:
                    w_z.append((s_sc, th_done[c - 2]))
                z_te[c] = P.emit("tensor", zmm, waits=w_z, inc=s_te)

                def tha(s, c=c, psz=psz):
                    return s.activation(SCRA[:, (c % 2) * GPP:(c % 2 + 1) * GPP], psz[:, :],
                                        AF.Tanh, scale=0.5, bias=dc("bah", c))
                w_th = [(s_te, z_te[c])]
                if c >= 2:
                    w_th.append((s_ve, red_done[c - 2]))
                    w_th.append((s_sc, a_done[c - 2]))
                th_done[c] = P.emit("scalar", tha, waits=w_th, inc=s_sc)

                w_a = [(s_sc, th_done[c])]
                if c == 0:
                    w_a.append((s_ve, last_rescan[1]))
                a_done[c] = P.emit("scalar",
                                   (lambda s, c=c: s.activation(gsl(Ba, c), SCRA[:, (c % 2) * GPP:(c % 2 + 1) * GPP],
                                                                AF.Exp, scale=dc("sh", c), bias=dc("sh", c))),
                                   waits=w_a, inc=s_sc)

                def red(v, c=c):
                    src = SCRA[:, (c % 2) * GPP:(c % 2 + 1) * GPP].rearrange("p (t q) -> p t q", t=GT)
                    return v.tensor_reduce(THS[:, c * GT:(c + 1) * GT], src, AX, OP.add)
                red_done[c] = P.emit("vector", red, waits=[(s_sc, th_done[c])], inc=s_ve)

                w_aa = [(s_sc, a_done[c])]
                if c == 0 and prev_x2 is not None:
                    w_aa.append((s_ve, prev_x2))
                aa_done[c] = P.emit("gpsimd",
                                    (lambda g_, c=c: g_.tensor_tensor(gsl(Baa, c), gsl(Ba, c), gsl(Ba, c), OP.mult)),
                                    waits=w_aa, inc=s_gp)

            u_done = {}
            thx_done = {}
            for c in range(NCH):
                psz = ps_a if c % 2 == 0 else ps_b

                def zmm(t_, c=c, psz=psz):
                    last = None
                    for k in range(NCH):
                        last = t_.matmul(psz[:, :],
                                         wslot[1][:, k * 1024 + c * 128: k * 1024 + (c + 1) * 128],
                                         Bh1b[:, k * GPP:(k + 1) * GPP],
                                         start=(k == 0), stop=(k == NCH - 1))
                    return last
                w_z = [(s_sc, th_done[NCH - 2 + (c % 2)] if c < 2 else thx_done[c - 2])]
                if c == 0:
                    w_z.append((s_sc, wx_done))
                zx_te = P.emit("tensor", zmm, waits=w_z, inc=s_te)
                if c == NCH - 1:
                    prev_zx_te = zx_te

                def thx(s, c=c, psz=psz):
                    return s.activation(SCRX[:, (c % 2) * GPP:(c % 2 + 1) * GPP], psz[:, :],
                                        AF.Tanh, scale=0.5, bias=dc("bxh", c))
                w_th = [(s_te, zx_te)]
                if c >= 2:
                    w_th.append((s_ve, u_done[c - 2]))
                thx_done[c] = P.emit("scalar", thx, waits=w_th, inc=s_sc)

                def ub(v, c=c):
                    return v.scalar_tensor_tensor(gsl(Bh1, c), SCRX[:, (c % 2) * GPP:(c % 2 + 1) * GPP],
                                                  1.0, hsl(c), OP.add, OP.mult)
                u_done[c] = P.emit("vector", ub, waits=[(s_sc, thx_done[c])], inc=s_ve)
            prev_zx_thx_last = thx_done[NCH - 1]
            prev_u_last = u_done[NCH - 1]

            def atot(s, g=g):
                last = None
                for c in range(NCH):
                    last = s.activation(ATOT[:, c * GT:(c + 1) * GT], THS[:, c * GT:(c + 1) * GT],
                                        AF.Exp, scale=dc("sh", c), bias=dc("s256", c))
                return last
            w_at = [(s_ve, red_done[NCH - 1])]
            if prev_extract_gp is not None:
                w_at.append((s_gp, prev_extract_gp))
            sc_atot = P.emit("scalar", atot, waits=w_at, inc=s_sc)

            H2 = GF // 2
            w_nl4a = [(s_gp, aa_done[NCH // 2 - 1]), (s_ve, u_done[NCH // 2 - 1]),
                      (s_gp, gp_ones)]
            if dbg:
                w_nl4a.append((ldm, prev_dbgh1))
            l1a = P.emit("scalar",
                         lambda s: s.activation(XBF[:, 0:H2], Baa[:, 0:H2], AF.Ln, scale=-0.25, bias=CONSTS[:, 0:1]),
                         waits=w_nl4a, inc=s_sc)
            l1b = P.emit("scalar",
                         lambda s: s.activation(XBF[:, H2:GF], Baa[:, H2:GF], AF.Ln, scale=-0.25, bias=CONSTS[:, 0:1]),
                         waits=[(s_gp, aa_done[NCH - 1]), (s_ve, u_done[NCH - 1])], inc=s_sc)
            P.emit("scalar",
                   lambda s: s.activation(Baa[:, 0:H2], XBF[:, 0:H2], AF.Exp, scale=0.5),
                   waits=[(s_sc, l1a)], inc=s_sc)
            sc_sr2 = P.emit("scalar",
                            lambda s: s.activation(Baa[:, H2:GF], XBF[:, H2:GF], AF.Exp, scale=0.5),
                            waits=[(s_sc, l1b)], inc=s_sc)

            if g >= 2:
                w_n1 = [(s_ve, ln2ev[g - 2])]
                if g >= 3 and rows_done[g - 3] is not None:
                    w_n1.append((sqr, rows_done[g - 3]))
                nb1 = P.emit("scalar", lambda s: s.activation(VTB[:, :], VAR2[:, :], AF.Ln),
                             waits=w_n1, inc=s_sc)
                nb2 = P.emit("scalar", lambda s: s.activation(RSTD2[:, :], VTB[:, :], AF.Exp, scale=-0.5),
                             waits=[(s_sc, nb1)], inc=s_sc)
                P.emit(SCQ, (lambda s, g=g: s.dma_start(out=rowsd[g - 2, 0], in_=MB2[0:1, :])),
                       waits=[(s_sc, nb2)], inc=sqr, inc_by=16)
                rows_done[g - 2] = P.emit(SCQ, (lambda s, g=g: s.dma_start(out=rowsd[g - 2, 1], in_=RSTD2[0:1, :])),
                                          inc=sqr, inc_by=16)

            b_hi_ve = P.emit("vector",
                             lambda v: v.tensor_tensor(Bh1[:, 0:GF], Bh1[:, 0:GF], Baa[:, 0:GF], OP.mult),
                             waits=[(s_sc, sc_sr2), (s_ve, u_done[NCH - 1])], inc=s_ve)
            prev_b = None

            if dbg:
                P.emit("sync", (lambda s, g=g: s.dma_start(out=dbg_a[g], in_=Ba[:, 0:GF])),
                       waits=[(s_ve, b_hi_ve)], inc=ldm, inc_by=16)
                prev_dbgb = P.emit("sync", (lambda s, g=g: s.dma_start(out=dbg_b[g], in_=Bh1[:, 0:GF])),
                                   inc=ldm, inc_by=16)
                P.emit("vector", lambda v: v.engine_nop(), waits=[(ldm, prev_dbgb)])

        def emit_ln2_casts(g):
            w_c1 = [(s_ve, prev_x2)]
            if g + 1 < NGq:
                w_c1.append((s_te, stats_of[g + 1]))
            hv = []
            v = None
            for h in range(2):
                for c in range(4 * h, 4 * h + 4):
                    P.emit("gpsimd", (lambda g_, g2=g, c=c:
                                      g_.tensor_tensor(WST[:, c * GPP:(c + 1) * GPP], bxsl(g2, c),
                                                       ONES5[:, 0:GPP], OP.mult)),
                           waits=(w_c1 if (h == 0 and c == 0) else ()), inc=s_gp)
                v = P.emit("gpsimd", (lambda g_, g2=g, h=h:
                                      g_.tensor_tensor(H2B[:, h * 4 * GPP:(h + 1) * 4 * GPP],
                                                       bxhalf(g2, h), bxhalf(g2, h), OP.mult)),
                           inc=s_gp)
                hv.append(v)
            ln2cast_h[g] = hv
            ln2cast[g] = v

        def emit_ln2_mm(g):
            def stats2_mm(t_):
                last = None
                for c in range(NCH):
                    t_.matmul(ps_mod[:, 0:GPP], ONESB[:, :], WST[:, c * GPP:(c + 1) * GPP],
                              start=(c == 0), stop=(c == NCH - 1))
                    last = t_.matmul(ps_g[:, 0:GPP], ONESB[:, :], H2B[:, c * GPP:(c + 1) * GPP],
                                     start=(c == 0), stop=(c == NCH - 1))
                return last
            w_l2 = [(s_gp, ln2cast[g])]
            if g == 0:
                w_l2.append((s_ve, acc_hist[-1]))
            else:
                w_l2.append((s_ve, ln2ev[g - 1]))
                w_l2.append((s_sc, ln2sc[g - 1]))
            ln2mm[g] = P.emit("tensor", stats2_mm, waits=w_l2, inc=s_te)
            w_ls = [(s_te, ln2mm[g])]
            if P.val(sqr):
                w_ls.append((sqr, P.val(sqr)))
            ln2sc[g] = P.emit("scalar", lambda s: s.activation(MB2[:, :], ps_mod[:, 0:GPP], AF.Copy, scale=1.0 / D),
                              waits=w_ls, inc=s_sc)
            w1 = P.emit("vector", lambda v: v.tensor_tensor(VT[:, :], MB2[:, :], MB2[:, :], OP.mult),
                        waits=[(s_sc, ln2sc[g])], inc=s_ve)
            w2 = P.emit("vector", lambda v: v.tensor_scalar_sub(VTB[:, :], VT[:, :], 1e-5),
                        waits=[(s_ve, w1)], inc=s_ve)
            ln2ev[g] = P.emit("vector", lambda v: v.scalar_tensor_tensor(VAR2[:, :], ps_g[:, 0:GPP], 1.0 / D,
                                                                         VTB[:, :], OP.mult, OP.subtract),
                              waits=[(s_ve, w2)], inc=s_ve)

        emit_xload(0)
        emit_xload(1)
        tailst = None
        for g in range(NG):
            emit_prelude_casts(g)
            if g > 0:
                tailst = emit_tail_scanA(g - 1)
            emit_prelude_stats(g)
            emit_nlw2(g)
            emit_lnblock(g)
            if g > 0:
                emit_tail_scanB(g - 1, *tailst)
                emit_ln2_casts(g - 1)
            emit_main(g)
            if g > 0:
                emit_ln2_mm(g - 1)
            if 2 <= g + 1 < NG:
                emit_xload(g + 1)

        tailst = emit_tail_scanA(NG - 1)
        emit_tail_scanB(NG - 1, *tailst)
        emit_ln2_casts(NG - 1)

        def emit_rows_fin(gg):
            w_n1 = [(s_ve, ln2ev[gg])]
            if rows_done[gg - 1] is not None:
                w_n1.append((sqr, rows_done[gg - 1]))
            nf1 = P.emit("scalar", lambda s: s.activation(VTB[:, :], VAR2[:, :], AF.Ln),
                         waits=w_n1, inc=s_sc)
            sc_fin = P.emit("scalar", lambda s: s.activation(RSTD2[:, :], VTB[:, :], AF.Exp, scale=-0.5),
                            waits=[(s_sc, nf1)], inc=s_sc)
            P.emit(SCQ, (lambda s, gg=gg: s.dma_start(out=rowsd[gg, 0], in_=MB2[0:1, :])),
                   waits=[(s_sc, sc_fin)], inc=sqr, inc_by=16)
            rows_done[gg] = P.emit(SCQ, (lambda s, gg=gg: s.dma_start(out=rowsd[gg, 1], in_=RSTD2[0:1, :])),
                                   inc=sqr, inc_by=16)

        emit_rows_fin(NG - 2)
        emit_ln2_mm(NG - 1)
        emit_rows_fin(NG - 1)


        attn_done = [(s_ve, P.val(s_ve)), (s_gp, P.val(s_gp)),
                     (s_sc, P.val(s_sc)), (s_te, P.val(s_te))]
        attn_te_total = P.val(s_te)

        ffn_w = {}
        for i, n in enumerate(["wfT", "wiT", "wcT", "ufT", "uiT", "ucT"]):
            ffn_w[n] = load_weight(i, wd[n],
                                   waits_extra=(attn_done if i == 0 else ()))
        ffn_w_done = ffn_w["ucT"]

        w_c0 = list(attn_done) + [(sqs, P.val(sqs))]
        cz = P.emit("vector", lambda v: v.memset(Z3[:, 0:PP], 0.0), waits=w_c0, inc=s_ve)

        def c0_build(v):
            last = None
            for c in range(NCH):
                last = v.tensor_scalar_add(C[:, c * PP:(c + 1) * PP], Z3[:, 0:PP], vc("inc", c))
            return last
        c0b = P.emit("vector", c0_build, waits=[(s_ve, cz)], inc=s_ve)
        ve_c0 = P.emit("vector", lambda v: v.tensor_copy(CB[:, 0:TF], C[:, :]),
                       waits=[(s_ve, c0b)], inc=s_ve)

        sqs_total = P.val(sqs)
        sqr_total = P.val(sqr)
        x2_ld = [None, None]
        row_ld = [None, None]
        h2prep = [None, None]
        h2cast = [None, None]
        bcast_te = [None, None]
        gate_te_last = [None, None]
        ew_last = [None, None]
        prev_ew_all = ve_c0
        last_h2cast = None

        W3 = {0: "wfT", 1: "wiT", 2: "wcT"}
        U3 = {0: "ufT", 1: "uiT", 2: "ucT"}
        WIDX = {"wfT": 0, "wiT": 1, "wcT": 2, "ufT": 3, "uiT": 4, "ucT": 5}

        def prep_step(t):
            par = t % 2
            g = t // GT
            t2 = t % GT
            w_x = [(sqs, sqs_total)]
            if t < 2:
                w_x.append((s_te, attn_te_total))
            if h2prep[par] is not None:
                w_x.append((s_ve, h2prep[par]))
            x2_ld[par] = P.emit("sync", (lambda s, t=t, par=par:
                                         s.dma_start(out=XBF[:, par * 2048: par * 2048 + TF], in_=x2d[t])),
                                waits=w_x, inc=ldx2[par], inc_by=16)

            w_r = [(sqr, sqr_total)]
            if bcast_te[par] is not None:
                w_r.append((s_te, bcast_te[par]))
            P.emit("sync", (lambda s, g=g, t2=t2, par=par:
                            s.dma_start(out=ROWSTAGE[0:1, par * 2 * PP: par * 2 * PP + PP],
                                        in_=rowsd[g, 0, t2 * PP:(t2 + 1) * PP])),
                   waits=w_r, inc=ldr[par], inc_by=16)
            row_ld[par] = P.emit("sync", (lambda s, g=g, t2=t2, par=par:
                                          s.dma_start(out=ROWSTAGE[0:1, par * 2 * PP + PP: par * 2 * PP + 2 * PP],
                                                      in_=rowsd[g, 1, t2 * PP:(t2 + 1) * PP])),
                                 inc=ldr[par], inc_by=16)

            psbc = ps_c if par == 0 else ps_d
            w_bc = [(ldr[par], row_ld[par]), (s_sc, ffn_w_done)]
            if h2prep[par] is not None:
                w_bc.append((s_ve, h2prep[par]))
            if t == 0:
                w_bc.append((s_te, P.val(s_te)))
            bcast_te[par] = P.emit("tensor", (lambda t_, par=par, psbc=psbc:
                                              t_.matmul(psbc[:, 0:2 * PP], ONESF[:, :],
                                                        ROWSTAGE[:, par * 2 * PP:(par + 1) * 2 * PP],
                                                        start=True, stop=True)),
                                   waits=w_bc, inc=s_te)

            def h2p_a(v, par=par, psbc=psbc):
                xv = XBF[:, par * 2048: par * 2048 + TF].rearrange("p (c q) -> p c q", c=NCH)
                yv = Y[:, :].rearrange("p (c q) -> p c q", c=NCH)
                mb_bc = sap(psbc, 0, [[2 * PP, 128], [0, NCH], [1, PP]])
                return v.tensor_tensor(yv, xv, mb_bc, OP.subtract)

            def h2p_b(v, psbc=psbc):
                yv = Y[:, :].rearrange("p (c q) -> p c q", c=NCH)
                rs_bc = sap(psbc, PP, [[2 * PP, 128], [0, NCH], [1, PP]])
                return v.tensor_tensor(yv, yv, rs_bc, OP.mult)

            def h2p_c(v, t=t):
                last = None
                for c in range(NCH):
                    last = v.scalar_tensor_tensor(Y[:, c * PP:(c + 1) * PP], Y[:, c * PP:(c + 1) * PP],
                                                  modcol(4, c, t), modcol_bc(3, c, t, PP),
                                                  OP.mult, OP.add)
                return last
            w_h2 = [(s_te, bcast_te[par]), (ldx2[par], x2_ld[par])]
            if last_h2cast is not None:
                w_h2.append((s_sc, last_h2cast))
            if t == 0:
                w_h2.append((s_ve, ve_c0))
            hpa = P.emit("vector", h2p_a, waits=w_h2, inc=s_ve)
            hpb = P.emit("vector", h2p_b, waits=[(s_ve, hpa)], inc=s_ve)
            h2prep[par] = P.emit("vector", h2p_c, waits=[(s_ve, hpb)], inc=s_ve)

        def cast_step(t):
            nonlocal last_h2cast
            par = t % 2
            w_hc = [(s_ve, h2prep[par])]
            if gate_te_last[par] is not None:
                w_hc.append((s_te, gate_te_last[par]))
            h2cast[par] = P.emit("scalar", (lambda s, par=par:
                                            s.copy(H2B[:, par * 2048: par * 2048 + TF], Y[:, :])),
                                 waits=w_hc, inc=s_sc)
            last_h2cast = h2cast[par]

        prep_step(0)
        cast_step(0)

        for t in range(T):
            par = t % 2
            if t + 1 < T:
                prep_step(t + 1)

            ew_mc = {}
            sc_ev_mc = {}
            if t == 0:
                prev_psg_ev = {0: None, 1: None}
            for mc in range(NCH):
                psg = ps_g if mc % 2 == 0 else ps_mod
                def gmm_w(t_, gate, mc=mc, par=par, psg=psg):
                    last = None
                    for k in range(NCH):
                        mov = H2B[:, par * 2048 + k * PP: par * 2048 + (k + 1) * PP]
                        last = t_.matmul(psg[:, gate * PP:(gate + 1) * PP],
                                         wslot[WIDX[W3[gate]]][:, k * 1024 + mc * 128: k * 1024 + (mc + 1) * 128],
                                         mov, start=(k == 0), stop=False)
                    return last

                def gmm_u(t_, gate, mc=mc, par=par, psg=psg):
                    last = None
                    for k in range(NCH):
                        mov = CB[:, par * 2048 + k * PP: par * 2048 + (k + 1) * PP]
                        last = t_.matmul(psg[:, gate * PP:(gate + 1) * PP],
                                         wslot[WIDX[U3[gate]]][:, k * 1024 + mc * 128: k * 1024 + (mc + 1) * 128],
                                         mov, start=False, stop=(k == NCH - 1))
                    return last
                w_g = []
                if mc == 0:
                    w_g = [(s_sc, h2cast[par]), (s_sc, ffn_w_done)]
                    if t == 0:
                        w_g.append((s_ve, ln2ev[NG - 1]))
                        w_g.append((s_sc, ln2sc[NG - 1]))
                    if ew_last[par] is not None:
                        w_g.append((s_ve, ew_last[par]))
                if mc >= 2:
                    w_g.append((s_sc, sc_ev_mc[mc - 2]))
                    w_g.append((s_ve, ew_mc.get(mc - 2)))
                else:
                    w_g.append((s_sc, prev_psg_ev[mc % 2]))
                import functools as _ft
                P.emit("tensor", _ft.partial(gmm_w, gate=0), waits=w_g, inc=s_te)
                w_gu = []
                if mc == 0:
                    w_gu = [(s_ve, prev_ew_all)]
                gt = P.emit("tensor", _ft.partial(gmm_u, gate=0), waits=w_gu, inc=s_te)
                for gate in range(1, 3):
                    P.emit("tensor", _ft.partial(gmm_w, gate=gate), inc=s_te)
                    gt = P.emit("tensor", _ft.partial(gmm_u, gate=gate), inc=s_te)
                if mc == NCH - 1:
                    gate_te_last[par] = gt

                mp = mc % 2

                def gevac(s, mc=mc, mp=mp, psg=psg):
                    s.activation(THF[:, mp * PP:(mp + 1) * PP], psg[:, 0:PP], AF.Tanh,
                                 scale=0.5, bias=dc("bfh", mc))
                    s.activation(THI[:, mp * PP:(mp + 1) * PP], psg[:, PP:2 * PP], AF.Tanh,
                                 scale=0.5, bias=dc("bih", mc))
                    return s.activation(THG[:, mp * PP:(mp + 1) * PP], psg[:, 2 * PP:3 * PP], AF.Tanh,
                                        bias=vc("bc", mc))
                w_e = [(s_te, gt)]
                if mc >= 2:
                    w_e.append((s_ve, ew_mc[mc - 2]))
                sc_ev = P.emit("scalar", gevac, waits=w_e, inc=s_sc)
                sc_ev_mc[mc] = sc_ev
                if mc >= NCH - 2:
                    prev_psg_ev[mc % 2] = sc_ev

                def gew_a(v, mc=mc, mp=mp):
                    v.scalar_tensor_tensor(U1s[:, mp * PP:(mp + 1) * PP], THF[:, mp * PP:(mp + 1) * PP],
                                           1.0, C[:, mc * PP:(mc + 1) * PP], OP.add, OP.mult)
                    return v.scalar_tensor_tensor(U2s[:, mp * PP:(mp + 1) * PP], THI[:, mp * PP:(mp + 1) * PP],
                                                  1.0, THG[:, mp * PP:(mp + 1) * PP], OP.add, OP.mult)

                def gew_b(v, mc=mc, mp=mp):
                    return v.tensor_tensor(U1s[:, mp * PP:(mp + 1) * PP], U1s[:, mp * PP:(mp + 1) * PP],
                                           U2s[:, mp * PP:(mp + 1) * PP], OP.add)

                def gew_c(v, mc=mc, mp=mp, t=t):
                    v.tensor_scalar_mul(C[:, mc * PP:(mc + 1) * PP], U1s[:, mp * PP:(mp + 1) * PP], 0.5)
                    return v.tensor_scalar_mul(CB[:, (1 - t % 2) * 2048 + mc * PP: (1 - t % 2) * 2048 + (mc + 1) * PP],
                                               U1s[:, mp * PP:(mp + 1) * PP], 0.5)
                ga_ = P.emit("vector", gew_a, waits=[(s_sc, sc_ev)], inc=s_ve)
                gb_ = P.emit("vector", gew_b, waits=[(s_ve, ga_)], inc=s_ve)
                ew_mc[mc] = P.emit("vector", gew_c, waits=[(s_ve, gb_)], inc=s_ve)
            ew_last[par] = ew_mc[NCH - 1]
            prev_ew_all = ew_mc[NCH - 1]
            if t + 1 < T:
                cast_step(t + 1)

        uo_done = load_weight(1, wd["uoT"], waits_extra=[(s_te, gate_te_last[(T - 1) % 2])])
        wo_done = load_weight(0, wd["woT"], waits_extra=[(s_te, gate_te_last[(T - 1) % 2])])
        cfin_par = T % 2

        u_ev = {}
        for oc in range(NCH):
            pu = ps_c if oc % 2 == 0 else ps_d

            def umm(t_, oc=oc, pu=pu):
                last = None
                for k in range(NCH):
                    last = t_.matmul(pu[:, 0:PP],
                                     wslot[1][:, k * 1024 + oc * 128: k * 1024 + (oc + 1) * 128],
                                     CB[:, cfin_par * 2048 + k * PP: cfin_par * 2048 + (k + 1) * PP],
                                     start=(k == 0), stop=(k == NCH - 1))
                return last
            w_u = [(s_ve, prev_ew_all), (s_sc, uo_done)]
            if oc == 0:
                w_u.append((s_te, bcast_te[0]))
                w_u.append((s_te, bcast_te[1]))
            if oc >= 2:
                w_u.append((s_sc, u_ev[oc - 2]))
            ut = P.emit("tensor", umm, waits=w_u, inc=s_te)
            u_ev[oc] = P.emit("scalar", (lambda s, oc=oc, pu=pu:
                                         s.activation(C[:, oc * PP:(oc + 1) * PP], pu[:, 0:PP],
                                                      AF.Identity, bias=vc("bo", oc))),
                              waits=[(s_te, ut), (s_ve, prev_ew_all)], inc=s_sc)
        u_done_all = u_ev[NCH - 1]

        out_dma_cnt = 0
        prev_outdma = {}
        prev_h2o = [None, None]
        prev_h2co = [None, None]
        prev_bc_o = [None, None]
        prev_mm_o = {}
        prev_stt_o = {}
        for gp_ in range(NG):
            t0o = GT * gp_
            for t2 in range(GT):
                t = t0o + t2
                par = t2

                w_x = []
                if gp_ == 0:
                    w_x.append((s_sc, h2cast[par]))
                    w_x.append((s_ve, h2prep[par]))
                else:
                    w_x.append((s_ve, prev_h2o[par]))
                    w_x.append((s_ve, prev_stt_o[(gp_ - 1, NCH - 1, par)]))
                x2o = P.emit("sync", (lambda s, t=t, par=par:
                                      s.dma_start(out=XBF[:, par * 2048: par * 2048 + TF], in_=x2d[t])),
                             waits=w_x, inc=ldx2[par], inc_by=16)

                w_r = []
                if prev_bc_o[par] is not None:
                    w_r.append((s_te, prev_bc_o[par]))
                elif gp_ == 0:
                    w_r.append((s_te, bcast_te[par]))

                P.emit("sync", (lambda s, g=gp_, t2=t2, par=par:
                                s.dma_start(out=ROWSTAGE[0:1, par * 2 * PP: par * 2 * PP + PP],
                                            in_=rowsd[g, 0, t2 * PP:(t2 + 1) * PP])),
                       waits=w_r, inc=ldr[par], inc_by=16)
                rowo = P.emit("sync", (lambda s, g=gp_, t2=t2, par=par:
                                       s.dma_start(out=ROWSTAGE[0:1, par * 2 * PP + PP: par * 2 * PP + 2 * PP],
                                                   in_=rowsd[g, 1, t2 * PP:(t2 + 1) * PP])),
                              inc=ldr[par], inc_by=16)

                psbc = ps_c if par == 0 else ps_d
                w_bc = [(ldr[par], rowo), (s_sc, u_done_all)]
                if prev_h2o[par] is not None:
                    w_bc.append((s_ve, prev_h2o[par]))
                bco = P.emit("tensor", (lambda t_, par=par, psbc=psbc:
                                        t_.matmul(psbc[:, 0:2 * PP], ONESF[:, :],
                                                  ROWSTAGE[:, par * 2 * PP:(par + 1) * 2 * PP],
                                                  start=True, stop=True)),
                             waits=w_bc, inc=s_te)
                prev_bc_o[par] = bco

                def h2po_a(v, par=par, psbc=psbc):
                    xv = XBF[:, par * 2048: par * 2048 + TF].rearrange("p (c q) -> p c q", c=NCH)
                    yv = Y[:, :].rearrange("p (c q) -> p c q", c=NCH)
                    mb_bc = sap(psbc, 0, [[2 * PP, 128], [0, NCH], [1, PP]])
                    return v.tensor_tensor(yv, xv, mb_bc, OP.subtract)

                def h2po_b(v, psbc=psbc):
                    yv = Y[:, :].rearrange("p (c q) -> p c q", c=NCH)
                    rs_bc = sap(psbc, PP, [[2 * PP, 128], [0, NCH], [1, PP]])
                    return v.tensor_tensor(yv, yv, rs_bc, OP.mult)

                def h2po_c(v, t=t):
                    last = None
                    for c in range(NCH):
                        last = v.scalar_tensor_tensor(Y[:, c * PP:(c + 1) * PP], Y[:, c * PP:(c + 1) * PP],
                                                      modcol(4, c, t), modcol_bc(3, c, t, PP),
                                                      OP.mult, OP.add)
                    return last
                w_h2 = [(s_te, bco), (ldx2[par], x2o)]
                w_h2.append((s_sc, last_h2cast))
                if gp_ == 0 and t2 == 0:
                    w_h2.append((s_ve, prev_ew_all))
                hoa = P.emit("vector", h2po_a, waits=w_h2, inc=s_ve)
                hob = P.emit("vector", h2po_b, waits=[(s_ve, hoa)], inc=s_ve)
                h2o = P.emit("vector", h2po_c, waits=[(s_ve, hob)], inc=s_ve)
                prev_h2o[par] = h2o

                w_hc = [(s_ve, h2o)]
                if prev_mm_o.get(par) is not None:
                    w_hc.append((s_te, prev_mm_o[par]))
                h2co = P.emit("scalar", (lambda s, par=par:
                                         s.copy(H2B[:, par * 2048: par * 2048 + TF], Y[:, :])),
                              waits=w_hc, inc=s_sc)
                prev_h2co[par] = h2co
                last_h2cast = h2co

            def psout(t2, op_):
                if t2 == 0:
                    return (ps_a if op_ == 0 else ps_b)[:, 0:PP]
                return (ps_g if op_ == 0 else ps_mod)[:, 0:PP]
            mm_oc = {}
            for oc in range(NCH):
                def omm(t_, oc=oc):
                    last = None
                    for k in range(NCH):
                        for t2 in range(GT):
                            last = t_.matmul(psout(t2, oc % 2),
                                             wslot[0][:, k * 1024 + oc * 128: k * 1024 + (oc + 1) * 128],
                                             H2B[:, t2 * 2048 + k * PP: t2 * 2048 + (k + 1) * PP],
                                             start=(k == 0), stop=(k == NCH - 1))
                    return last
                w_m = [(s_sc, prev_h2co[1]), (s_sc, wo_done)]
                if gp_ == 0 and oc == 0:
                    w_m.append((s_ve, prev_ew_all))
                    w_m.append((s_sc, u_done_all))
                if oc >= 2:
                    w_m.append((s_ve, prev_stt_o[(gp_, oc - 2, 1)]))
                mm_oc[oc] = P.emit("tensor", omm, waits=w_m, inc=s_te)

                op_ = oc % 2
                for t2 in range(GT):
                    t = t0o + t2

                    def oev(v, oc=oc, op_=op_, t2=t2):
                        return v.tensor_tensor(Z3[:, op_ * PP:(op_ + 1) * PP],
                                               psout(t2, op_),
                                               C[:, oc * PP:(oc + 1) * PP], OP.add)
                    w_e = [(s_te, mm_oc[oc]), (s_sc, u_done_all)]
                    if (gp_, oc - 2, t2) in prev_stt_o or t2 > 0 or oc > 0 or gp_ > 0:
                        w_e.append((s_sc, P.val(s_sc)))
                    vz = P.emit("vector", oev, waits=w_e, inc=s_ve)
                    sz = P.emit("scalar", (lambda s, op_=op_: s.activation(Z3B[:, op_ * PP:(op_ + 1) * PP],
                                                                           Z3[:, op_ * PP:(op_ + 1) * PP], AF.Tanh)),
                                waits=[(s_ve, vz)], inc=s_sc)

                    def ostt(v, oc=oc, op_=op_, t=t, par=t2):
                        return v.scalar_tensor_tensor(U1s[:, op_ * PP:(op_ + 1) * PP],
                                                      Z3B[:, op_ * PP:(op_ + 1) * PP],
                                                      modcol(5, oc, t),
                                                      XBF[:, par * 2048 + oc * PP: par * 2048 + (oc + 1) * PP],
                                                      OP.mult, OP.add)
                    w_s = [(s_sc, sz)]
                    if prev_outdma.get(op_) is not None:
                        w_s.append((sqo[op_], prev_outdma[op_]))
                    so = P.emit("vector", ostt, waits=w_s, inc=s_ve)
                    prev_stt_o[(gp_, oc, t2)] = so

                    od = P.emit(SCQ, (lambda s, t=t, oc=oc, op_=op_:
                                           s.dma_start(out=outp[t][:, oc * PP:(oc + 1) * PP],
                                                       in_=U1s[:, op_ * PP:(op_ + 1) * PP])),
                                waits=[(s_ve, so)], inc=sqo[op_], inc_by=16)
                    prev_outdma[op_] = od
                    out_dma_cnt += 1

        if dbg:
            def dvd(s):
                s.dma_start(out=dbg_dv[:, 0:64], in_=VEC[:, :]).then_inc(ldm, 16)
                return s.dma_start(out=dbg_dv[:, 64:112], in_=DVEC[:, 0:48])
            P.cnt[id(ldm)] = P.val(ldm) + 16
            P.emit("sync", dvd, waits=[(sqs, P.val(sqs))], inc=ldm, inc_by=16)

        if dbg:
            first_w = [(sqs, P.val(sqs))]
            for t in range(T):
                P.emit("sync", (lambda s, t=t: s.dma_start(out=dbg_x2[t], in_=x2d[t])),
                       waits=(first_w if t == 0 else ()), inc=ldm, inc_by=16)
                P.emit("sync", (lambda s, t=t: s.dma_start(out=dbg_ci[t], in_=cind[t])),
                       inc=ldm, inc_by=16)
            P.emit("sync", lambda s: None, waits=[(ldm, P.val(ldm))])

        P.emit("sync", lambda s: None,
               waits=[(sqo[0], P.val(sqo[0])), (sqo[1], P.val(sqo[1])),
                      (sqs, P.val(sqs)), (sqr, P.val(sqr)), (ccs, P.val(ccs)),
                      (gqa[0], P.val(gqa[0])), (gqa[1], P.val(gqa[1]))])

        block = e(nc.Block())

        @block.sync
        def _(s):
            P.replay("sync", s)

        @block.tensor
        def _(t_):
            P.replay("tensor", t_)

        @block.vector
        def _(v):
            P.replay("vector", v)

        @block.scalar
        def _(s):
            P.replay("scalar", s)

        @block.gpsimd
        def _(g_):
            P.replay("gpsimd", g_)

    return nc



def _pack_vec(v):
    return np.ascontiguousarray(v.reshape(8, 128).T)


def prep_inputs(inputs, T, PP):
    NG = T // GT
    GF = NCH * GT * PP
    x = np.asarray(inputs["L0_attn_x"], dtype=np.float32)
    Wa = np.asarray(inputs["L0_attn_Wa"], np.float32)
    Wx = np.asarray(inputs["L0_attn_Wx"], np.float32)
    weights = {
        "WaT": np.ascontiguousarray(Wa.T), "WxT": np.ascontiguousarray(Wx.T),
        "wfT": np.ascontiguousarray(np.asarray(inputs["L0_ffn_wf"], np.float32).T),
        "wiT": np.ascontiguousarray(np.asarray(inputs["L0_ffn_wi"], np.float32).T),
        "wcT": np.ascontiguousarray(np.asarray(inputs["L0_ffn_wc"], np.float32).T),
        "ufT": np.ascontiguousarray(np.asarray(inputs["L0_ffn_uf"], np.float32).T),
        "uiT": np.ascontiguousarray(np.asarray(inputs["L0_ffn_ui"], np.float32).T),
        "ucT": np.ascontiguousarray(np.asarray(inputs["L0_ffn_uc"], np.float32).T),
        "woT": np.ascontiguousarray(np.asarray(inputs["L0_ffn_wo"], np.float32).T),
        "uoT": np.ascontiguousarray(np.asarray(inputs["L0_ffn_uo"], np.float32).T),
    }
    wmod_full = np.asarray(inputs["L0_wmod"], np.float32)
    cc = np.asarray(inputs["L0_c"], np.float32)
    scT = np.ascontiguousarray(cc.reshape(T, 8, 128).transpose(2, 1, 0).reshape(128, 8 * T))
    vecs = np.zeros((128, 64), np.float32)
    order = ["L0_attn_ba", "L0_attn_bx", "L0_ffn_bf", "L0_ffn_bi", "L0_ffn_bc",
             "L0_ffn_bo", "L0_ffn_in_c", "L0_attn_Lam"]
    for i, n in enumerate(order):
        vecs[:, 8 * i:8 * (i + 1)] = _pack_vec(np.asarray(inputs[n], np.float32))

    in_maps = []
    for core in range(NCORES):
        s0 = core * PP
        xc = x[:, s0:s0 + PP, :]
        xr = xc.reshape(NG, GT, PP, NCH, 128)
        xhost = np.ascontiguousarray(xr.transpose(0, 4, 3, 1, 2).reshape(NG, 128, GF))
        masks = np.zeros((128, 128), np.float32)
        for c in range(8):
            for j in range(8):
                masks[:, c * 8 + j] = 1.0 if j < core else 0.0
                masks[:, 64 + c * 8 + j] = 0.0 if j < core else 1.0
        m = {"xh": xhost, "scT": scT, "vecs": vecs, "cmask": masks,
             "wmod": np.ascontiguousarray(wmod_full[:, core * 768:(core + 1) * 768])}
        m.update(weights)
        in_maps.append(m)
    return in_maps


def assemble(results, T, PP):
    S = NCORES * PP
    out = np.empty((T, S, D), np.float32)
    for core in range(NCORES):
        oc = results[core]["outp"]
        o = oc.reshape(T, 128, NCH, PP).transpose(0, 3, 2, 1).reshape(T, PP, D)
        out[:, core * PP:(core + 1) * PP, :] = o
    return out


def kernel(**inputs):
    T, PP = 16, 256
    nc = build_nc(T, PP)
    in_maps = prep_inputs(inputs, T, PP)
    res = run_bass_kernel_spmd(nc, in_maps, core_ids=list(range(NCORES)))
    return assemble(res.results, T, PP)


# version 14
# speedup vs baseline: 1.0623x; 1.0009x over previous
import os as _os
import numpy as np
import concourse.bass as bass
import concourse.mybir as mybir
from concourse.bass_utils import run_bass_kernel_spmd
from contextlib import ExitStack

F32 = mybir.dt.float32
BF16 = mybir.dt.bfloat16
AF = mybir.ActivationFunctionType
OP = mybir.AluOpType
AX = mybir.AxisListType.X

NCORES = 8
D = 1024
NCH = 8
GT = 2
LN2C = 0.6931471805599453

ENGS = ("sync", "tensor", "vector", "scalar", "gpsimd")


PARANOID = _os.environ.get("KPARANOID", "0") == "1"
SCQ = "sync" if _os.environ.get("KSYNCQ", "0") == "1" else "scalar"


class Prog:

    def __init__(self):
        self.ops = {e: [] for e in ENGS}
        self.cnt = {}
        self.esem = {}

    def set_esems(self, m):
        self.esem = m

    def val(self, sem):
        return self.cnt.get(id(sem), 0)

    def emit(self, eng, builder, waits=(), inc=None, inc_by=1):
        waits = list(waits)
        if PARANOID and self.esem:
            for k in ("te", "ve", "sc", "gp"):
                s = self.esem[k]
                waits.append((s, self.val(s)))
            if inc is None and eng in self.esem_by_eng:
                inc = self.esem_by_eng[eng]
                inc_by = 1
        after = None
        if inc is not None:
            after = self.val(inc) + inc_by
            self.cnt[id(inc)] = after
        self.ops[eng].append((waits, builder, inc, inc_by))
        return after

    def replay(self, eng_name, e):
        for waits, builder, inc, inc_by in self.ops[eng_name]:
            for sem, v in waits:
                if v is None or v == 0:
                    continue
                e.wait_ge(sem, v)
            ins = builder(e)
            if inc is not None:
                if ins is None:
                    if PARANOID:
                        continue
                    assert False, "builder for inc op returned no instruction"
                ins.then_inc(inc, inc_by)


def sap(t, off, dims):
    return bass.AP(t, off, dims)


def build_nc(T, PP, dbg=False):
    NG = T // GT
    GPP = GT * PP
    GF = NCH * GPP
    TF = NCH * PP
    JC = 6 * D // 128
    MODW = JC * T

    nc = bass.Bass("TRN2", target_bir_lowering=False, detect_race_conditions=False)

    xh = nc.dram_tensor("xh", [NG, 128, GF], F32, kind="ExternalInput")
    wnames = ["WaT", "WxT", "wfT", "wiT", "wcT", "ufT", "uiT", "ucT", "woT", "uoT"]
    wd = {n: nc.dram_tensor(n, [D, D], F32, kind="ExternalInput") for n in wnames}
    wmodd = nc.dram_tensor("wmod", [D, 6 * D // NCORES], F32, kind="ExternalInput")
    sctd = nc.dram_tensor("scT", [128, NCH * T], F32, kind="ExternalInput")
    vecsd = nc.dram_tensor("vecs", [128, 64], F32, kind="ExternalInput")
    masksd = nc.dram_tensor("cmask", [128, 128], F32, kind="ExternalInput")
    outp = nc.dram_tensor("outp", [T, 128, TF], F32, kind="ExternalOutput")
    x2d = nc.dram_tensor("x2d", [T, 128, TF], F32)
    rowsd = nc.dram_tensor("rowsd", [NG, 2, GPP], F32)
    cind = nc.dram_tensor("cin", [T, 128, 16], F32)
    coutd = nc.dram_tensor("cout", [T, 8, 128, 16], F32)
    modind = nc.dram_tensor("modin", [128, 6 * NCH * T // NCORES], F32)
    modoutd = nc.dram_tensor("modout", [NCORES, 128, 6 * NCH * T // NCORES], F32)
    dbg_x2 = nc.dram_tensor("dbg_x2", [T, 128, NCH * PP], F32, kind="ExternalOutput") if dbg else None
    dbg_a = nc.dram_tensor("dbg_a", [T // GT, 128, NCH * GT * PP], F32, kind="ExternalOutput") if dbg else None
    dbg_b = nc.dram_tensor("dbg_b", [T // GT, 128, NCH * GT * PP], F32, kind="ExternalOutput") if dbg else None
    dbg_ci = nc.dram_tensor("dbg_ci", [T, 128, 16], F32, kind="ExternalOutput") if dbg else None
    dbg_h1 = nc.dram_tensor("dbg_h1", [T // GT, 128, NCH * GT * PP], F32, kind="ExternalOutput") if dbg else None
    dbg_dv = nc.dram_tensor("dbg_dv", [128, 128], F32, kind="ExternalOutput") if dbg else None
    dbg_cmb = nc.dram_tensor("dbg_cmb", [128, 512], F32, kind="ExternalOutput") if dbg else None
    dbg_mod = nc.dram_tensor("dbg_mod", [128, 6 * NCH * T], F32, kind="ExternalOutput") if dbg else None

    with ExitStack() as stack:
        e = stack.enter_context

        wslot = [e(nc.sbuf_tensor(f"wslot{i}", [128, 8192], BF16)) for i in range(6)]
        Bx = wslot[2].ap().bitcast(F32)
        Bh1 = wslot[3].ap().bitcast(F32)
        Ba = wslot[4].ap().bitcast(F32)
        Baa = wslot[5].ap().bitcast(F32)
        Bh1b = e(nc.sbuf_tensor("Bh1b", [128, 4096], BF16))
        XBIG = e(nc.sbuf_tensor("XBIG", [128, 8192], BF16))
        XBF = XBIG.ap().bitcast(F32)
        SCRA = e(nc.sbuf_tensor("SCRA", [128, 2 * GPP], F32))
        SCRX = e(nc.sbuf_tensor("SCRX", [128, 2 * GPP], F32))
        Y = e(nc.sbuf_tensor("Y", [128, TF], F32))
        MB = e(nc.sbuf_tensor("MB", [128, GPP], F32))
        RSTD = e(nc.sbuf_tensor("RSTD", [128, GPP], F32))
        MB2 = e(nc.sbuf_tensor("MB2", [128, GPP], F32))
        RSTD2 = e(nc.sbuf_tensor("RSTD2", [128, GPP], F32))
        VAR = e(nc.sbuf_tensor("VAR", [128, GPP], F32))
        VAR2 = e(nc.sbuf_tensor("VAR2", [128, GPP], F32))
        VT = e(nc.sbuf_tensor("VT", [128, GPP], F32))
        VTB = e(nc.sbuf_tensor("VTB", [128, GPP], F32))
        CARRY = e(nc.sbuf_tensor("CARRY", [128, 32], F32))
        AGB = e(nc.sbuf_tensor("AGB", [128, 256], F32))
        HIN = e(nc.sbuf_tensor("HIN", [128, 16], F32))
        CMB1 = e(nc.sbuf_tensor("CMB1", [128, 64], F32))
        CMB2 = e(nc.sbuf_tensor("CMB2", [128, 64], F32))
        CMB3 = e(nc.sbuf_tensor("CMB3", [128, 64], F32))
        ATOT = e(nc.sbuf_tensor("ATOT", [128, 16], F32))
        THS = e(nc.sbuf_tensor("THS", [128, 16], F32))
        MOD = e(nc.sbuf_tensor("MOD", [128, MODW], F32))
        SCT = e(nc.sbuf_tensor("SCT", [128, NCH * T], F32))
        SCT2 = e(nc.sbuf_tensor("SCT2", [128, NCH * T], F32))
        VEC = e(nc.sbuf_tensor("VEC", [128, 64], F32))
        DVEC = e(nc.sbuf_tensor("DVEC", [128, 64], F32))
        MASK = e(nc.sbuf_tensor("MASK", [128, 128], F32))
        ONESB = e(nc.sbuf_tensor("ONESB", [128, 128], BF16))
        ONESF = e(nc.sbuf_tensor("ONESF", [128, 128], F32))
        ONES5 = e(nc.sbuf_tensor("ONES5", [128, 512], F32))
        WST = e(nc.sbuf_tensor("WST", [128, 4096], BF16))
        WSTf = WST.ap().bitcast(F32)
        ROWSTAGE = e(nc.sbuf_tensor("ROWSTAGE", [128, 4 * PP], F32))
        C = e(nc.sbuf_tensor("C", [128, TF], F32))
        CB = e(nc.sbuf_tensor("CB", [128, 2 * 2048], BF16))
        H2B = e(nc.sbuf_tensor("H2B", [128, 2 * 2048], BF16))
        THF = e(nc.sbuf_tensor("THF", [128, 2 * PP], F32))
        THI = e(nc.sbuf_tensor("THI", [128, 2 * PP], F32))
        THG = e(nc.sbuf_tensor("THG", [128, 2 * PP], F32))
        U1s = e(nc.sbuf_tensor("U1s", [128, 2 * PP], F32))
        U2s = e(nc.sbuf_tensor("U2s", [128, 2 * PP], F32))
        Z3 = THF.ap()
        Z3B = U2s.ap()
        CONSTS = e(nc.sbuf_tensor("CONSTS", [128, 1], F32))

        ps_a = e(nc.psum_tensor("ps_a", [128, GPP], F32))
        ps_b = e(nc.psum_tensor("ps_b", [128, GPP], F32))
        ps_c = e(nc.psum_tensor("ps_c", [128, GPP], F32))
        ps_d = e(nc.psum_tensor("ps_d", [128, GPP], F32))
        ps_mod = e(nc.psum_tensor("ps_mod", [128, max(MODW, 3 * PP)], F32))
        ps_g = e(nc.psum_tensor("ps_g", [128, 3 * PP], F32))

        ldm = e(nc.semaphore("ldm"))
        ldw0 = e(nc.semaphore("ldw0"))
        ldw1 = e(nc.semaphore("ldw1"))
        ldv0 = e(nc.semaphore("ldv0"))
        ldv1 = e(nc.semaphore("ldv1"))
        ldx = e(nc.semaphore("ldx"))
        ldx2 = [e(nc.semaphore("ldx2_0")), e(nc.semaphore("ldx2_1"))]
        ldr = [e(nc.semaphore("ldr0")), e(nc.semaphore("ldr1"))]
        sqs = e(nc.semaphore("sqs"))
        sqr = e(nc.semaphore("sqr"))
        sqo = [e(nc.semaphore("sqo0")), e(nc.semaphore("sqo1"))]
        gqc = [e(nc.semaphore("gqc0")), e(nc.semaphore("gqc1"))]
        gqa = [e(nc.semaphore("gqa0")), e(nc.semaphore("gqa1"))]
        ccs = e(nc.semaphore("ccs"))
        s_te = e(nc.semaphore("s_te"))
        s_ve = e(nc.semaphore("s_ve"))
        s_sc = e(nc.semaphore("s_sc"))
        s_gp = e(nc.semaphore("s_gp"))

        MODL_BUF = VT
        P = Prog()
        P.esem_by_eng = {"tensor": s_te, "vector": s_ve, "scalar": s_sc, "gpsimd": s_gp}
        P.set_esems({"te": s_te, "ve": s_ve, "sc": s_sc, "gp": s_gp,
                     "ldm": ldm, "ldw0": ldw0, "ldw1": ldw1, "ldv0": ldv0, "ldv1": ldv1,
                     "ldx": ldx, "ldx20": ldx2[0], "ldx21": ldx2[1],
                     "ldr0": ldr[0], "ldr1": ldr[1], "sqs": sqs, "sqr": sqr,
                     "sqo0": sqo[0], "sqo1": sqo[1], "gqc0": gqc[0], "gqc1": gqc[1],
                     "gqa0": gqa[0], "gqa1": gqa[1], "ccs": ccs})

        VORD = ["ba", "bx", "bf", "bi", "bc", "bo", "inc", "lam"]
        vcol = {n: 8 * i for i, n in enumerate(VORD)}
        dcol = {"bah": 0, "bxh": 8, "bfh": 16, "bih": 24, "sh": 32, "s256": 40}

        def vc(name, c):
            return VEC[:, vcol[name] + c: vcol[name] + c + 1]

        def dc(name, c):
            return DVEC[:, dcol[name] + c: dcol[name] + c + 1]

        def modcol(g, c, t):
            return MOD[:, g * NCH * T + c * T + t: g * NCH * T + c * T + t + 1]

        def modcol_bc(g, c, t, n):
            off = g * NCH * T + c * T + t
            return sap(MOD, off, [[MODW, 128], [0, n]])

        def gsl(ap, c, t2=None):
            if t2 is None:
                return ap[:, c * GPP:(c + 1) * GPP]
            return ap[:, c * GPP + t2 * PP: c * GPP + (t2 + 1) * PP]

        def b(f):
            return f

        P.emit("sync", lambda s: s.dma_start(out=VEC[:, :], in_=vecsd[:, :]), inc=ldm, inc_by=16)
        P.emit("sync", lambda s: s.dma_start(out=SCT[:, :], in_=sctd[:, :]), inc=ldm, inc_by=16)
        ld_msk = P.emit("sync", lambda s: s.dma_start(out=MASK[:, :], in_=masksd[:, :]), inc=ldm, inc_by=16)
        if dbg:
            P.emit("sync", lambda s: s.dma_start(out=dbg_dv[:, 112:128], in_=masksd[:, 0:16]), inc=ldm, inc_by=16)

        gp_ones = P.emit("gpsimd", lambda g: (g.memset(ONESB[:, :], 1.0),
                                              g.memset(ONESF[:, :], 1.0),
                                              g.memset(ONES5[:, :], 1.0),
                                              g.memset(CONSTS[:, :], 0.25),
                                              g.memset(ROWSTAGE[:, :], 0.0))[-1],
                         inc=s_gp)

        def deriv_a(s):
            s.mul(DVEC[:, 0:8], VEC[:, vcol["ba"]:vcol["ba"] + 8], 0.5)
            s.mul(DVEC[:, 8:16], VEC[:, vcol["bx"]:vcol["bx"] + 8], 0.5)
            s.mul(DVEC[:, 16:24], VEC[:, vcol["bf"]:vcol["bf"] + 8], 0.5)
            s.mul(DVEC[:, 24:32], VEC[:, vcol["bi"]:vcol["bi"] + 8], 0.5)
            return s.activation(VT[:, 0:8], VEC[:, vcol["lam"]:vcol["lam"] + 8], AF.Exp, scale=-1.0)
        d1v = P.emit("scalar", deriv_a, waits=[(ldm, ld_msk)], inc=s_sc)
        d2v = P.emit("scalar", lambda s: s.activation(VT[:, 8:16], VT[:, 0:8], AF.Ln, bias=1.0),
                     waits=[(s_sc, d1v)], inc=s_sc)
        d3v = P.emit("scalar", lambda s: s.mul(DVEC[:, 32:40], VT[:, 8:16], -4.0),
                     waits=[(s_sc, d2v)], inc=s_sc)
        d4v = P.emit("scalar", lambda s: s.mul(DVEC[:, 40:48], DVEC[:, 32:40], float(PP)),
                     waits=[(s_sc, d3v)], inc=s_sc)
        sc_deriv = P.emit("scalar", lambda s: s.activation(SCT2[:, :], SCT[:, :], AF.Silu),
                          waits=[(s_sc, d4v)], inc=s_sc)

        JCL = JC // NCORES
        MODL = JCL * T
        halfW = JCL * 128
        stage_slots = [Bx, Bh1]
        band_ld = []
        mm_hist = []
        acc_hist = []
        for k in range(NCH):
            s0 = stage_slots[k % 2]
            w = [(s_te, mm_hist[k - 2] if k >= 2 else None)]
            lw = ldw0 if k % 2 == 0 else ldw1
            a1 = P.emit("sync", (lambda s, k=k, s0=s0: s.dma_start(out=s0[:, 0:halfW], in_=wmodd[k * 128:(k + 1) * 128, :])),
                        waits=w, inc=lw, inc_by=16)

            def mmb(t_, k=k, s0=s0):
                last = None
                for jc in range(JCL):
                    last = t_.matmul(ps_mod[:, jc * T:(jc + 1) * T],
                                     s0[:, jc * 128:(jc + 1) * 128],
                                     SCT2[:, k * T:(k + 1) * T],
                                     start=True, stop=True)
                return last
            w_mm = [(lw, a1), (s_sc, sc_deriv)]
            if k > 0:
                w_mm.append((s_ve, acc_hist[k - 1]))
            mm_hist.append(P.emit("tensor", mmb, waits=w_mm, inc=s_te))

            def accb(v, k=k):
                sv = Y[:, 0:MODL]
                if k == 0:
                    return v.tensor_copy(sv, ps_mod[:, 0:MODL])
                src_, dst_ = (sv, MODL_BUF[:, 0:MODL]) if k % 2 == 1 else (MODL_BUF[:, 0:MODL], sv)
                return v.tensor_tensor(dst_, src_, ps_mod[:, 0:MODL], OP.add)
            acc_hist.append(P.emit("vector", accb, waits=[(s_te, mm_hist[k])], inc=s_ve))
        mm_after = mm_hist[-1]

        md1 = P.emit("gpsimd", lambda g_: g_.dma_start(out=modind[:, :], in_=MODL_BUF[:, 0:MODL]),
                     waits=[(s_ve, acc_hist[-1])], inc=gqc[0], inc_by=16)
        mag = P.emit("gpsimd", lambda g_: g_.collective_compute(
            "AllGather", OP.bypass, replica_groups=[list(range(NCORES))],
            ins=[modind.ap().opt()], outs=[modoutd.ap().opt()]),
            waits=[(gqc[0], md1)], inc=ccs)
        md2 = P.emit("gpsimd", lambda g_: g_.dma_start(
            out=MOD[:, :],
            in_=sap(modoutd, 0, [[MODL, 128], [128 * MODL, NCORES], [1, MODL]])),
            waits=[(ccs, mag)], inc=gqc[0], inc_by=16)
        acc_hist.append(P.emit("vector", lambda v: v.engine_nop(), waits=[(gqc[0], md2)], inc=s_ve))

        NT = NCH * T
        e1 = P.emit("vector", lambda v: (v.tensor_scalar_add(VT[:, 0:NT], MOD[:, 1 * NT:2 * NT], 1.0),
                                         v.tensor_scalar_add(VTB[:, 0:NT], MOD[:, 4 * NT:5 * NT], 1.0))[-1],
                    waits=[(s_ve, acc_hist[-1])], inc=s_ve)
        ve_add1 = P.emit("vector", lambda v: (v.tensor_copy(MOD[:, 1 * NT:2 * NT], VT[:, 0:NT]),
                                              v.tensor_copy(MOD[:, 4 * NT:5 * NT], VTB[:, 0:NT]))[-1],
                         waits=[(s_ve, e1)], inc=s_ve)
        ve_modcl = ve_add1
        for gsec in (2, 5):
            lo = gsec * NT
            sA = P.emit("scalar", (lambda s, lo=lo: (
                s.sign(VT[:, 0:NT], MOD[:, lo:lo + NT]),
                s.activation(VTB[:, 0:NT], MOD[:, lo:lo + NT], AF.Abs))[-1]),
                waits=[(s_ve, ve_modcl)], inc=s_sc)
            eM = P.emit("vector", lambda v: v.tensor_scalar_max(SCRA[:, 0:NT], VTB[:, 0:NT], 0.25),
                        waits=[(s_sc, sA), (s_ve, ve_modcl)], inc=s_ve)
            ve_modcl = P.emit("vector", (lambda v, lo=lo:
                              v.tensor_tensor(MOD[:, lo:lo + NT], VT[:, 0:NT], SCRA[:, 0:NT], OP.mult)),
                              waits=[(s_ve, eM)], inc=s_ve)
        if dbg:
            ve_modcl = P.emit("vector", lambda v: v.engine_nop(), waits=[(s_ve, ve_modcl)], inc=s_ve)
            dmod_e = P.emit("sync", lambda s: s.dma_start(out=dbg_mod[:, :], in_=MOD[:, :]),
                            waits=[(s_ve, ve_modcl)], inc=ldm, inc_by=16)
            ve_modcl = P.emit("vector", lambda v: v.engine_nop(), waits=[(ldm, dmod_e)], inc=s_ve)
        mod_ready_v = ve_modcl

        def load_weight(slot_idx, dram, waits_extra=()):
            done = None
            for k in range(NCH):
                half = load_weight.seq % 2
                load_weight.seq += 1
                w = [(s_sc, load_weight.cast_hist.get(load_weight.seq - 2))]
                if k < 2:
                    w.extend(waits_extra)
                lv = ldv0 if half == 0 else ldv1
                a = P.emit("sync", (lambda s, k=k, half=half, dram=dram:
                                    s.dma_start(out=WSTf[:, half * 1024:(half + 1) * 1024],
                                                in_=dram[k * 128:(k + 1) * 128, :])),
                           waits=w, inc=lv, inc_by=16)
                done = P.emit("scalar", (lambda s, k=k, half=half, si=slot_idx:
                                         s.copy(wslot[si][:, k * 1024:(k + 1) * 1024],
                                                WSTf[:, half * 1024:(half + 1) * 1024])),
                              waits=[(lv, a)], inc=s_sc)
                load_weight.cast_hist[load_weight.seq] = done
            return done
        load_weight.cast_hist = {}
        load_weight.seq = 0


        CBf = CB.ap().bitcast(F32)
        NGq = NG

        def bx_pieces(g):
            if g % 2 == 0:
                return [(Bx, 0, 8)]
            return [(C.ap(), 0, 4), (CBf, 4, 4)]

        def bxsl(g, c, t2=None):
            for ap, base, n in bx_pieces(g):
                if base <= c < base + n:
                    off = (c - base) * GPP
                    if t2 is None:
                        return ap[:, off: off + GPP]
                    return ap[:, off + t2 * PP: off + (t2 + 1) * PP]

        def bxhalf(g, h):
            for ap, base, n in bx_pieces(g):
                if base <= 4 * h < base + n:
                    off = (4 * h - base) * GPP
                    return ap[:, off: off + 4 * GPP]

        def hsl(c, t2=None):
            if t2 is None:
                return XBF[:, c * GPP:(c + 1) * GPP]
            return XBF[:, c * GPP + t2 * PP: c * GPP + (t2 + 1) * PP]

        def lanex(g):
            return ldx if g % 2 == 0 else ldx2[1]

        x_ld = [None] * NGq
        x_ld_half = [None] * NGq
        pcast = [None] * NGq
        ln2cast_h = [None] * NGq
        spill_done = [None] * NGq
        ln2mm = [None] * NGq
        ln2ev = [None] * NGq
        ln2sc = [None] * NGq
        ln2cast = [None] * NGq
        rows_done = [None] * NGq
        stats_of = [None] * NGq
        var_of = [None] * NGq
        rstd_of = [None] * NGq
        last_rescan = [None, None]
        prev_x2 = None
        prev_b = None
        b_hi_ve = None
        prev_u_last = None
        prev_carry_dma = [None, None]
        prev_agb_cmb = [None, None]
        prev_zx_te = None
        prev_zx_thx_last = None
        prev_extract_gp = None
        prev_dbgb = None
        prev_dbgh1 = None
        sc_sr2 = None
        sc_atot = None
        ve_h1 = None

        def emit_xload(g):
            halves = []
            for h in range(2):
                w = []
                if g == 0 and h == 0:
                    w.append((s_te, mm_after))
                if g >= 2:
                    w.append((s_gp, ln2cast_h[g - 2][h]))
                    if h == 0:
                        w.append((sqs, spill_done[g - 2]))
                v = P.emit("sync", (lambda s, g=g, h=h:
                                    s.dma_start(out=bxhalf(g, h),
                                                in_=xh[g][:, h * 4 * GPP:(h + 1) * 4 * GPP])),
                           waits=w, inc=lanex(g), inc_by=16)
                halves.append(v)
            x_ld_half[g] = halves
            x_ld[g] = halves[-1]

        def emit_prelude_casts(g):
            w_free = []
            if g == 1:
                w_free.append((s_te, stats_of[0]))
            elif g >= 2:
                w_free.append((s_te, ln2mm[g - 2]))
            v = None
            for h in range(2):
                P.emit("scalar", (lambda s, g2=g, h=h:
                                  s.copy(WST[:, h * 4 * GPP:(h + 1) * 4 * GPP], bxhalf(g2, h))),
                       waits=[(lanex(g), x_ld_half[g][h])] + (w_free if h == 0 else []), inc=s_sc)
                v = P.emit("scalar", (lambda s, g2=g, h=h:
                                      s.activation(H2B[:, h * 4 * GPP:(h + 1) * 4 * GPP],
                                                   bxhalf(g2, h), AF.Square)),
                           inc=s_sc)
            pcast[g] = v

        def emit_prelude_stats(g):
            def stats_mm(t_):
                last = None
                for c in range(NCH):
                    t_.matmul(ps_c[:, :], ONESB[:, :], WST[:, c * GPP:(c + 1) * GPP],
                              start=(c == 0), stop=(c == NCH - 1))
                    last = t_.matmul(ps_d[:, :], ONESB[:, :], H2B[:, c * GPP:(c + 1) * GPP],
                                     start=(c == 0), stop=(c == NCH - 1))
                return last
            w_st = [(s_sc, pcast[g]), (s_gp, gp_ones)]
            if g > 0:
                w_st.append((s_ve, var_of[g - 1]))
            stats_of[g] = P.emit("tensor", stats_mm, waits=w_st, inc=s_te)

            sc_mb = P.emit("scalar", lambda s: s.activation(MB[:, :], ps_c[:, :], AF.Copy, scale=1.0 / D),
                           waits=[(s_te, stats_of[g])], inc=s_sc)
            v1 = P.emit("vector", lambda v: v.tensor_tensor(VT[:, :], MB[:, :], MB[:, :], OP.mult),
                        waits=[(s_sc, sc_mb)], inc=s_ve)
            v2 = P.emit("vector", lambda v: v.tensor_scalar_sub(VTB[:, :], VT[:, :], 1e-5),
                        waits=[(s_ve, v1)], inc=s_ve)
            var_of[g] = P.emit("vector", lambda v: v.scalar_tensor_tensor(VAR[:, :], ps_d[:, :], 1.0 / D,
                                                                          VTB[:, :], OP.mult, OP.subtract),
                               waits=[(s_ve, v2)], inc=s_ve)

        def emit_nlw2(g):
            nv = P.emit("scalar", lambda s: s.activation(VT[:, :], VAR[:, :], AF.Ln),
                        waits=[(s_ve, var_of[g])], inc=s_sc)
            rstd_of[g] = P.emit("scalar", lambda s: s.activation(RSTD[:, :], VT[:, :], AF.Exp, scale=-0.5),
                                waits=[(s_sc, nv)], inc=s_sc)

        def emit_lnblock(g):
            nonlocal ve_h1
            lt1 = None
            for ap, base, n in bx_pieces(g):
                def ln_t1(v, ap=ap, base=base, n=n):
                    bx3 = ap[:, 0:n * GPP].rearrange("p (c q) -> p c q", c=n)
                    bh3 = XBF[:, base * GPP:(base + n) * GPP].rearrange("p (c q) -> p c q", c=n)
                    mb_bc = sap(MB, 0, [[GPP, 128], [0, n], [1, GPP]])
                    return v.tensor_tensor(bh3, bx3, mb_bc, OP.subtract)
                w_t1 = []
                if base == 0:
                    w_t1 = [(s_sc, rstd_of[g]), (lanex(g), x_ld[g])]
                    if dbg and prev_dbgh1 is not None:
                        w_t1.append((ldm, prev_dbgh1))
                lt1 = P.emit("vector", ln_t1, waits=w_t1, inc=s_ve)

            def ln_t2(v):
                bh3 = XBF[:, 0:GF].rearrange("p (c q) -> p c q", c=NCH)
                rs_bc = sap(RSTD, 0, [[GPP, 128], [0, NCH], [1, GPP]])
                return v.tensor_tensor(bh3, bh3, rs_bc, OP.mult)
            lt2 = P.emit("vector", ln_t2, waits=[(s_ve, lt1)], inc=s_ve)

            def h1_build(v, g=g):
                last = None
                for c in range(NCH):
                    for t2 in range(GT):
                        t = GT * g + t2
                        last = v.scalar_tensor_tensor(
                            hsl(c, t2), hsl(c, t2),
                            modcol(1, c, t), modcol_bc(0, c, t, PP),
                            OP.mult, OP.add)
                return last
            ve_h1 = P.emit("vector", h1_build, waits=[(s_ve, mod_ready_v), (s_ve, lt2)], inc=s_ve)

        def emit_tail_scanA(g):
            nonlocal prev_extract_gp
            hbuf = [Y, Baa]
            ag_done = {}
            agv_of = {}
            for t2 in range(GT):
                t = GT * g + t2
                par = t % 2
                hb = hbuf[t2]

                def scan1(v, t2=t2, hb=hb):
                    last = None
                    for c in range(NCH):
                        last = v.tensor_tensor_scan(hb[:, c * PP:(c + 1) * PP],
                                                    gsl(Ba, c, t2), gsl(Bh1, c, t2),
                                                    0.0, OP.mult, OP.add)
                    return last
                ve_s1 = P.emit("vector", scan1, waits=[(s_gp, prev_b), (s_ve, b_hi_ve)], inc=s_ve)

                def extract(g_, t2=t2, par=par, hb=hb):
                    for c in range(NCH):
                        g_.tensor_copy(CARRY[:, par * 16 + c: par * 16 + c + 1],
                                       hb[:, c * PP + PP - 1: c * PP + PP])
                    return g_.tensor_copy(sap(CARRY, par * 16 + 8, [[32, 128], [1, 8]]),
                                          sap(ATOT, t2, [[16, 128], [GT, 8]]))
                w_ex = [(s_ve, ve_s1), (s_sc, sc_atot)]
                if prev_carry_dma[par] is not None:
                    w_ex.append((gqc[par], prev_carry_dma[par]))
                gp_ex = P.emit("gpsimd", extract, waits=w_ex, inc=s_gp)
                if t2 == GT - 1:
                    prev_extract_gp = gp_ex

                d1 = P.emit("gpsimd", (lambda g_, t=t, par=par:
                                       g_.dma_start(out=cind[t], in_=CARRY[:, par * 16:(par + 1) * 16])),
                            waits=[(s_gp, gp_ex)], inc=gqc[par], inc_by=16)

                agv_of[t2] = P.emit("gpsimd", (lambda g_, t=t: g_.collective_compute(
                    "AllGather", OP.bypass,
                    replica_groups=[list(range(NCORES))],
                    ins=[cind[t].opt()],
                    outs=[coutd[t].opt()])),
                    waits=[(gqc[par], d1)], inc=ccs)
                prev_carry_dma[par] = d1

            for t2 in range(GT):
                t = GT * g + t2
                par = t % 2
                w_ab = [(ccs, agv_of[t2])]
                if prev_agb_cmb[par] is not None:
                    w_ab.append((s_ve, prev_agb_cmb[par]))
                d2 = P.emit("gpsimd", (lambda g_, t=t, par=par:
                                       g_.dma_start(out=AGB[:, par * 128:(par + 1) * 128],
                                                    in_=sap(coutd, t * 8 * 128 * 16, [[16, 128], [2048, 8], [1, 16]]))),
                            waits=w_ab, inc=gqa[par], inc_by=16)
                ag_done[t2] = d2
            return hbuf, ag_done

        def emit_tail_scanB(g, hbuf, ag_done):
            nonlocal prev_x2
            for t2 in range(GT):
                t = GT * g + t2
                par = t % 2
                hb = hbuf[t2]

                def cmb_a(v, par=par):
                    at_ap = sap(AGB, par * 128 + 8, [[256, 128], [1, 8], [16, 8]])
                    he_ap = sap(AGB, par * 128 + 0, [[256, 128], [1, 8], [16, 8]])
                    am = MASK[:, 0:64].rearrange("p (c j) -> p c j", c=8)
                    c1 = CMB1[:, :].rearrange("p (c j) -> p c j", c=8)
                    c2 = CMB2[:, :].rearrange("p (c j) -> p c j", c=8)
                    v.tensor_tensor(c2, he_ap, am, OP.mult)
                    return v.tensor_tensor(c1, at_ap, am, OP.mult)

                def cmb_b(v):
                    c1 = CMB1[:, :].rearrange("p (c j) -> p c j", c=8)
                    bm = MASK[:, 64:128].rearrange("p (c j) -> p c j", c=8)
                    return v.tensor_tensor(c1, c1, bm, OP.add)

                def cmb_c(v):
                    last = None
                    for c in range(NCH):
                        last = v.tensor_tensor_scan(CMB3[:, c * 8:(c + 1) * 8],
                                                    CMB1[:, c * 8:(c + 1) * 8],
                                                    CMB2[:, c * 8:(c + 1) * 8],
                                                    0.0, OP.mult, OP.add)
                    return last
                w_cmb = [(gqa[par], ag_done[t2])]
                if t < 2:
                    w_cmb.append((ldm, ld_msk))
                ca_ = P.emit("vector", cmb_a, waits=w_cmb, inc=s_ve)
                cb_ = P.emit("vector", cmb_b, waits=[(s_ve, ca_)], inc=s_ve)
                cc_ = P.emit("vector", cmb_c, waits=[(s_ve, cb_)], inc=s_ve)
                cmb = P.emit("vector", (lambda v, par=par:
                                        v.tensor_copy(HIN[:, par * 8:(par + 1) * 8],
                                                      sap(CMB3, 7, [[64, 128], [8, 8]]))),
                             waits=[(s_ve, cc_)], inc=s_ve)
                prev_agb_cmb[par] = cmb

                def rescan(v, t2=t2, par=par, hb=hb):
                    last = None
                    for c in range(NCH):
                        last = v.tensor_tensor_scan(hb[:, c * PP:(c + 1) * PP],
                                                    gsl(Ba, c, t2), gsl(Bh1, c, t2),
                                                    HIN[:, par * 8 + c: par * 8 + c + 1],
                                                    OP.mult, OP.add)
                    return last
                ve_rs = P.emit("vector", rescan, waits=[(s_ve, cmb)], inc=s_ve)
                last_rescan[t2] = ve_rs

                def x2b(v, t2=t2, g=g, hb=hb):
                    t = GT * g + t2
                    last = None
                    for c in range(NCH):
                        last = v.scalar_tensor_tensor(bxsl(g, c, t2), hb[:, c * PP:(c + 1) * PP],
                                                      modcol(2, c, t), bxsl(g, c, t2),
                                                      OP.mult, OP.add)
                    return last
                prev_x2 = P.emit("vector", x2b, waits=[(s_ve, ve_rs)], inc=s_ve)

                sp = None
                for pi, (ap, base, n) in enumerate(bx_pieces(g)):
                    w_sp = []
                    if pi == 0:
                        w_sp.append((s_ve, prev_x2))
                        if t2 == 0 and g > 0:
                            w_sp.append((sqs, spill_done[g - 1]))
                    def spl(s, ap=ap, base=base, n=n, t2=t2, t=t):
                        src = ap[:, 0:n * GPP].rearrange("p (c u q) -> p c u q", c=n, u=GT)[:, :, t2, :]
                        return s.dma_start(out=x2d[t][:, base * PP:(base + n) * PP], in_=src)
                    sp = P.emit("sync", spl, waits=w_sp, inc=sqs, inc_by=16)
                if t2 == GT - 1:
                    spill_done[g] = sp

        def emit_main(g):
            nonlocal prev_b, b_hi_ve, prev_u_last, prev_zx_te, prev_zx_thx_last
            nonlocal sc_sr2, sc_atot, prev_dbgb, prev_dbgh1
            if dbg:
                prev_dbgh1 = P.emit("sync", (lambda s, g=g: s.dma_start(out=dbg_h1[g], in_=XBF[:, 0:GF])),
                                    waits=[(s_ve, ve_h1)], inc=ldm, inc_by=16)
            w_h1b = [(s_ve, ve_h1)]
            if prev_zx_te is not None:
                w_h1b.append((s_te, prev_zx_te))
            gp_h1b = P.emit("scalar", lambda s: s.copy(Bh1b[:, 0:GF], XBF[:, 0:GF]),
                            waits=w_h1b, inc=s_sc)

            th_done = {}
            red_done = {}
            a_done = {}
            aa_done = {}
            z_te = {}
            for c in range(NCH):
                psz = ps_a if c % 2 == 0 else ps_b

                def zmm(t_, c=c, psz=psz):
                    last = None
                    for k in range(NCH):
                        last = t_.matmul(psz[:, :],
                                         wslot[0][:, k * 1024 + c * 128: k * 1024 + (c + 1) * 128],
                                         Bh1b[:, k * GPP:(k + 1) * GPP],
                                         start=(k == 0), stop=(k == NCH - 1))
                    return last
                w_z = [(s_sc, gp_h1b)] if c == 0 else []
                if c == 0:
                    w_z.append((s_sc, wa_done))
                    if g > 0:
                        w_z.append((s_sc, prev_zx_thx_last))
                if c >= 2:
                    w_z.append((s_sc, th_done[c - 2]))
                z_te[c] = P.emit("tensor", zmm, waits=w_z, inc=s_te)

                def tha(s, c=c, psz=psz):
                    return s.activation(SCRA[:, (c % 2) * GPP:(c % 2 + 1) * GPP], psz[:, :],
                                        AF.Tanh, scale=0.5, bias=dc("bah", c))
                w_th = [(s_te, z_te[c])]
                if c >= 2:
                    w_th.append((s_ve, red_done[c - 2]))
                    w_th.append((s_sc, a_done[c - 2]))
                th_done[c] = P.emit("scalar", tha, waits=w_th, inc=s_sc)

                w_a = [(s_sc, th_done[c])]
                if c == 0:
                    w_a.append((s_ve, last_rescan[1]))
                a_done[c] = P.emit("scalar",
                                   (lambda s, c=c: s.activation(gsl(Ba, c), SCRA[:, (c % 2) * GPP:(c % 2 + 1) * GPP],
                                                                AF.Exp, scale=dc("sh", c), bias=dc("sh", c))),
                                   waits=w_a, inc=s_sc)

                def red(v, c=c):
                    src = SCRA[:, (c % 2) * GPP:(c % 2 + 1) * GPP].rearrange("p (t q) -> p t q", t=GT)
                    return v.tensor_reduce(THS[:, c * GT:(c + 1) * GT], src, AX, OP.add)
                red_done[c] = P.emit("vector", red, waits=[(s_sc, th_done[c])], inc=s_ve)

                w_aa = [(s_sc, a_done[c])]
                if c == 0 and prev_x2 is not None:
                    w_aa.append((s_ve, prev_x2))
                aa_done[c] = P.emit("gpsimd",
                                    (lambda g_, c=c: g_.tensor_tensor(gsl(Baa, c), gsl(Ba, c), gsl(Ba, c), OP.mult)),
                                    waits=w_aa, inc=s_gp)

            u_done = {}
            thx_done = {}
            for c in range(NCH):
                psz = ps_a if c % 2 == 0 else ps_b

                def zmm(t_, c=c, psz=psz):
                    last = None
                    for k in range(NCH):
                        last = t_.matmul(psz[:, :],
                                         wslot[1][:, k * 1024 + c * 128: k * 1024 + (c + 1) * 128],
                                         Bh1b[:, k * GPP:(k + 1) * GPP],
                                         start=(k == 0), stop=(k == NCH - 1))
                    return last
                w_z = [(s_sc, th_done[NCH - 2 + (c % 2)] if c < 2 else thx_done[c - 2])]
                if c == 0:
                    w_z.append((s_sc, wx_done))
                zx_te = P.emit("tensor", zmm, waits=w_z, inc=s_te)
                if c == NCH - 1:
                    prev_zx_te = zx_te

                def thx(s, c=c, psz=psz):
                    return s.activation(SCRX[:, (c % 2) * GPP:(c % 2 + 1) * GPP], psz[:, :],
                                        AF.Tanh, scale=0.5, bias=dc("bxh", c))
                w_th = [(s_te, zx_te)]
                if c >= 2:
                    w_th.append((s_ve, u_done[c - 2]))
                thx_done[c] = P.emit("scalar", thx, waits=w_th, inc=s_sc)

                def ub(v, c=c):
                    return v.scalar_tensor_tensor(gsl(Bh1, c), SCRX[:, (c % 2) * GPP:(c % 2 + 1) * GPP],
                                                  1.0, hsl(c), OP.add, OP.mult)
                u_done[c] = P.emit("vector", ub, waits=[(s_sc, thx_done[c])], inc=s_ve)
            prev_zx_thx_last = thx_done[NCH - 1]
            prev_u_last = u_done[NCH - 1]

            def atot(s, g=g):
                last = None
                for c in range(NCH):
                    last = s.activation(ATOT[:, c * GT:(c + 1) * GT], THS[:, c * GT:(c + 1) * GT],
                                        AF.Exp, scale=dc("sh", c), bias=dc("s256", c))
                return last
            w_at = [(s_ve, red_done[NCH - 1])]
            if prev_extract_gp is not None:
                w_at.append((s_gp, prev_extract_gp))
            sc_atot = P.emit("scalar", atot, waits=w_at, inc=s_sc)

            H2 = GF // 2
            w_nl4a = [(s_gp, aa_done[NCH // 2 - 1]), (s_ve, u_done[NCH // 2 - 1]),
                      (s_gp, gp_ones)]
            if dbg:
                w_nl4a.append((ldm, prev_dbgh1))
            l1a = P.emit("scalar",
                         lambda s: s.activation(XBF[:, 0:H2], Baa[:, 0:H2], AF.Ln, scale=-0.25, bias=CONSTS[:, 0:1]),
                         waits=w_nl4a, inc=s_sc)
            l1b = P.emit("scalar",
                         lambda s: s.activation(XBF[:, H2:GF], Baa[:, H2:GF], AF.Ln, scale=-0.25, bias=CONSTS[:, 0:1]),
                         waits=[(s_gp, aa_done[NCH - 1]), (s_ve, u_done[NCH - 1])], inc=s_sc)
            P.emit("scalar",
                   lambda s: s.activation(Baa[:, 0:H2], XBF[:, 0:H2], AF.Exp, scale=0.5),
                   waits=[(s_sc, l1a)], inc=s_sc)
            sc_sr2 = P.emit("scalar",
                            lambda s: s.activation(Baa[:, H2:GF], XBF[:, H2:GF], AF.Exp, scale=0.5),
                            waits=[(s_sc, l1b)], inc=s_sc)

            if g >= 2:
                w_n1 = [(s_ve, ln2ev[g - 2])]
                if g >= 3 and rows_done[g - 3] is not None:
                    w_n1.append((sqr, rows_done[g - 3]))
                nb1 = P.emit("scalar", lambda s: s.activation(VTB[:, :], VAR2[:, :], AF.Ln),
                             waits=w_n1, inc=s_sc)
                nb2 = P.emit("scalar", lambda s: s.activation(RSTD2[:, :], VTB[:, :], AF.Exp, scale=-0.5),
                             waits=[(s_sc, nb1)], inc=s_sc)
                P.emit(SCQ, (lambda s, g=g: s.dma_start(out=rowsd[g - 2, 0], in_=MB2[0:1, :])),
                       waits=[(s_sc, nb2)], inc=sqr, inc_by=16)
                rows_done[g - 2] = P.emit(SCQ, (lambda s, g=g: s.dma_start(out=rowsd[g - 2, 1], in_=RSTD2[0:1, :])),
                                          inc=sqr, inc_by=16)

            b_hi_ve = P.emit("vector",
                             lambda v: v.tensor_tensor(Bh1[:, 0:GF], Bh1[:, 0:GF], Baa[:, 0:GF], OP.mult),
                             waits=[(s_sc, sc_sr2), (s_ve, u_done[NCH - 1])], inc=s_ve)
            prev_b = None

            if dbg:
                P.emit("sync", (lambda s, g=g: s.dma_start(out=dbg_a[g], in_=Ba[:, 0:GF])),
                       waits=[(s_ve, b_hi_ve)], inc=ldm, inc_by=16)
                prev_dbgb = P.emit("sync", (lambda s, g=g: s.dma_start(out=dbg_b[g], in_=Bh1[:, 0:GF])),
                                   inc=ldm, inc_by=16)
                P.emit("vector", lambda v: v.engine_nop(), waits=[(ldm, prev_dbgb)])

        def emit_ln2_casts(g):
            w_c1 = [(s_ve, prev_x2)]
            if g + 1 < NGq:
                w_c1.append((s_te, stats_of[g + 1]))
            hv = []
            v = None
            for h in range(2):
                for c in range(4 * h, 4 * h + 4):
                    P.emit("gpsimd", (lambda g_, g2=g, c=c:
                                      g_.tensor_tensor(WST[:, c * GPP:(c + 1) * GPP], bxsl(g2, c),
                                                       ONES5[:, 0:GPP], OP.mult)),
                           waits=(w_c1 if (h == 0 and c == 0) else ()), inc=s_gp)
                v = P.emit("gpsimd", (lambda g_, g2=g, h=h:
                                      g_.tensor_tensor(H2B[:, h * 4 * GPP:(h + 1) * 4 * GPP],
                                                       bxhalf(g2, h), bxhalf(g2, h), OP.mult)),
                           inc=s_gp)
                hv.append(v)
            ln2cast_h[g] = hv
            ln2cast[g] = v

        def emit_ln2_mm(g):
            def stats2_mm(t_):
                last = None
                for c in range(NCH):
                    t_.matmul(ps_mod[:, 0:GPP], ONESB[:, :], WST[:, c * GPP:(c + 1) * GPP],
                              start=(c == 0), stop=(c == NCH - 1))
                    last = t_.matmul(ps_g[:, 0:GPP], ONESB[:, :], H2B[:, c * GPP:(c + 1) * GPP],
                                     start=(c == 0), stop=(c == NCH - 1))
                return last
            w_l2 = [(s_gp, ln2cast[g])]
            if g == 0:
                w_l2.append((s_ve, acc_hist[-1]))
            else:
                w_l2.append((s_ve, ln2ev[g - 1]))
                w_l2.append((s_sc, ln2sc[g - 1]))
            ln2mm[g] = P.emit("tensor", stats2_mm, waits=w_l2, inc=s_te)
            w_ls = [(s_te, ln2mm[g])]
            if P.val(sqr):
                w_ls.append((sqr, P.val(sqr)))
            ln2sc[g] = P.emit("scalar", lambda s: s.activation(MB2[:, :], ps_mod[:, 0:GPP], AF.Copy, scale=1.0 / D),
                              waits=w_ls, inc=s_sc)
            w1 = P.emit("vector", lambda v: v.tensor_tensor(VT[:, :], MB2[:, :], MB2[:, :], OP.mult),
                        waits=[(s_sc, ln2sc[g])], inc=s_ve)
            w2 = P.emit("vector", lambda v: v.tensor_scalar_sub(VTB[:, :], VT[:, :], 1e-5),
                        waits=[(s_ve, w1)], inc=s_ve)
            ln2ev[g] = P.emit("vector", lambda v: v.scalar_tensor_tensor(VAR2[:, :], ps_g[:, 0:GPP], 1.0 / D,
                                                                         VTB[:, :], OP.mult, OP.subtract),
                              waits=[(s_ve, w2)], inc=s_ve)

        emit_xload(0)
        emit_xload(1)
        wa_done = load_weight(0, wd["WaT"])
        wx_done = load_weight(1, wd["WxT"])
        tailst = None
        for g in range(NG):
            emit_prelude_casts(g)
            if g > 0:
                tailst = emit_tail_scanA(g - 1)
            emit_prelude_stats(g)
            emit_nlw2(g)
            emit_lnblock(g)
            if g > 0:
                emit_tail_scanB(g - 1, *tailst)
                emit_ln2_casts(g - 1)
            emit_main(g)
            if g > 0:
                emit_ln2_mm(g - 1)
            if 2 <= g + 1 < NG:
                emit_xload(g + 1)

        tailst = emit_tail_scanA(NG - 1)
        emit_tail_scanB(NG - 1, *tailst)
        emit_ln2_casts(NG - 1)

        def emit_rows_fin(gg):
            w_n1 = [(s_ve, ln2ev[gg])]
            if rows_done[gg - 1] is not None:
                w_n1.append((sqr, rows_done[gg - 1]))
            nf1 = P.emit("scalar", lambda s: s.activation(VTB[:, :], VAR2[:, :], AF.Ln),
                         waits=w_n1, inc=s_sc)
            sc_fin = P.emit("scalar", lambda s: s.activation(RSTD2[:, :], VTB[:, :], AF.Exp, scale=-0.5),
                            waits=[(s_sc, nf1)], inc=s_sc)
            P.emit(SCQ, (lambda s, gg=gg: s.dma_start(out=rowsd[gg, 0], in_=MB2[0:1, :])),
                   waits=[(s_sc, sc_fin)], inc=sqr, inc_by=16)
            rows_done[gg] = P.emit(SCQ, (lambda s, gg=gg: s.dma_start(out=rowsd[gg, 1], in_=RSTD2[0:1, :])),
                                   inc=sqr, inc_by=16)

        emit_rows_fin(NG - 2)
        emit_ln2_mm(NG - 1)
        emit_rows_fin(NG - 1)


        attn_done = [(s_ve, P.val(s_ve)), (s_gp, P.val(s_gp)),
                     (s_sc, P.val(s_sc)), (s_te, P.val(s_te))]
        attn_te_total = P.val(s_te)

        ffn_w = {}
        for i, n in enumerate(["wfT", "wiT", "wcT", "ufT", "uiT", "ucT"]):
            ffn_w[n] = load_weight(i, wd[n],
                                   waits_extra=(attn_done if i == 0 else ()))
        ffn_w_done = ffn_w["ucT"]

        w_c0 = list(attn_done) + [(sqs, P.val(sqs))]
        cz = P.emit("vector", lambda v: v.memset(Z3[:, 0:PP], 0.0), waits=w_c0, inc=s_ve)

        def c0_build(v):
            last = None
            for c in range(NCH):
                last = v.tensor_scalar_add(C[:, c * PP:(c + 1) * PP], Z3[:, 0:PP], vc("inc", c))
            return last
        c0b = P.emit("vector", c0_build, waits=[(s_ve, cz)], inc=s_ve)
        ve_c0 = P.emit("vector", lambda v: v.tensor_copy(CB[:, 0:TF], C[:, :]),
                       waits=[(s_ve, c0b)], inc=s_ve)

        sqs_total = P.val(sqs)
        sqr_total = P.val(sqr)
        x2_ld = [None, None]
        row_ld = [None, None]
        h2prep = [None, None]
        h2cast = [None, None]
        bcast_te = [None, None]
        gate_te_last = [None, None]
        ew_last = [None, None]
        prev_ew_all = ve_c0
        last_h2cast = None

        W3 = {0: "wfT", 1: "wiT", 2: "wcT"}
        U3 = {0: "ufT", 1: "uiT", 2: "ucT"}
        WIDX = {"wfT": 0, "wiT": 1, "wcT": 2, "ufT": 3, "uiT": 4, "ucT": 5}

        def prep_step(t):
            par = t % 2
            g = t // GT
            t2 = t % GT
            w_x = [(sqs, sqs_total)]
            if t < 2:
                w_x.append((s_te, attn_te_total))
            if h2prep[par] is not None:
                w_x.append((s_ve, h2prep[par]))
            x2_ld[par] = P.emit("sync", (lambda s, t=t, par=par:
                                         s.dma_start(out=XBF[:, par * 2048: par * 2048 + TF], in_=x2d[t])),
                                waits=w_x, inc=ldx2[par], inc_by=16)

            w_r = [(sqr, sqr_total)]
            if bcast_te[par] is not None:
                w_r.append((s_te, bcast_te[par]))
            P.emit("sync", (lambda s, g=g, t2=t2, par=par:
                            s.dma_start(out=ROWSTAGE[0:1, par * 2 * PP: par * 2 * PP + PP],
                                        in_=rowsd[g, 0, t2 * PP:(t2 + 1) * PP])),
                   waits=w_r, inc=ldr[par], inc_by=16)
            row_ld[par] = P.emit("sync", (lambda s, g=g, t2=t2, par=par:
                                          s.dma_start(out=ROWSTAGE[0:1, par * 2 * PP + PP: par * 2 * PP + 2 * PP],
                                                      in_=rowsd[g, 1, t2 * PP:(t2 + 1) * PP])),
                                 inc=ldr[par], inc_by=16)

            psbc = ps_c if par == 0 else ps_d
            w_bc = [(ldr[par], row_ld[par]), (s_sc, ffn_w_done)]
            if h2prep[par] is not None:
                w_bc.append((s_ve, h2prep[par]))
            if t == 0:
                w_bc.append((s_te, P.val(s_te)))
            bcast_te[par] = P.emit("tensor", (lambda t_, par=par, psbc=psbc:
                                              t_.matmul(psbc[:, 0:2 * PP], ONESF[:, :],
                                                        ROWSTAGE[:, par * 2 * PP:(par + 1) * 2 * PP],
                                                        start=True, stop=True)),
                                   waits=w_bc, inc=s_te)

            def h2p_a(v, par=par, psbc=psbc):
                xv = XBF[:, par * 2048: par * 2048 + TF].rearrange("p (c q) -> p c q", c=NCH)
                yv = Y[:, :].rearrange("p (c q) -> p c q", c=NCH)
                mb_bc = sap(psbc, 0, [[2 * PP, 128], [0, NCH], [1, PP]])
                return v.tensor_tensor(yv, xv, mb_bc, OP.subtract)

            def h2p_b(v, psbc=psbc):
                yv = Y[:, :].rearrange("p (c q) -> p c q", c=NCH)
                rs_bc = sap(psbc, PP, [[2 * PP, 128], [0, NCH], [1, PP]])
                return v.tensor_tensor(yv, yv, rs_bc, OP.mult)

            def h2p_c(v, t=t):
                last = None
                for c in range(NCH):
                    last = v.scalar_tensor_tensor(Y[:, c * PP:(c + 1) * PP], Y[:, c * PP:(c + 1) * PP],
                                                  modcol(4, c, t), modcol_bc(3, c, t, PP),
                                                  OP.mult, OP.add)
                return last
            w_h2 = [(s_te, bcast_te[par]), (ldx2[par], x2_ld[par])]
            if last_h2cast is not None:
                w_h2.append((s_sc, last_h2cast))
            if t == 0:
                w_h2.append((s_ve, ve_c0))
            hpa = P.emit("vector", h2p_a, waits=w_h2, inc=s_ve)
            hpb = P.emit("vector", h2p_b, waits=[(s_ve, hpa)], inc=s_ve)
            h2prep[par] = P.emit("vector", h2p_c, waits=[(s_ve, hpb)], inc=s_ve)

        def cast_step(t):
            nonlocal last_h2cast
            par = t % 2
            w_hc = [(s_ve, h2prep[par])]
            if gate_te_last[par] is not None:
                w_hc.append((s_te, gate_te_last[par]))
            h2cast[par] = P.emit("scalar", (lambda s, par=par:
                                            s.copy(H2B[:, par * 2048: par * 2048 + TF], Y[:, :])),
                                 waits=w_hc, inc=s_sc)
            last_h2cast = h2cast[par]

        prep_step(0)
        cast_step(0)

        for t in range(T):
            par = t % 2
            if t + 1 < T:
                prep_step(t + 1)

            ew_mc = {}
            sc_ev_mc = {}
            if t == 0:
                prev_psg_ev = {0: None, 1: None}
            for mc in range(NCH):
                psg = ps_g if mc % 2 == 0 else ps_mod
                def gmm_w(t_, gate, mc=mc, par=par, psg=psg):
                    last = None
                    for k in range(NCH):
                        mov = H2B[:, par * 2048 + k * PP: par * 2048 + (k + 1) * PP]
                        last = t_.matmul(psg[:, gate * PP:(gate + 1) * PP],
                                         wslot[WIDX[W3[gate]]][:, k * 1024 + mc * 128: k * 1024 + (mc + 1) * 128],
                                         mov, start=(k == 0), stop=False)
                    return last

                def gmm_u(t_, gate, mc=mc, par=par, psg=psg):
                    last = None
                    for k in range(NCH):
                        mov = CB[:, par * 2048 + k * PP: par * 2048 + (k + 1) * PP]
                        last = t_.matmul(psg[:, gate * PP:(gate + 1) * PP],
                                         wslot[WIDX[U3[gate]]][:, k * 1024 + mc * 128: k * 1024 + (mc + 1) * 128],
                                         mov, start=False, stop=(k == NCH - 1))
                    return last
                w_g = []
                if mc == 0:
                    w_g = [(s_sc, h2cast[par]), (s_sc, ffn_w_done)]
                    if t == 0:
                        w_g.append((s_ve, ln2ev[NG - 1]))
                        w_g.append((s_sc, ln2sc[NG - 1]))
                    if ew_last[par] is not None:
                        w_g.append((s_ve, ew_last[par]))
                if mc >= 2:
                    w_g.append((s_sc, sc_ev_mc[mc - 2]))
                    w_g.append((s_ve, ew_mc.get(mc - 2)))
                else:
                    w_g.append((s_sc, prev_psg_ev[mc % 2]))
                import functools as _ft
                P.emit("tensor", _ft.partial(gmm_w, gate=0), waits=w_g, inc=s_te)
                w_gu = []
                if mc == 0:
                    w_gu = [(s_ve, prev_ew_all)]
                gt = P.emit("tensor", _ft.partial(gmm_u, gate=0), waits=w_gu, inc=s_te)
                for gate in range(1, 3):
                    P.emit("tensor", _ft.partial(gmm_w, gate=gate), inc=s_te)
                    gt = P.emit("tensor", _ft.partial(gmm_u, gate=gate), inc=s_te)
                if mc == NCH - 1:
                    gate_te_last[par] = gt

                mp = mc % 2

                def gevac(s, mc=mc, mp=mp, psg=psg):
                    s.activation(THF[:, mp * PP:(mp + 1) * PP], psg[:, 0:PP], AF.Tanh,
                                 scale=0.5, bias=dc("bfh", mc))
                    s.activation(THI[:, mp * PP:(mp + 1) * PP], psg[:, PP:2 * PP], AF.Tanh,
                                 scale=0.5, bias=dc("bih", mc))
                    return s.activation(THG[:, mp * PP:(mp + 1) * PP], psg[:, 2 * PP:3 * PP], AF.Tanh,
                                        bias=vc("bc", mc))
                w_e = [(s_te, gt)]
                if mc >= 2:
                    w_e.append((s_ve, ew_mc[mc - 2]))
                sc_ev = P.emit("scalar", gevac, waits=w_e, inc=s_sc)
                sc_ev_mc[mc] = sc_ev
                if mc >= NCH - 2:
                    prev_psg_ev[mc % 2] = sc_ev

                def gew_a(v, mc=mc, mp=mp):
                    v.scalar_tensor_tensor(U1s[:, mp * PP:(mp + 1) * PP], THF[:, mp * PP:(mp + 1) * PP],
                                           1.0, C[:, mc * PP:(mc + 1) * PP], OP.add, OP.mult)
                    return v.scalar_tensor_tensor(U2s[:, mp * PP:(mp + 1) * PP], THI[:, mp * PP:(mp + 1) * PP],
                                                  1.0, THG[:, mp * PP:(mp + 1) * PP], OP.add, OP.mult)

                def gew_b(v, mc=mc, mp=mp):
                    return v.tensor_tensor(U1s[:, mp * PP:(mp + 1) * PP], U1s[:, mp * PP:(mp + 1) * PP],
                                           U2s[:, mp * PP:(mp + 1) * PP], OP.add)

                def gew_c(v, mc=mc, mp=mp, t=t):
                    v.tensor_scalar_mul(C[:, mc * PP:(mc + 1) * PP], U1s[:, mp * PP:(mp + 1) * PP], 0.5)
                    return v.tensor_scalar_mul(CB[:, (1 - t % 2) * 2048 + mc * PP: (1 - t % 2) * 2048 + (mc + 1) * PP],
                                               U1s[:, mp * PP:(mp + 1) * PP], 0.5)
                ga_ = P.emit("vector", gew_a, waits=[(s_sc, sc_ev)], inc=s_ve)
                gb_ = P.emit("vector", gew_b, waits=[(s_ve, ga_)], inc=s_ve)
                ew_mc[mc] = P.emit("vector", gew_c, waits=[(s_ve, gb_)], inc=s_ve)
            ew_last[par] = ew_mc[NCH - 1]
            prev_ew_all = ew_mc[NCH - 1]
            if t + 1 < T:
                cast_step(t + 1)

        uo_done = load_weight(1, wd["uoT"], waits_extra=[(s_te, gate_te_last[(T - 1) % 2])])
        wo_done = load_weight(0, wd["woT"], waits_extra=[(s_te, gate_te_last[(T - 1) % 2])])
        cfin_par = T % 2

        u_ev = {}
        for oc in range(NCH):
            pu = ps_c if oc % 2 == 0 else ps_d

            def umm(t_, oc=oc, pu=pu):
                last = None
                for k in range(NCH):
                    last = t_.matmul(pu[:, 0:PP],
                                     wslot[1][:, k * 1024 + oc * 128: k * 1024 + (oc + 1) * 128],
                                     CB[:, cfin_par * 2048 + k * PP: cfin_par * 2048 + (k + 1) * PP],
                                     start=(k == 0), stop=(k == NCH - 1))
                return last
            w_u = [(s_ve, prev_ew_all), (s_sc, uo_done)]
            if oc == 0:
                w_u.append((s_te, bcast_te[0]))
                w_u.append((s_te, bcast_te[1]))
            if oc >= 2:
                w_u.append((s_sc, u_ev[oc - 2]))
            ut = P.emit("tensor", umm, waits=w_u, inc=s_te)
            u_ev[oc] = P.emit("scalar", (lambda s, oc=oc, pu=pu:
                                         s.activation(C[:, oc * PP:(oc + 1) * PP], pu[:, 0:PP],
                                                      AF.Identity, bias=vc("bo", oc))),
                              waits=[(s_te, ut), (s_ve, prev_ew_all)], inc=s_sc)
        u_done_all = u_ev[NCH - 1]

        out_dma_cnt = 0
        prev_outdma = {}
        prev_h2o = [None, None]
        prev_h2co = [None, None]
        prev_bc_o = [None, None]
        prev_mm_o = {}
        prev_stt_o = {}
        for gp_ in range(NG):
            t0o = GT * gp_
            for t2 in range(GT):
                t = t0o + t2
                par = t2

                w_x = []
                if gp_ == 0:
                    w_x.append((s_sc, h2cast[par]))
                    w_x.append((s_ve, h2prep[par]))
                else:
                    w_x.append((s_ve, prev_h2o[par]))
                    w_x.append((s_ve, prev_stt_o[(gp_ - 1, NCH - 1, par)]))
                x2o = P.emit("sync", (lambda s, t=t, par=par:
                                      s.dma_start(out=XBF[:, par * 2048: par * 2048 + TF], in_=x2d[t])),
                             waits=w_x, inc=ldx2[par], inc_by=16)

                w_r = []
                if prev_bc_o[par] is not None:
                    w_r.append((s_te, prev_bc_o[par]))
                elif gp_ == 0:
                    w_r.append((s_te, bcast_te[par]))

                P.emit("sync", (lambda s, g=gp_, t2=t2, par=par:
                                s.dma_start(out=ROWSTAGE[0:1, par * 2 * PP: par * 2 * PP + PP],
                                            in_=rowsd[g, 0, t2 * PP:(t2 + 1) * PP])),
                       waits=w_r, inc=ldr[par], inc_by=16)
                rowo = P.emit("sync", (lambda s, g=gp_, t2=t2, par=par:
                                       s.dma_start(out=ROWSTAGE[0:1, par * 2 * PP + PP: par * 2 * PP + 2 * PP],
                                                   in_=rowsd[g, 1, t2 * PP:(t2 + 1) * PP])),
                              inc=ldr[par], inc_by=16)

                psbc = ps_c if par == 0 else ps_d
                w_bc = [(ldr[par], rowo), (s_sc, u_done_all)]
                if prev_h2o[par] is not None:
                    w_bc.append((s_ve, prev_h2o[par]))
                bco = P.emit("tensor", (lambda t_, par=par, psbc=psbc:
                                        t_.matmul(psbc[:, 0:2 * PP], ONESF[:, :],
                                                  ROWSTAGE[:, par * 2 * PP:(par + 1) * 2 * PP],
                                                  start=True, stop=True)),
                             waits=w_bc, inc=s_te)
                prev_bc_o[par] = bco

                def h2po_a(v, par=par, psbc=psbc):
                    xv = XBF[:, par * 2048: par * 2048 + TF].rearrange("p (c q) -> p c q", c=NCH)
                    yv = Y[:, :].rearrange("p (c q) -> p c q", c=NCH)
                    mb_bc = sap(psbc, 0, [[2 * PP, 128], [0, NCH], [1, PP]])
                    return v.tensor_tensor(yv, xv, mb_bc, OP.subtract)

                def h2po_b(v, psbc=psbc):
                    yv = Y[:, :].rearrange("p (c q) -> p c q", c=NCH)
                    rs_bc = sap(psbc, PP, [[2 * PP, 128], [0, NCH], [1, PP]])
                    return v.tensor_tensor(yv, yv, rs_bc, OP.mult)

                def h2po_c(v, t=t):
                    last = None
                    for c in range(NCH):
                        last = v.scalar_tensor_tensor(Y[:, c * PP:(c + 1) * PP], Y[:, c * PP:(c + 1) * PP],
                                                      modcol(4, c, t), modcol_bc(3, c, t, PP),
                                                      OP.mult, OP.add)
                    return last
                w_h2 = [(s_te, bco), (ldx2[par], x2o)]
                w_h2.append((s_sc, last_h2cast))
                if gp_ == 0 and t2 == 0:
                    w_h2.append((s_ve, prev_ew_all))
                hoa = P.emit("vector", h2po_a, waits=w_h2, inc=s_ve)
                hob = P.emit("vector", h2po_b, waits=[(s_ve, hoa)], inc=s_ve)
                h2o = P.emit("vector", h2po_c, waits=[(s_ve, hob)], inc=s_ve)
                prev_h2o[par] = h2o

                w_hc = [(s_ve, h2o)]
                if prev_mm_o.get(par) is not None:
                    w_hc.append((s_te, prev_mm_o[par]))
                h2co = P.emit("scalar", (lambda s, par=par:
                                         s.copy(H2B[:, par * 2048: par * 2048 + TF], Y[:, :])),
                              waits=w_hc, inc=s_sc)
                prev_h2co[par] = h2co
                last_h2cast = h2co

            def psout(t2, op_):
                if t2 == 0:
                    return (ps_a if op_ == 0 else ps_b)[:, 0:PP]
                return (ps_g if op_ == 0 else ps_mod)[:, 0:PP]
            mm_oc = {}
            for oc in range(NCH):
                def omm(t_, oc=oc):
                    last = None
                    for k in range(NCH):
                        for t2 in range(GT):
                            last = t_.matmul(psout(t2, oc % 2),
                                             wslot[0][:, k * 1024 + oc * 128: k * 1024 + (oc + 1) * 128],
                                             H2B[:, t2 * 2048 + k * PP: t2 * 2048 + (k + 1) * PP],
                                             start=(k == 0), stop=(k == NCH - 1))
                    return last
                w_m = [(s_sc, prev_h2co[1]), (s_sc, wo_done)]
                if gp_ == 0 and oc == 0:
                    w_m.append((s_ve, prev_ew_all))
                    w_m.append((s_sc, u_done_all))
                if oc >= 2:
                    w_m.append((s_ve, prev_stt_o[(gp_, oc - 2, 1)]))
                mm_oc[oc] = P.emit("tensor", omm, waits=w_m, inc=s_te)

                op_ = oc % 2
                for t2 in range(GT):
                    t = t0o + t2

                    def oev(v, oc=oc, op_=op_, t2=t2):
                        return v.tensor_tensor(Z3[:, op_ * PP:(op_ + 1) * PP],
                                               psout(t2, op_),
                                               C[:, oc * PP:(oc + 1) * PP], OP.add)
                    w_e = [(s_te, mm_oc[oc]), (s_sc, u_done_all)]
                    if (gp_, oc - 2, t2) in prev_stt_o or t2 > 0 or oc > 0 or gp_ > 0:
                        w_e.append((s_sc, P.val(s_sc)))
                    vz = P.emit("vector", oev, waits=w_e, inc=s_ve)
                    sz = P.emit("scalar", (lambda s, op_=op_: s.activation(Z3B[:, op_ * PP:(op_ + 1) * PP],
                                                                           Z3[:, op_ * PP:(op_ + 1) * PP], AF.Tanh)),
                                waits=[(s_ve, vz)], inc=s_sc)

                    def ostt(v, oc=oc, op_=op_, t=t, par=t2):
                        return v.scalar_tensor_tensor(U1s[:, op_ * PP:(op_ + 1) * PP],
                                                      Z3B[:, op_ * PP:(op_ + 1) * PP],
                                                      modcol(5, oc, t),
                                                      XBF[:, par * 2048 + oc * PP: par * 2048 + (oc + 1) * PP],
                                                      OP.mult, OP.add)
                    w_s = [(s_sc, sz)]
                    if prev_outdma.get(op_) is not None:
                        w_s.append((sqo[op_], prev_outdma[op_]))
                    so = P.emit("vector", ostt, waits=w_s, inc=s_ve)
                    prev_stt_o[(gp_, oc, t2)] = so

                    od = P.emit(SCQ, (lambda s, t=t, oc=oc, op_=op_:
                                           s.dma_start(out=outp[t][:, oc * PP:(oc + 1) * PP],
                                                       in_=U1s[:, op_ * PP:(op_ + 1) * PP])),
                                waits=[(s_ve, so)], inc=sqo[op_], inc_by=16)
                    prev_outdma[op_] = od
                    out_dma_cnt += 1

        if dbg:
            def dvd(s):
                s.dma_start(out=dbg_dv[:, 0:64], in_=VEC[:, :]).then_inc(ldm, 16)
                return s.dma_start(out=dbg_dv[:, 64:112], in_=DVEC[:, 0:48])
            P.cnt[id(ldm)] = P.val(ldm) + 16
            P.emit("sync", dvd, waits=[(sqs, P.val(sqs))], inc=ldm, inc_by=16)

        if dbg:
            first_w = [(sqs, P.val(sqs))]
            for t in range(T):
                P.emit("sync", (lambda s, t=t: s.dma_start(out=dbg_x2[t], in_=x2d[t])),
                       waits=(first_w if t == 0 else ()), inc=ldm, inc_by=16)
                P.emit("sync", (lambda s, t=t: s.dma_start(out=dbg_ci[t], in_=cind[t])),
                       inc=ldm, inc_by=16)
            P.emit("sync", lambda s: None, waits=[(ldm, P.val(ldm))])

        P.emit("sync", lambda s: None,
               waits=[(sqo[0], P.val(sqo[0])), (sqo[1], P.val(sqo[1])),
                      (sqs, P.val(sqs)), (sqr, P.val(sqr)), (ccs, P.val(ccs)),
                      (gqa[0], P.val(gqa[0])), (gqa[1], P.val(gqa[1]))])

        block = e(nc.Block())

        @block.sync
        def _(s):
            P.replay("sync", s)

        @block.tensor
        def _(t_):
            P.replay("tensor", t_)

        @block.vector
        def _(v):
            P.replay("vector", v)

        @block.scalar
        def _(s):
            P.replay("scalar", s)

        @block.gpsimd
        def _(g_):
            P.replay("gpsimd", g_)

    return nc



def _pack_vec(v):
    return np.ascontiguousarray(v.reshape(8, 128).T)


def prep_inputs(inputs, T, PP):
    NG = T // GT
    GF = NCH * GT * PP
    x = np.asarray(inputs["L0_attn_x"], dtype=np.float32)
    Wa = np.asarray(inputs["L0_attn_Wa"], np.float32)
    Wx = np.asarray(inputs["L0_attn_Wx"], np.float32)
    weights = {
        "WaT": np.ascontiguousarray(Wa.T), "WxT": np.ascontiguousarray(Wx.T),
        "wfT": np.ascontiguousarray(np.asarray(inputs["L0_ffn_wf"], np.float32).T),
        "wiT": np.ascontiguousarray(np.asarray(inputs["L0_ffn_wi"], np.float32).T),
        "wcT": np.ascontiguousarray(np.asarray(inputs["L0_ffn_wc"], np.float32).T),
        "ufT": np.ascontiguousarray(np.asarray(inputs["L0_ffn_uf"], np.float32).T),
        "uiT": np.ascontiguousarray(np.asarray(inputs["L0_ffn_ui"], np.float32).T),
        "ucT": np.ascontiguousarray(np.asarray(inputs["L0_ffn_uc"], np.float32).T),
        "woT": np.ascontiguousarray(np.asarray(inputs["L0_ffn_wo"], np.float32).T),
        "uoT": np.ascontiguousarray(np.asarray(inputs["L0_ffn_uo"], np.float32).T),
    }
    wmod_full = np.asarray(inputs["L0_wmod"], np.float32)
    cc = np.asarray(inputs["L0_c"], np.float32)
    scT = np.ascontiguousarray(cc.reshape(T, 8, 128).transpose(2, 1, 0).reshape(128, 8 * T))
    vecs = np.zeros((128, 64), np.float32)
    order = ["L0_attn_ba", "L0_attn_bx", "L0_ffn_bf", "L0_ffn_bi", "L0_ffn_bc",
             "L0_ffn_bo", "L0_ffn_in_c", "L0_attn_Lam"]
    for i, n in enumerate(order):
        vecs[:, 8 * i:8 * (i + 1)] = _pack_vec(np.asarray(inputs[n], np.float32))

    in_maps = []
    for core in range(NCORES):
        s0 = core * PP
        xc = x[:, s0:s0 + PP, :]
        xr = xc.reshape(NG, GT, PP, NCH, 128)
        xhost = np.ascontiguousarray(xr.transpose(0, 4, 3, 1, 2).reshape(NG, 128, GF))
        masks = np.zeros((128, 128), np.float32)
        for c in range(8):
            for j in range(8):
                masks[:, c * 8 + j] = 1.0 if j < core else 0.0
                masks[:, 64 + c * 8 + j] = 0.0 if j < core else 1.0
        m = {"xh": xhost, "scT": scT, "vecs": vecs, "cmask": masks,
             "wmod": np.ascontiguousarray(wmod_full[:, core * 768:(core + 1) * 768])}
        m.update(weights)
        in_maps.append(m)
    return in_maps


def assemble(results, T, PP):
    S = NCORES * PP
    out = np.empty((T, S, D), np.float32)
    for core in range(NCORES):
        oc = results[core]["outp"]
        o = oc.reshape(T, 128, NCH, PP).transpose(0, 3, 2, 1).reshape(T, PP, D)
        out[:, core * PP:(core + 1) * PP, :] = o
    return out


def kernel(**inputs):
    T, PP = 16, 256
    nc = build_nc(T, PP)
    in_maps = prep_inputs(inputs, T, PP)
    res = run_bass_kernel_spmd(nc, in_maps, core_ids=list(range(NCORES)))
    return assemble(res.results, T, PP)


# version 15
# speedup vs baseline: 1.0665x; 1.0039x over previous
import os as _os
import numpy as np
import concourse.bass as bass
import concourse.mybir as mybir
from concourse.bass_utils import run_bass_kernel_spmd
from contextlib import ExitStack

F32 = mybir.dt.float32
BF16 = mybir.dt.bfloat16
AF = mybir.ActivationFunctionType
OP = mybir.AluOpType
AX = mybir.AxisListType.X

NCORES = 8
D = 1024
NCH = 8
GT = 2
LN2C = 0.6931471805599453

ENGS = ("sync", "tensor", "vector", "scalar", "gpsimd")


PARANOID = _os.environ.get("KPARANOID", "0") == "1"
SCQ = "sync" if _os.environ.get("KSYNCQ", "0") == "1" else "scalar"


class Prog:

    def __init__(self):
        self.ops = {e: [] for e in ENGS}
        self.cnt = {}
        self.esem = {}

    def set_esems(self, m):
        self.esem = m

    def val(self, sem):
        return self.cnt.get(id(sem), 0)

    def emit(self, eng, builder, waits=(), inc=None, inc_by=1):
        waits = list(waits)
        if PARANOID and self.esem:
            for k in ("te", "ve", "sc", "gp"):
                s = self.esem[k]
                waits.append((s, self.val(s)))
            if inc is None and eng in self.esem_by_eng:
                inc = self.esem_by_eng[eng]
                inc_by = 1
        after = None
        if inc is not None:
            after = self.val(inc) + inc_by
            self.cnt[id(inc)] = after
        self.ops[eng].append((waits, builder, inc, inc_by))
        return after

    def replay(self, eng_name, e):
        for waits, builder, inc, inc_by in self.ops[eng_name]:
            for sem, v in waits:
                if v is None or v == 0:
                    continue
                e.wait_ge(sem, v)
            ins = builder(e)
            if inc is not None:
                if ins is None:
                    if PARANOID:
                        continue
                    assert False, "builder for inc op returned no instruction"
                ins.then_inc(inc, inc_by)


def sap(t, off, dims):
    return bass.AP(t, off, dims)


def build_nc(T, PP, dbg=False):
    NG = T // GT
    GPP = GT * PP
    GF = NCH * GPP
    TF = NCH * PP
    JC = 6 * D // 128
    MODW = JC * T

    nc = bass.Bass("TRN2", target_bir_lowering=False, detect_race_conditions=False)

    xh = nc.dram_tensor("xh", [NG, 128, GF], F32, kind="ExternalInput")
    wnames = ["WaT", "WxT", "wfT", "wiT", "wcT", "ufT", "uiT", "ucT", "woT", "uoT"]
    wd = {n: nc.dram_tensor(n, [D, D], F32, kind="ExternalInput") for n in wnames}
    wmodd = nc.dram_tensor("wmod", [D, 6 * D // NCORES], F32, kind="ExternalInput")
    sctd = nc.dram_tensor("scT", [128, NCH * T], F32, kind="ExternalInput")
    vecsd = nc.dram_tensor("vecs", [128, 64], F32, kind="ExternalInput")
    masksd = nc.dram_tensor("cmask", [128, 128], F32, kind="ExternalInput")
    outp = nc.dram_tensor("outp", [T, 128, TF], F32, kind="ExternalOutput")
    x2d = nc.dram_tensor("x2d", [T, 128, TF], F32)
    rowsd = nc.dram_tensor("rowsd", [NG, 2, GPP], F32)
    cind = nc.dram_tensor("cin", [T, 128, 16], F32)
    coutd = nc.dram_tensor("cout", [T, 8, 128, 16], F32)
    modind = nc.dram_tensor("modin", [128, 6 * NCH * T // NCORES], F32)
    modoutd = nc.dram_tensor("modout", [NCORES, 128, 6 * NCH * T // NCORES], F32)
    dbg_x2 = nc.dram_tensor("dbg_x2", [T, 128, NCH * PP], F32, kind="ExternalOutput") if dbg else None
    dbg_a = nc.dram_tensor("dbg_a", [T // GT, 128, NCH * GT * PP], F32, kind="ExternalOutput") if dbg else None
    dbg_b = nc.dram_tensor("dbg_b", [T // GT, 128, NCH * GT * PP], F32, kind="ExternalOutput") if dbg else None
    dbg_ci = nc.dram_tensor("dbg_ci", [T, 128, 16], F32, kind="ExternalOutput") if dbg else None
    dbg_h1 = nc.dram_tensor("dbg_h1", [T // GT, 128, NCH * GT * PP], F32, kind="ExternalOutput") if dbg else None
    dbg_dv = nc.dram_tensor("dbg_dv", [128, 128], F32, kind="ExternalOutput") if dbg else None
    dbg_cmb = nc.dram_tensor("dbg_cmb", [128, 512], F32, kind="ExternalOutput") if dbg else None
    dbg_mod = nc.dram_tensor("dbg_mod", [128, 6 * NCH * T], F32, kind="ExternalOutput") if dbg else None

    with ExitStack() as stack:
        e = stack.enter_context

        wslot = [e(nc.sbuf_tensor(f"wslot{i}", [128, 8192], BF16)) for i in range(6)]
        Bx = wslot[2].ap().bitcast(F32)
        Bh1 = wslot[3].ap().bitcast(F32)
        Ba = wslot[4].ap().bitcast(F32)
        Baa = wslot[5].ap().bitcast(F32)
        Bh1b = e(nc.sbuf_tensor("Bh1b", [128, 4096], BF16))
        XBIG = e(nc.sbuf_tensor("XBIG", [128, 8192], BF16))
        XBF = XBIG.ap().bitcast(F32)
        SCRA = e(nc.sbuf_tensor("SCRA", [128, max(2 * GPP, 1024)], F32))
        SCRX = e(nc.sbuf_tensor("SCRX", [128, max(2 * GPP, 1024)], F32))
        Y = e(nc.sbuf_tensor("Y", [128, TF], F32))
        MB = e(nc.sbuf_tensor("MB", [128, GPP], F32))
        RSTD = e(nc.sbuf_tensor("RSTD", [128, GPP], F32))
        MB2 = e(nc.sbuf_tensor("MB2", [128, GPP], F32))
        RSTD2 = e(nc.sbuf_tensor("RSTD2", [128, GPP], F32))
        VAR = e(nc.sbuf_tensor("VAR", [128, GPP], F32))
        VAR2 = e(nc.sbuf_tensor("VAR2", [128, GPP], F32))
        VT = e(nc.sbuf_tensor("VT", [128, GPP], F32))
        VTB = e(nc.sbuf_tensor("VTB", [128, GPP], F32))
        CARRY = e(nc.sbuf_tensor("CARRY", [128, 32], F32))
        AGB = e(nc.sbuf_tensor("AGB", [128, 256], F32))
        HIN = e(nc.sbuf_tensor("HIN", [128, 16], F32))
        CMB1 = e(nc.sbuf_tensor("CMB1", [128, 64], F32))
        CMB2 = e(nc.sbuf_tensor("CMB2", [128, 64], F32))
        CMB3 = e(nc.sbuf_tensor("CMB3", [128, 64], F32))
        ATOT = e(nc.sbuf_tensor("ATOT", [128, 16], F32))
        THS = e(nc.sbuf_tensor("THS", [128, 16], F32))
        MOD = e(nc.sbuf_tensor("MOD", [128, MODW], F32))
        SCT = e(nc.sbuf_tensor("SCT", [128, NCH * T], F32))
        SCT2 = e(nc.sbuf_tensor("SCT2", [128, NCH * T], F32))
        VEC = e(nc.sbuf_tensor("VEC", [128, 64], F32))
        DVEC = e(nc.sbuf_tensor("DVEC", [128, 64], F32))
        MASK = e(nc.sbuf_tensor("MASK", [128, 128], F32))
        ONESB = e(nc.sbuf_tensor("ONESB", [128, 128], BF16))
        ONESF = e(nc.sbuf_tensor("ONESF", [128, 128], F32))
        ONES5 = e(nc.sbuf_tensor("ONES5", [128, 512], F32))
        WST = e(nc.sbuf_tensor("WST", [128, 4096], BF16))
        WSTf = WST.ap().bitcast(F32)
        ROWSTAGE = e(nc.sbuf_tensor("ROWSTAGE", [128, 4 * PP], F32))
        C = e(nc.sbuf_tensor("C", [128, TF], F32))
        CB = e(nc.sbuf_tensor("CB", [128, 2 * 2048], BF16))
        H2B = e(nc.sbuf_tensor("H2B", [128, 2 * 2048], BF16))
        THF = e(nc.sbuf_tensor("THF", [128, 2 * PP], F32))
        THI = e(nc.sbuf_tensor("THI", [128, 2 * PP], F32))
        THG = e(nc.sbuf_tensor("THG", [128, 2 * PP], F32))
        U1s = e(nc.sbuf_tensor("U1s", [128, 2 * PP], F32))
        U2s = e(nc.sbuf_tensor("U2s", [128, 2 * PP], F32))
        Z3 = THF.ap()
        Z3B = U2s.ap()
        CONSTS = e(nc.sbuf_tensor("CONSTS", [128, 1], F32))

        ps_a = e(nc.psum_tensor("ps_a", [128, GPP], F32))
        ps_b = e(nc.psum_tensor("ps_b", [128, GPP], F32))
        ps_c = e(nc.psum_tensor("ps_c", [128, GPP], F32))
        ps_d = e(nc.psum_tensor("ps_d", [128, GPP], F32))
        ps_mod = e(nc.psum_tensor("ps_mod", [128, max(MODW, 3 * PP)], F32))
        ps_g = e(nc.psum_tensor("ps_g", [128, 3 * PP], F32))

        ldm = e(nc.semaphore("ldm"))
        ldw0 = e(nc.semaphore("ldw0"))
        ldw1 = e(nc.semaphore("ldw1"))
        ldv0 = e(nc.semaphore("ldv0"))
        ldv1 = e(nc.semaphore("ldv1"))
        ldx = e(nc.semaphore("ldx"))
        ldx2 = [e(nc.semaphore("ldx2_0")), e(nc.semaphore("ldx2_1"))]
        ldr = [e(nc.semaphore("ldr0")), e(nc.semaphore("ldr1"))]
        sqs = e(nc.semaphore("sqs"))
        sqr = e(nc.semaphore("sqr"))
        sqo = [e(nc.semaphore("sqo0")), e(nc.semaphore("sqo1"))]
        gqc = [e(nc.semaphore("gqc0")), e(nc.semaphore("gqc1"))]
        gqa = [e(nc.semaphore("gqa0")), e(nc.semaphore("gqa1"))]
        ccs = e(nc.semaphore("ccs"))
        s_te = e(nc.semaphore("s_te"))
        s_ve = e(nc.semaphore("s_ve"))
        s_sc = e(nc.semaphore("s_sc"))
        s_gp = e(nc.semaphore("s_gp"))

        MODL_BUF = VT
        P = Prog()
        P.esem_by_eng = {"tensor": s_te, "vector": s_ve, "scalar": s_sc, "gpsimd": s_gp}
        P.set_esems({"te": s_te, "ve": s_ve, "sc": s_sc, "gp": s_gp,
                     "ldm": ldm, "ldw0": ldw0, "ldw1": ldw1, "ldv0": ldv0, "ldv1": ldv1,
                     "ldx": ldx, "ldx20": ldx2[0], "ldx21": ldx2[1],
                     "ldr0": ldr[0], "ldr1": ldr[1], "sqs": sqs, "sqr": sqr,
                     "sqo0": sqo[0], "sqo1": sqo[1], "gqc0": gqc[0], "gqc1": gqc[1],
                     "gqa0": gqa[0], "gqa1": gqa[1], "ccs": ccs})

        VORD = ["ba", "bx", "bf", "bi", "bc", "bo", "inc", "lam"]
        vcol = {n: 8 * i for i, n in enumerate(VORD)}
        dcol = {"bah": 0, "bxh": 8, "bfh": 16, "bih": 24, "sh": 32, "s256": 40}

        def vc(name, c):
            return VEC[:, vcol[name] + c: vcol[name] + c + 1]

        def dc(name, c):
            return DVEC[:, dcol[name] + c: dcol[name] + c + 1]

        def modcol(g, c, t):
            return MOD[:, g * NCH * T + c * T + t: g * NCH * T + c * T + t + 1]

        def modcol_bc(g, c, t, n):
            off = g * NCH * T + c * T + t
            return sap(MOD, off, [[MODW, 128], [0, n]])

        def gsl(ap, c, t2=None):
            if t2 is None:
                return ap[:, c * GPP:(c + 1) * GPP]
            return ap[:, c * GPP + t2 * PP: c * GPP + (t2 + 1) * PP]

        def b(f):
            return f

        P.emit("sync", lambda s: s.dma_start(out=VEC[:, :], in_=vecsd[:, :]), inc=ldm, inc_by=16)
        P.emit("sync", lambda s: s.dma_start(out=SCT[:, :], in_=sctd[:, :]), inc=ldm, inc_by=16)
        ld_msk = P.emit("sync", lambda s: s.dma_start(out=MASK[:, :], in_=masksd[:, :]), inc=ldm, inc_by=16)
        if dbg:
            P.emit("sync", lambda s: s.dma_start(out=dbg_dv[:, 112:128], in_=masksd[:, 0:16]), inc=ldm, inc_by=16)

        gp_ones = P.emit("gpsimd", lambda g: (g.memset(ONESB[:, :], 1.0),
                                              g.memset(ONESF[:, :], 1.0),
                                              g.memset(ONES5[:, :], 1.0),
                                              g.memset(CONSTS[:, :], 0.25),
                                              g.memset(ROWSTAGE[:, :], 0.0))[-1],
                         inc=s_gp)

        def deriv_a(s):
            s.mul(DVEC[:, 0:8], VEC[:, vcol["ba"]:vcol["ba"] + 8], 0.5)
            s.mul(DVEC[:, 8:16], VEC[:, vcol["bx"]:vcol["bx"] + 8], 0.5)
            s.mul(DVEC[:, 16:24], VEC[:, vcol["bf"]:vcol["bf"] + 8], 0.5)
            s.mul(DVEC[:, 24:32], VEC[:, vcol["bi"]:vcol["bi"] + 8], 0.5)
            return s.activation(VT[:, 0:8], VEC[:, vcol["lam"]:vcol["lam"] + 8], AF.Exp, scale=-1.0)
        d1v = P.emit("scalar", deriv_a, waits=[(ldm, ld_msk)], inc=s_sc)
        d2v = P.emit("scalar", lambda s: s.activation(VT[:, 8:16], VT[:, 0:8], AF.Ln, bias=1.0),
                     waits=[(s_sc, d1v)], inc=s_sc)
        d3v = P.emit("scalar", lambda s: s.mul(DVEC[:, 32:40], VT[:, 8:16], -4.0),
                     waits=[(s_sc, d2v)], inc=s_sc)
        d4v = P.emit("scalar", lambda s: s.mul(DVEC[:, 40:48], DVEC[:, 32:40], float(PP)),
                     waits=[(s_sc, d3v)], inc=s_sc)
        sc_deriv = P.emit("scalar", lambda s: s.activation(SCT2[:, :], SCT[:, :], AF.Silu),
                          waits=[(s_sc, d4v)], inc=s_sc)

        JCL = JC // NCORES
        MODL = JCL * T
        halfW = JCL * 128
        stage_slots = [Bx, Bh1]
        band_ld = []
        mm_hist = []
        acc_hist = []
        for k in range(NCH):
            s0 = stage_slots[k % 2]
            w = [(s_te, mm_hist[k - 2] if k >= 2 else None)]
            lw = ldw0 if k % 2 == 0 else ldw1
            a1 = P.emit("sync", (lambda s, k=k, s0=s0: s.dma_start(out=s0[:, 0:halfW], in_=wmodd[k * 128:(k + 1) * 128, :])),
                        waits=w, inc=lw, inc_by=16)

            def mmb(t_, k=k, s0=s0):
                last = None
                for jc in range(JCL):
                    last = t_.matmul(ps_mod[:, jc * T:(jc + 1) * T],
                                     s0[:, jc * 128:(jc + 1) * 128],
                                     SCT2[:, k * T:(k + 1) * T],
                                     start=True, stop=True)
                return last
            w_mm = [(lw, a1), (s_sc, sc_deriv)]
            if k > 0:
                w_mm.append((s_ve, acc_hist[k - 1]))
            mm_hist.append(P.emit("tensor", mmb, waits=w_mm, inc=s_te))

            def accb(v, k=k):
                sv = Y[:, 0:MODL]
                if k == 0:
                    return v.tensor_copy(sv, ps_mod[:, 0:MODL])
                src_, dst_ = (sv, MODL_BUF[:, 0:MODL]) if k % 2 == 1 else (MODL_BUF[:, 0:MODL], sv)
                return v.tensor_tensor(dst_, src_, ps_mod[:, 0:MODL], OP.add)
            acc_hist.append(P.emit("vector", accb, waits=[(s_te, mm_hist[k])], inc=s_ve))
        mm_after = mm_hist[-1]

        md1 = P.emit("gpsimd", lambda g_: g_.dma_start(out=modind[:, :], in_=MODL_BUF[:, 0:MODL]),
                     waits=[(s_ve, acc_hist[-1])], inc=gqc[0], inc_by=16)
        mag = P.emit("gpsimd", lambda g_: g_.collective_compute(
            "AllGather", OP.bypass, replica_groups=[list(range(NCORES))],
            ins=[modind.ap().opt()], outs=[modoutd.ap().opt()]),
            waits=[(gqc[0], md1)], inc=ccs)
        md2 = P.emit("gpsimd", lambda g_: g_.dma_start(
            out=MOD[:, :],
            in_=sap(modoutd, 0, [[MODL, 128], [128 * MODL, NCORES], [1, MODL]])),
            waits=[(ccs, mag)], inc=gqc[0], inc_by=16)
        acc_hist.append(P.emit("vector", lambda v: v.engine_nop(), waits=[(gqc[0], md2)], inc=s_ve))

        NT = NCH * T
        e1 = P.emit("vector", lambda v: (v.tensor_scalar_add(VT[:, 0:NT], MOD[:, 1 * NT:2 * NT], 1.0),
                                         v.tensor_scalar_add(VTB[:, 0:NT], MOD[:, 4 * NT:5 * NT], 1.0))[-1],
                    waits=[(s_ve, acc_hist[-1])], inc=s_ve)
        ve_add1 = P.emit("vector", lambda v: (v.tensor_copy(MOD[:, 1 * NT:2 * NT], VT[:, 0:NT]),
                                              v.tensor_copy(MOD[:, 4 * NT:5 * NT], VTB[:, 0:NT]))[-1],
                         waits=[(s_ve, e1)], inc=s_ve)
        ve_modcl = ve_add1
        for gsec in (2, 5):
            lo = gsec * NT
            sA = P.emit("scalar", (lambda s, lo=lo: (
                s.sign(VT[:, 0:NT], MOD[:, lo:lo + NT]),
                s.activation(VTB[:, 0:NT], MOD[:, lo:lo + NT], AF.Abs))[-1]),
                waits=[(s_ve, ve_modcl)], inc=s_sc)
            eM = P.emit("vector", lambda v: v.tensor_scalar_max(SCRA[:, 0:NT], VTB[:, 0:NT], 0.25),
                        waits=[(s_sc, sA), (s_ve, ve_modcl)], inc=s_ve)
            ve_modcl = P.emit("vector", (lambda v, lo=lo:
                              v.tensor_tensor(MOD[:, lo:lo + NT], VT[:, 0:NT], SCRA[:, 0:NT], OP.mult)),
                              waits=[(s_ve, eM)], inc=s_ve)
        if dbg:
            ve_modcl = P.emit("vector", lambda v: v.engine_nop(), waits=[(s_ve, ve_modcl)], inc=s_ve)
            dmod_e = P.emit("sync", lambda s: s.dma_start(out=dbg_mod[:, :], in_=MOD[:, :]),
                            waits=[(s_ve, ve_modcl)], inc=ldm, inc_by=16)
            ve_modcl = P.emit("vector", lambda v: v.engine_nop(), waits=[(ldm, dmod_e)], inc=s_ve)
        mod_ready_v = ve_modcl

        def load_weight(slot_idx, dram, waits_extra=(), stages=None, cast_extra=()):
            if stages is None:
                stages = (WSTf[:, 0:1024], WSTf[:, 1024:2048])
            done = None
            for k in range(NCH):
                half = load_weight.seq % 2
                load_weight.seq += 1
                w = [(s_sc, load_weight.cast_hist.get(load_weight.seq - 2))]
                if k < 2:
                    w.extend(waits_extra)
                lv = ldv0 if half == 0 else ldv1
                st = stages[half]
                a = P.emit("sync", (lambda s, k=k, st=st, dram=dram:
                                    s.dma_start(out=st, in_=dram[k * 128:(k + 1) * 128, :])),
                           waits=w, inc=lv, inc_by=16)
                wc = [(lv, a)]
                if k == 0:
                    wc.extend(cast_extra)
                done = P.emit("scalar", (lambda s, k=k, st=st, si=slot_idx:
                                         s.copy(wslot[si][:, k * 1024:(k + 1) * 1024], st)),
                              waits=wc, inc=s_sc)
                load_weight.cast_hist[load_weight.seq] = done
            return done
        load_weight.cast_hist = {}
        load_weight.seq = 0


        CBf = CB.ap().bitcast(F32)
        NGq = NG

        def bx_pieces(g):
            if g % 2 == 0:
                return [(Bx, 0, 8)]
            return [(C.ap(), 0, 4), (CBf, 4, 4)]

        def bxsl(g, c, t2=None):
            for ap, base, n in bx_pieces(g):
                if base <= c < base + n:
                    off = (c - base) * GPP
                    if t2 is None:
                        return ap[:, off: off + GPP]
                    return ap[:, off + t2 * PP: off + (t2 + 1) * PP]

        def bxhalf(g, h):
            for ap, base, n in bx_pieces(g):
                if base <= 4 * h < base + n:
                    off = (4 * h - base) * GPP
                    return ap[:, off: off + 4 * GPP]

        def hsl(c, t2=None):
            if t2 is None:
                return XBF[:, c * GPP:(c + 1) * GPP]
            return XBF[:, c * GPP + t2 * PP: c * GPP + (t2 + 1) * PP]

        def lanex(g):
            return ldx if g % 2 == 0 else ldx2[1]

        x_ld = [None] * NGq
        x_ld_half = [None] * NGq
        pcast = [None] * NGq
        ln2cast_h = [None] * NGq
        spill_done = [None] * NGq
        ln2mm = [None] * NGq
        ln2ev = [None] * NGq
        ln2sc = [None] * NGq
        ln2cast = [None] * NGq
        rows_done = [None] * NGq
        stats_of = [None] * NGq
        var_of = [None] * NGq
        rstd_of = [None] * NGq
        last_rescan = [None, None]
        prev_x2 = None
        prev_b = None
        b_hi_ve = None
        prev_u_last = None
        prev_carry_dma = [None, None]
        prev_agb_cmb = [None, None]
        prev_zx_te = None
        prev_zx_thx_last = None
        prev_extract_gp = None
        prev_dbgb = None
        prev_dbgh1 = None
        sc_sr2 = None
        sc_atot = None
        ve_h1 = None

        def emit_xload(g):
            halves = []
            for h in range(2):
                w = []
                if g == 0 and h == 0:
                    w.append((s_te, mm_after))
                if g >= 2:
                    w.append((s_gp, ln2cast_h[g - 2][h]))
                    if h == 0:
                        w.append((sqs, spill_done[g - 2]))
                v = P.emit("sync", (lambda s, g=g, h=h:
                                    s.dma_start(out=bxhalf(g, h),
                                                in_=xh[g][:, h * 4 * GPP:(h + 1) * 4 * GPP])),
                           waits=w, inc=lanex(g), inc_by=16)
                halves.append(v)
            x_ld_half[g] = halves
            x_ld[g] = halves[-1]

        def emit_prelude_casts(g):
            w_free = []
            if g == 1:
                w_free.append((s_te, stats_of[0]))
            elif g >= 2:
                w_free.append((s_te, ln2mm[g - 2]))
            v = None
            for h in range(2):
                P.emit("scalar", (lambda s, g2=g, h=h:
                                  s.copy(WST[:, h * 4 * GPP:(h + 1) * 4 * GPP], bxhalf(g2, h))),
                       waits=[(lanex(g), x_ld_half[g][h])] + (w_free if h == 0 else []), inc=s_sc)
                v = P.emit("scalar", (lambda s, g2=g, h=h:
                                      s.activation(H2B[:, h * 4 * GPP:(h + 1) * 4 * GPP],
                                                   bxhalf(g2, h), AF.Square)),
                           inc=s_sc)
            pcast[g] = v

        def emit_prelude_stats(g):
            def stats_mm(t_):
                last = None
                for c in range(NCH):
                    t_.matmul(ps_c[:, :], ONESB[:, :], WST[:, c * GPP:(c + 1) * GPP],
                              start=(c == 0), stop=(c == NCH - 1))
                    last = t_.matmul(ps_d[:, :], ONESB[:, :], H2B[:, c * GPP:(c + 1) * GPP],
                                     start=(c == 0), stop=(c == NCH - 1))
                return last
            w_st = [(s_sc, pcast[g]), (s_gp, gp_ones)]
            if g > 0:
                w_st.append((s_ve, var_of[g - 1]))
            stats_of[g] = P.emit("tensor", stats_mm, waits=w_st, inc=s_te)

            sc_mb = P.emit("scalar", lambda s: s.activation(MB[:, :], ps_c[:, :], AF.Copy, scale=1.0 / D),
                           waits=[(s_te, stats_of[g])], inc=s_sc)
            v1 = P.emit("vector", lambda v: v.tensor_tensor(VT[:, :], MB[:, :], MB[:, :], OP.mult),
                        waits=[(s_sc, sc_mb)], inc=s_ve)
            v2 = P.emit("vector", lambda v: v.tensor_scalar_sub(VTB[:, :], VT[:, :], 1e-5),
                        waits=[(s_ve, v1)], inc=s_ve)
            var_of[g] = P.emit("vector", lambda v: v.scalar_tensor_tensor(VAR[:, :], ps_d[:, :], 1.0 / D,
                                                                          VTB[:, :], OP.mult, OP.subtract),
                               waits=[(s_ve, v2)], inc=s_ve)

        def emit_nlw2(g):
            nv = P.emit("scalar", lambda s: s.activation(VT[:, :], VAR[:, :], AF.Ln),
                        waits=[(s_ve, var_of[g])], inc=s_sc)
            rstd_of[g] = P.emit("scalar", lambda s: s.activation(RSTD[:, :], VT[:, :], AF.Exp, scale=-0.5),
                                waits=[(s_sc, nv)], inc=s_sc)

        def emit_lnblock(g):
            nonlocal ve_h1
            lt1 = None
            for ap, base, n in bx_pieces(g):
                def ln_t1(v, ap=ap, base=base, n=n):
                    bx3 = ap[:, 0:n * GPP].rearrange("p (c q) -> p c q", c=n)
                    bh3 = XBF[:, base * GPP:(base + n) * GPP].rearrange("p (c q) -> p c q", c=n)
                    mb_bc = sap(MB, 0, [[GPP, 128], [0, n], [1, GPP]])
                    return v.tensor_tensor(bh3, bx3, mb_bc, OP.subtract)
                w_t1 = []
                if base == 0:
                    w_t1 = [(s_sc, rstd_of[g]), (lanex(g), x_ld[g])]
                    if dbg and prev_dbgh1 is not None:
                        w_t1.append((ldm, prev_dbgh1))
                lt1 = P.emit("vector", ln_t1, waits=w_t1, inc=s_ve)

            def ln_t2(v):
                bh3 = XBF[:, 0:GF].rearrange("p (c q) -> p c q", c=NCH)
                rs_bc = sap(RSTD, 0, [[GPP, 128], [0, NCH], [1, GPP]])
                return v.tensor_tensor(bh3, bh3, rs_bc, OP.mult)
            lt2 = P.emit("vector", ln_t2, waits=[(s_ve, lt1)], inc=s_ve)

            def h1_build(v, g=g):
                last = None
                for c in range(NCH):
                    for t2 in range(GT):
                        t = GT * g + t2
                        last = v.scalar_tensor_tensor(
                            hsl(c, t2), hsl(c, t2),
                            modcol(1, c, t), modcol_bc(0, c, t, PP),
                            OP.mult, OP.add)
                return last
            ve_h1 = P.emit("vector", h1_build, waits=[(s_ve, mod_ready_v), (s_ve, lt2)], inc=s_ve)

        def emit_tail_scanA(g):
            nonlocal prev_extract_gp
            hbuf = [Y, Baa]
            ag_done = {}
            agv_of = {}
            for t2 in range(GT):
                t = GT * g + t2
                par = t % 2
                hb = hbuf[t2]

                def scan1(v, t2=t2, hb=hb):
                    last = None
                    for c in range(NCH):
                        last = v.tensor_tensor_scan(hb[:, c * PP:(c + 1) * PP],
                                                    gsl(Ba, c, t2), gsl(Bh1, c, t2),
                                                    0.0, OP.mult, OP.add)
                    return last
                ve_s1 = P.emit("vector", scan1, waits=[(s_gp, prev_b), (s_ve, b_hi_ve)], inc=s_ve)

                def extract(g_, t2=t2, par=par, hb=hb):
                    for c in range(NCH):
                        g_.tensor_copy(CARRY[:, par * 16 + c: par * 16 + c + 1],
                                       hb[:, c * PP + PP - 1: c * PP + PP])
                    return g_.tensor_copy(sap(CARRY, par * 16 + 8, [[32, 128], [1, 8]]),
                                          sap(ATOT, t2, [[16, 128], [GT, 8]]))
                w_ex = [(s_ve, ve_s1), (s_sc, sc_atot)]
                if prev_carry_dma[par] is not None:
                    w_ex.append((gqc[par], prev_carry_dma[par]))
                gp_ex = P.emit("gpsimd", extract, waits=w_ex, inc=s_gp)
                if t2 == GT - 1:
                    prev_extract_gp = gp_ex

                d1 = P.emit("gpsimd", (lambda g_, t=t, par=par:
                                       g_.dma_start(out=cind[t], in_=CARRY[:, par * 16:(par + 1) * 16])),
                            waits=[(s_gp, gp_ex)], inc=gqc[par], inc_by=16)

                agv_of[t2] = P.emit("gpsimd", (lambda g_, t=t: g_.collective_compute(
                    "AllGather", OP.bypass,
                    replica_groups=[list(range(NCORES))],
                    ins=[cind[t].opt()],
                    outs=[coutd[t].opt()])),
                    waits=[(gqc[par], d1)], inc=ccs)
                prev_carry_dma[par] = d1

            for t2 in range(GT):
                t = GT * g + t2
                par = t % 2
                w_ab = [(ccs, agv_of[t2])]
                if prev_agb_cmb[par] is not None:
                    w_ab.append((s_ve, prev_agb_cmb[par]))
                d2 = P.emit("gpsimd", (lambda g_, t=t, par=par:
                                       g_.dma_start(out=AGB[:, par * 128:(par + 1) * 128],
                                                    in_=sap(coutd, t * 8 * 128 * 16, [[16, 128], [2048, 8], [1, 16]]))),
                            waits=w_ab, inc=gqa[par], inc_by=16)
                ag_done[t2] = d2
            return hbuf, ag_done

        def emit_tail_scanB(g, hbuf, ag_done):
            nonlocal prev_x2
            for t2 in range(GT):
                t = GT * g + t2
                par = t % 2
                hb = hbuf[t2]

                def cmb_a(v, par=par):
                    at_ap = sap(AGB, par * 128 + 8, [[256, 128], [1, 8], [16, 8]])
                    he_ap = sap(AGB, par * 128 + 0, [[256, 128], [1, 8], [16, 8]])
                    am = MASK[:, 0:64].rearrange("p (c j) -> p c j", c=8)
                    c1 = CMB1[:, :].rearrange("p (c j) -> p c j", c=8)
                    c2 = CMB2[:, :].rearrange("p (c j) -> p c j", c=8)
                    v.tensor_tensor(c2, he_ap, am, OP.mult)
                    return v.tensor_tensor(c1, at_ap, am, OP.mult)

                def cmb_b(v):
                    c1 = CMB1[:, :].rearrange("p (c j) -> p c j", c=8)
                    bm = MASK[:, 64:128].rearrange("p (c j) -> p c j", c=8)
                    return v.tensor_tensor(c1, c1, bm, OP.add)

                def cmb_c(v):
                    last = None
                    for c in range(NCH):
                        last = v.tensor_tensor_scan(CMB3[:, c * 8:(c + 1) * 8],
                                                    CMB1[:, c * 8:(c + 1) * 8],
                                                    CMB2[:, c * 8:(c + 1) * 8],
                                                    0.0, OP.mult, OP.add)
                    return last
                w_cmb = [(gqa[par], ag_done[t2])]
                if t < 2:
                    w_cmb.append((ldm, ld_msk))
                ca_ = P.emit("vector", cmb_a, waits=w_cmb, inc=s_ve)
                cb_ = P.emit("vector", cmb_b, waits=[(s_ve, ca_)], inc=s_ve)
                cc_ = P.emit("vector", cmb_c, waits=[(s_ve, cb_)], inc=s_ve)
                cmb = P.emit("vector", (lambda v, par=par:
                                        v.tensor_copy(HIN[:, par * 8:(par + 1) * 8],
                                                      sap(CMB3, 7, [[64, 128], [8, 8]]))),
                             waits=[(s_ve, cc_)], inc=s_ve)
                prev_agb_cmb[par] = cmb

                def rescan(v, t2=t2, par=par, hb=hb):
                    last = None
                    for c in range(NCH):
                        last = v.tensor_tensor_scan(hb[:, c * PP:(c + 1) * PP],
                                                    gsl(Ba, c, t2), gsl(Bh1, c, t2),
                                                    HIN[:, par * 8 + c: par * 8 + c + 1],
                                                    OP.mult, OP.add)
                    return last
                ve_rs = P.emit("vector", rescan, waits=[(s_ve, cmb)], inc=s_ve)
                last_rescan[t2] = ve_rs

                def x2b(v, t2=t2, g=g, hb=hb):
                    t = GT * g + t2
                    last = None
                    for c in range(NCH):
                        last = v.scalar_tensor_tensor(bxsl(g, c, t2), hb[:, c * PP:(c + 1) * PP],
                                                      modcol(2, c, t), bxsl(g, c, t2),
                                                      OP.mult, OP.add)
                    return last
                prev_x2 = P.emit("vector", x2b, waits=[(s_ve, ve_rs)], inc=s_ve)

                sp = None
                for pi, (ap, base, n) in enumerate(bx_pieces(g)):
                    w_sp = []
                    if pi == 0:
                        w_sp.append((s_ve, prev_x2))
                        if t2 == 0 and g > 0:
                            w_sp.append((sqs, spill_done[g - 1]))
                    def spl(s, ap=ap, base=base, n=n, t2=t2, t=t):
                        src = ap[:, 0:n * GPP].rearrange("p (c u q) -> p c u q", c=n, u=GT)[:, :, t2, :]
                        return s.dma_start(out=x2d[t][:, base * PP:(base + n) * PP], in_=src)
                    sp = P.emit("sync", spl, waits=w_sp, inc=sqs, inc_by=16)
                if t2 == GT - 1:
                    spill_done[g] = sp

        def emit_main(g):
            nonlocal prev_b, b_hi_ve, prev_u_last, prev_zx_te, prev_zx_thx_last
            nonlocal sc_sr2, sc_atot, prev_dbgb, prev_dbgh1
            if dbg:
                prev_dbgh1 = P.emit("sync", (lambda s, g=g: s.dma_start(out=dbg_h1[g], in_=XBF[:, 0:GF])),
                                    waits=[(s_ve, ve_h1)], inc=ldm, inc_by=16)
            w_h1b = [(s_ve, ve_h1)]
            if prev_zx_te is not None:
                w_h1b.append((s_te, prev_zx_te))
            gp_h1b = P.emit("scalar", lambda s: s.copy(Bh1b[:, 0:GF], XBF[:, 0:GF]),
                            waits=w_h1b, inc=s_sc)

            th_done = {}
            red_done = {}
            a_done = {}
            aa_done = {}
            z_te = {}
            for c in range(NCH):
                psz = ps_a if c % 2 == 0 else ps_b

                def zmm(t_, c=c, psz=psz):
                    last = None
                    for k in range(NCH):
                        last = t_.matmul(psz[:, :],
                                         wslot[0][:, k * 1024 + c * 128: k * 1024 + (c + 1) * 128],
                                         Bh1b[:, k * GPP:(k + 1) * GPP],
                                         start=(k == 0), stop=(k == NCH - 1))
                    return last
                w_z = [(s_sc, gp_h1b)] if c == 0 else []
                if c == 0:
                    w_z.append((s_sc, wa_done))
                    if g > 0:
                        w_z.append((s_sc, prev_zx_thx_last))
                if c >= 2:
                    w_z.append((s_sc, th_done[c - 2]))
                z_te[c] = P.emit("tensor", zmm, waits=w_z, inc=s_te)

                def tha(s, c=c, psz=psz):
                    return s.activation(SCRA[:, (c % 2) * GPP:(c % 2 + 1) * GPP], psz[:, :],
                                        AF.Tanh, scale=0.5, bias=dc("bah", c))
                w_th = [(s_te, z_te[c])]
                if c >= 2:
                    w_th.append((s_ve, red_done[c - 2]))
                    w_th.append((s_sc, a_done[c - 2]))
                th_done[c] = P.emit("scalar", tha, waits=w_th, inc=s_sc)

                w_a = [(s_sc, th_done[c])]
                if c == 0:
                    w_a.append((s_ve, last_rescan[1]))
                a_done[c] = P.emit("scalar",
                                   (lambda s, c=c: s.activation(gsl(Ba, c), SCRA[:, (c % 2) * GPP:(c % 2 + 1) * GPP],
                                                                AF.Exp, scale=dc("sh", c), bias=dc("sh", c))),
                                   waits=w_a, inc=s_sc)

                def red(v, c=c):
                    src = SCRA[:, (c % 2) * GPP:(c % 2 + 1) * GPP].rearrange("p (t q) -> p t q", t=GT)
                    return v.tensor_reduce(THS[:, c * GT:(c + 1) * GT], src, AX, OP.add)
                red_done[c] = P.emit("vector", red, waits=[(s_sc, th_done[c])], inc=s_ve)

                w_aa = [(s_sc, a_done[c])]
                if c == 0 and prev_x2 is not None:
                    w_aa.append((s_ve, prev_x2))
                aa_done[c] = P.emit("gpsimd",
                                    (lambda g_, c=c: g_.tensor_tensor(gsl(Baa, c), gsl(Ba, c), gsl(Ba, c), OP.mult)),
                                    waits=w_aa, inc=s_gp)

            u_done = {}
            thx_done = {}
            for c in range(NCH):
                psz = ps_a if c % 2 == 0 else ps_b

                def zmm(t_, c=c, psz=psz):
                    last = None
                    for k in range(NCH):
                        last = t_.matmul(psz[:, :],
                                         wslot[1][:, k * 1024 + c * 128: k * 1024 + (c + 1) * 128],
                                         Bh1b[:, k * GPP:(k + 1) * GPP],
                                         start=(k == 0), stop=(k == NCH - 1))
                    return last
                w_z = [(s_sc, th_done[NCH - 2 + (c % 2)] if c < 2 else thx_done[c - 2])]
                if c == 0:
                    w_z.append((s_sc, wx_done))
                zx_te = P.emit("tensor", zmm, waits=w_z, inc=s_te)
                if c == NCH - 1:
                    prev_zx_te = zx_te

                def thx(s, c=c, psz=psz):
                    return s.activation(SCRX[:, (c % 2) * GPP:(c % 2 + 1) * GPP], psz[:, :],
                                        AF.Tanh, scale=0.5, bias=dc("bxh", c))
                w_th = [(s_te, zx_te)]
                if c >= 2:
                    w_th.append((s_ve, u_done[c - 2]))
                thx_done[c] = P.emit("scalar", thx, waits=w_th, inc=s_sc)

                def ub(v, c=c):
                    return v.scalar_tensor_tensor(gsl(Bh1, c), SCRX[:, (c % 2) * GPP:(c % 2 + 1) * GPP],
                                                  1.0, hsl(c), OP.add, OP.mult)
                u_done[c] = P.emit("vector", ub, waits=[(s_sc, thx_done[c])], inc=s_ve)
            prev_zx_thx_last = thx_done[NCH - 1]
            prev_u_last = u_done[NCH - 1]

            def atot(s, g=g):
                last = None
                for c in range(NCH):
                    last = s.activation(ATOT[:, c * GT:(c + 1) * GT], THS[:, c * GT:(c + 1) * GT],
                                        AF.Exp, scale=dc("sh", c), bias=dc("s256", c))
                return last
            w_at = [(s_ve, red_done[NCH - 1])]
            if prev_extract_gp is not None:
                w_at.append((s_gp, prev_extract_gp))
            sc_atot = P.emit("scalar", atot, waits=w_at, inc=s_sc)

            H2 = GF // 2
            w_nl4a = [(s_gp, aa_done[NCH // 2 - 1]), (s_ve, u_done[NCH // 2 - 1]),
                      (s_gp, gp_ones)]
            if dbg:
                w_nl4a.append((ldm, prev_dbgh1))
            l1a = P.emit("scalar",
                         lambda s: s.activation(XBF[:, 0:H2], Baa[:, 0:H2], AF.Ln, scale=-0.25, bias=CONSTS[:, 0:1]),
                         waits=w_nl4a, inc=s_sc)
            l1b = P.emit("scalar",
                         lambda s: s.activation(XBF[:, H2:GF], Baa[:, H2:GF], AF.Ln, scale=-0.25, bias=CONSTS[:, 0:1]),
                         waits=[(s_gp, aa_done[NCH - 1]), (s_ve, u_done[NCH - 1])], inc=s_sc)
            P.emit("scalar",
                   lambda s: s.activation(Baa[:, 0:H2], XBF[:, 0:H2], AF.Exp, scale=0.5),
                   waits=[(s_sc, l1a)], inc=s_sc)
            sc_sr2 = P.emit("scalar",
                            lambda s: s.activation(Baa[:, H2:GF], XBF[:, H2:GF], AF.Exp, scale=0.5),
                            waits=[(s_sc, l1b)], inc=s_sc)

            if g >= 2:
                w_n1 = [(s_ve, ln2ev[g - 2])]
                if g >= 3 and rows_done[g - 3] is not None:
                    w_n1.append((sqr, rows_done[g - 3]))
                nb1 = P.emit("scalar", lambda s: s.activation(VTB[:, :], VAR2[:, :], AF.Ln),
                             waits=w_n1, inc=s_sc)
                nb2 = P.emit("scalar", lambda s: s.activation(RSTD2[:, :], VTB[:, :], AF.Exp, scale=-0.5),
                             waits=[(s_sc, nb1)], inc=s_sc)
                P.emit(SCQ, (lambda s, g=g: s.dma_start(out=rowsd[g - 2, 0], in_=MB2[0:1, :])),
                       waits=[(s_sc, nb2)], inc=sqr, inc_by=16)
                rows_done[g - 2] = P.emit(SCQ, (lambda s, g=g: s.dma_start(out=rowsd[g - 2, 1], in_=RSTD2[0:1, :])),
                                          inc=sqr, inc_by=16)

            b_hi_ve = P.emit("vector",
                             lambda v: v.tensor_tensor(Bh1[:, 0:GF], Bh1[:, 0:GF], Baa[:, 0:GF], OP.mult),
                             waits=[(s_sc, sc_sr2), (s_ve, u_done[NCH - 1])], inc=s_ve)
            prev_b = None

            if dbg:
                P.emit("sync", (lambda s, g=g: s.dma_start(out=dbg_a[g], in_=Ba[:, 0:GF])),
                       waits=[(s_ve, b_hi_ve)], inc=ldm, inc_by=16)
                prev_dbgb = P.emit("sync", (lambda s, g=g: s.dma_start(out=dbg_b[g], in_=Bh1[:, 0:GF])),
                                   inc=ldm, inc_by=16)
                P.emit("vector", lambda v: v.engine_nop(), waits=[(ldm, prev_dbgb)])

        def emit_ln2_casts(g):
            w_c1 = [(s_ve, prev_x2)]
            if g + 1 < NGq:
                w_c1.append((s_te, stats_of[g + 1]))
            hv = []
            v = None
            for h in range(2):
                for c in range(4 * h, 4 * h + 4):
                    P.emit("gpsimd", (lambda g_, g2=g, c=c:
                                      g_.tensor_tensor(WST[:, c * GPP:(c + 1) * GPP], bxsl(g2, c),
                                                       ONES5[:, 0:GPP], OP.mult)),
                           waits=(w_c1 if (h == 0 and c == 0) else ()), inc=s_gp)
                v = P.emit("gpsimd", (lambda g_, g2=g, h=h:
                                      g_.tensor_tensor(H2B[:, h * 4 * GPP:(h + 1) * 4 * GPP],
                                                       bxhalf(g2, h), bxhalf(g2, h), OP.mult)),
                           inc=s_gp)
                hv.append(v)
            ln2cast_h[g] = hv
            ln2cast[g] = v

        def emit_ln2_mm(g):
            def stats2_mm(t_):
                last = None
                for c in range(NCH):
                    t_.matmul(ps_mod[:, 0:GPP], ONESB[:, :], WST[:, c * GPP:(c + 1) * GPP],
                              start=(c == 0), stop=(c == NCH - 1))
                    last = t_.matmul(ps_g[:, 0:GPP], ONESB[:, :], H2B[:, c * GPP:(c + 1) * GPP],
                                     start=(c == 0), stop=(c == NCH - 1))
                return last
            w_l2 = [(s_gp, ln2cast[g])]
            if g == 0:
                w_l2.append((s_ve, acc_hist[-1]))
            else:
                w_l2.append((s_ve, ln2ev[g - 1]))
                w_l2.append((s_sc, ln2sc[g - 1]))
            ln2mm[g] = P.emit("tensor", stats2_mm, waits=w_l2, inc=s_te)
            w_ls = [(s_te, ln2mm[g])]
            if P.val(sqr):
                w_ls.append((sqr, P.val(sqr)))
            ln2sc[g] = P.emit("scalar", lambda s: s.activation(MB2[:, :], ps_mod[:, 0:GPP], AF.Copy, scale=1.0 / D),
                              waits=w_ls, inc=s_sc)
            w1 = P.emit("vector", lambda v: v.tensor_tensor(VT[:, :], MB2[:, :], MB2[:, :], OP.mult),
                        waits=[(s_sc, ln2sc[g])], inc=s_ve)
            w2 = P.emit("vector", lambda v: v.tensor_scalar_sub(VTB[:, :], VT[:, :], 1e-5),
                        waits=[(s_ve, w1)], inc=s_ve)
            ln2ev[g] = P.emit("vector", lambda v: v.scalar_tensor_tensor(VAR2[:, :], ps_g[:, 0:GPP], 1.0 / D,
                                                                         VTB[:, :], OP.mult, OP.subtract),
                              waits=[(s_ve, w2)], inc=s_ve)

        emit_xload(0)
        emit_xload(1)
        wa_done = load_weight(0, wd["WaT"])
        wx_done = load_weight(1, wd["WxT"])
        tailst = None
        for g in range(NG):
            emit_prelude_casts(g)
            if g > 0:
                tailst = emit_tail_scanA(g - 1)
            emit_prelude_stats(g)
            emit_nlw2(g)
            emit_lnblock(g)
            if g > 0:
                emit_tail_scanB(g - 1, *tailst)
                emit_ln2_casts(g - 1)
            emit_main(g)
            if g > 0:
                emit_ln2_mm(g - 1)
            if 2 <= g + 1 < NG:
                emit_xload(g + 1)

        tailst = emit_tail_scanA(NG - 1)
        emit_tail_scanB(NG - 1, *tailst)
        emit_ln2_casts(NG - 1)

        def emit_rows_fin(gg):
            w_n1 = [(s_ve, ln2ev[gg])]
            if rows_done[gg - 1] is not None:
                w_n1.append((sqr, rows_done[gg - 1]))
            nf1 = P.emit("scalar", lambda s: s.activation(VTB[:, :], VAR2[:, :], AF.Ln),
                         waits=w_n1, inc=s_sc)
            sc_fin = P.emit("scalar", lambda s: s.activation(RSTD2[:, :], VTB[:, :], AF.Exp, scale=-0.5),
                            waits=[(s_sc, nf1)], inc=s_sc)
            P.emit(SCQ, (lambda s, gg=gg: s.dma_start(out=rowsd[gg, 0], in_=MB2[0:1, :])),
                   waits=[(s_sc, sc_fin)], inc=sqr, inc_by=16)
            rows_done[gg] = P.emit(SCQ, (lambda s, gg=gg: s.dma_start(out=rowsd[gg, 1], in_=RSTD2[0:1, :])),
                                   inc=sqr, inc_by=16)

        emit_rows_fin(NG - 2)
        emit_ln2_mm(NG - 1)
        emit_rows_fin(NG - 1)


        attn_done = [(s_ve, P.val(s_ve)), (s_gp, P.val(s_gp)),
                     (s_sc, P.val(s_sc)), (s_te, P.val(s_te))]
        attn_te_total = P.val(s_te)

        slot_free = {
            0: [(s_te, prev_zx_te)],
            1: [(s_te, prev_zx_te)],
            2: [(s_gp, ln2cast_h[NG - 2][1]), (sqs, spill_done[NG - 2])],
            3: [(s_ve, last_rescan[1])],
            4: [(s_ve, last_rescan[1])],
            5: [(s_ve, prev_x2)],
        }
        stage_free = [(s_ve, prev_u_last), (s_sc, sc_atot)]
        ffn_w = {}
        for i, n in enumerate(["wfT", "wiT", "wcT", "ufT", "uiT", "ucT"]):
            ffn_w[n] = load_weight(i, wd[n],
                                   waits_extra=(stage_free if i == 0 else ()),
                                   stages=(SCRA[:, 0:1024], SCRX[:, 0:1024]),
                                   cast_extra=slot_free[i])
        ffn_w_done = ffn_w["ucT"]

        w_c0 = list(attn_done) + [(sqs, P.val(sqs))]
        cz = P.emit("vector", lambda v: v.memset(Z3[:, 0:PP], 0.0), waits=w_c0, inc=s_ve)

        def c0_build(v):
            last = None
            for c in range(NCH):
                last = v.tensor_scalar_add(C[:, c * PP:(c + 1) * PP], Z3[:, 0:PP], vc("inc", c))
            return last
        c0b = P.emit("vector", c0_build, waits=[(s_ve, cz)], inc=s_ve)
        ve_c0 = P.emit("vector", lambda v: v.tensor_copy(CB[:, 0:TF], C[:, :]),
                       waits=[(s_ve, c0b)], inc=s_ve)

        sqs_total = P.val(sqs)
        sqr_total = P.val(sqr)
        x2_ld = [None, None]
        row_ld = [None, None]
        h2prep = [None, None]
        h2cast = [None, None]
        bcast_te = [None, None]
        gate_te_last = [None, None]
        ew_last = [None, None]
        prev_ew_all = ve_c0
        last_h2cast = None

        W3 = {0: "wfT", 1: "wiT", 2: "wcT"}
        U3 = {0: "ufT", 1: "uiT", 2: "ucT"}
        WIDX = {"wfT": 0, "wiT": 1, "wcT": 2, "ufT": 3, "uiT": 4, "ucT": 5}

        def prep_step(t):
            par = t % 2
            g = t // GT
            t2 = t % GT
            w_x = [(sqs, sqs_total)]
            if t < 2:
                w_x.append((s_te, attn_te_total))
            if h2prep[par] is not None:
                w_x.append((s_ve, h2prep[par]))
            x2_ld[par] = P.emit("sync", (lambda s, t=t, par=par:
                                         s.dma_start(out=XBF[:, par * 2048: par * 2048 + TF], in_=x2d[t])),
                                waits=w_x, inc=ldx2[par], inc_by=16)

            w_r = [(sqr, sqr_total)]
            if bcast_te[par] is not None:
                w_r.append((s_te, bcast_te[par]))
            P.emit("sync", (lambda s, g=g, t2=t2, par=par:
                            s.dma_start(out=ROWSTAGE[0:1, par * 2 * PP: par * 2 * PP + PP],
                                        in_=rowsd[g, 0, t2 * PP:(t2 + 1) * PP])),
                   waits=w_r, inc=ldr[par], inc_by=16)
            row_ld[par] = P.emit("sync", (lambda s, g=g, t2=t2, par=par:
                                          s.dma_start(out=ROWSTAGE[0:1, par * 2 * PP + PP: par * 2 * PP + 2 * PP],
                                                      in_=rowsd[g, 1, t2 * PP:(t2 + 1) * PP])),
                                 inc=ldr[par], inc_by=16)

            psbc = ps_c if par == 0 else ps_d
            w_bc = [(ldr[par], row_ld[par]), (s_sc, ffn_w_done)]
            if h2prep[par] is not None:
                w_bc.append((s_ve, h2prep[par]))
            if t == 0:
                w_bc.append((s_te, P.val(s_te)))
            bcast_te[par] = P.emit("tensor", (lambda t_, par=par, psbc=psbc:
                                              t_.matmul(psbc[:, 0:2 * PP], ONESF[:, :],
                                                        ROWSTAGE[:, par * 2 * PP:(par + 1) * 2 * PP],
                                                        start=True, stop=True)),
                                   waits=w_bc, inc=s_te)

            def h2p_a(v, par=par, psbc=psbc):
                xv = XBF[:, par * 2048: par * 2048 + TF].rearrange("p (c q) -> p c q", c=NCH)
                yv = Y[:, :].rearrange("p (c q) -> p c q", c=NCH)
                mb_bc = sap(psbc, 0, [[2 * PP, 128], [0, NCH], [1, PP]])
                return v.tensor_tensor(yv, xv, mb_bc, OP.subtract)

            def h2p_b(v, psbc=psbc):
                yv = Y[:, :].rearrange("p (c q) -> p c q", c=NCH)
                rs_bc = sap(psbc, PP, [[2 * PP, 128], [0, NCH], [1, PP]])
                return v.tensor_tensor(yv, yv, rs_bc, OP.mult)

            def h2p_c(v, t=t):
                last = None
                for c in range(NCH):
                    last = v.scalar_tensor_tensor(Y[:, c * PP:(c + 1) * PP], Y[:, c * PP:(c + 1) * PP],
                                                  modcol(4, c, t), modcol_bc(3, c, t, PP),
                                                  OP.mult, OP.add)
                return last
            w_h2 = [(s_te, bcast_te[par]), (ldx2[par], x2_ld[par])]
            if last_h2cast is not None:
                w_h2.append((s_sc, last_h2cast))
            if t == 0:
                w_h2.append((s_ve, ve_c0))
            hpa = P.emit("vector", h2p_a, waits=w_h2, inc=s_ve)
            hpb = P.emit("vector", h2p_b, waits=[(s_ve, hpa)], inc=s_ve)
            h2prep[par] = P.emit("vector", h2p_c, waits=[(s_ve, hpb)], inc=s_ve)

        def cast_step(t):
            nonlocal last_h2cast
            par = t % 2
            w_hc = [(s_ve, h2prep[par])]
            if gate_te_last[par] is not None:
                w_hc.append((s_te, gate_te_last[par]))
            h2cast[par] = P.emit("scalar", (lambda s, par=par:
                                            s.copy(H2B[:, par * 2048: par * 2048 + TF], Y[:, :])),
                                 waits=w_hc, inc=s_sc)
            last_h2cast = h2cast[par]

        prep_step(0)
        cast_step(0)

        for t in range(T):
            par = t % 2
            if t + 1 < T:
                prep_step(t + 1)

            ew_mc = {}
            sc_ev_mc = {}
            if t == 0:
                prev_psg_ev = {0: None, 1: None}
            for mc in range(NCH):
                psg = ps_g if mc % 2 == 0 else ps_mod
                def gmm_w(t_, gate, mc=mc, par=par, psg=psg):
                    last = None
                    for k in range(NCH):
                        mov = H2B[:, par * 2048 + k * PP: par * 2048 + (k + 1) * PP]
                        last = t_.matmul(psg[:, gate * PP:(gate + 1) * PP],
                                         wslot[WIDX[W3[gate]]][:, k * 1024 + mc * 128: k * 1024 + (mc + 1) * 128],
                                         mov, start=(k == 0), stop=False)
                    return last

                def gmm_u(t_, gate, mc=mc, par=par, psg=psg):
                    last = None
                    for k in range(NCH):
                        mov = CB[:, par * 2048 + k * PP: par * 2048 + (k + 1) * PP]
                        last = t_.matmul(psg[:, gate * PP:(gate + 1) * PP],
                                         wslot[WIDX[U3[gate]]][:, k * 1024 + mc * 128: k * 1024 + (mc + 1) * 128],
                                         mov, start=False, stop=(k == NCH - 1))
                    return last
                w_g = []
                if mc == 0:
                    w_g = [(s_sc, h2cast[par]), (s_sc, ffn_w_done)]
                    if t == 0:
                        w_g.append((s_ve, ln2ev[NG - 1]))
                        w_g.append((s_sc, ln2sc[NG - 1]))
                    if ew_last[par] is not None:
                        w_g.append((s_ve, ew_last[par]))
                if mc >= 2:
                    w_g.append((s_sc, sc_ev_mc[mc - 2]))
                    w_g.append((s_ve, ew_mc.get(mc - 2)))
                else:
                    w_g.append((s_sc, prev_psg_ev[mc % 2]))
                import functools as _ft
                P.emit("tensor", _ft.partial(gmm_w, gate=0), waits=w_g, inc=s_te)
                w_gu = []
                if mc == 0:
                    w_gu = [(s_ve, prev_ew_all)]
                gt = P.emit("tensor", _ft.partial(gmm_u, gate=0), waits=w_gu, inc=s_te)
                for gate in range(1, 3):
                    P.emit("tensor", _ft.partial(gmm_w, gate=gate), inc=s_te)
                    gt = P.emit("tensor", _ft.partial(gmm_u, gate=gate), inc=s_te)
                if mc == NCH - 1:
                    gate_te_last[par] = gt

                mp = mc % 2

                def gevac(s, mc=mc, mp=mp, psg=psg):
                    s.activation(THF[:, mp * PP:(mp + 1) * PP], psg[:, 0:PP], AF.Tanh,
                                 scale=0.5, bias=dc("bfh", mc))
                    s.activation(THI[:, mp * PP:(mp + 1) * PP], psg[:, PP:2 * PP], AF.Tanh,
                                 scale=0.5, bias=dc("bih", mc))
                    return s.activation(THG[:, mp * PP:(mp + 1) * PP], psg[:, 2 * PP:3 * PP], AF.Tanh,
                                        bias=vc("bc", mc))
                w_e = [(s_te, gt)]
                if mc >= 2:
                    w_e.append((s_ve, ew_mc[mc - 2]))
                sc_ev = P.emit("scalar", gevac, waits=w_e, inc=s_sc)
                sc_ev_mc[mc] = sc_ev
                if mc >= NCH - 2:
                    prev_psg_ev[mc % 2] = sc_ev

                def gew_a(v, mc=mc, mp=mp):
                    v.scalar_tensor_tensor(U1s[:, mp * PP:(mp + 1) * PP], THF[:, mp * PP:(mp + 1) * PP],
                                           1.0, C[:, mc * PP:(mc + 1) * PP], OP.add, OP.mult)
                    return v.scalar_tensor_tensor(U2s[:, mp * PP:(mp + 1) * PP], THI[:, mp * PP:(mp + 1) * PP],
                                                  1.0, THG[:, mp * PP:(mp + 1) * PP], OP.add, OP.mult)

                def gew_b(v, mc=mc, mp=mp):
                    return v.tensor_tensor(U1s[:, mp * PP:(mp + 1) * PP], U1s[:, mp * PP:(mp + 1) * PP],
                                           U2s[:, mp * PP:(mp + 1) * PP], OP.add)

                def gew_c(v, mc=mc, mp=mp, t=t):
                    v.tensor_scalar_mul(C[:, mc * PP:(mc + 1) * PP], U1s[:, mp * PP:(mp + 1) * PP], 0.5)
                    return v.tensor_scalar_mul(CB[:, (1 - t % 2) * 2048 + mc * PP: (1 - t % 2) * 2048 + (mc + 1) * PP],
                                               U1s[:, mp * PP:(mp + 1) * PP], 0.5)
                ga_ = P.emit("vector", gew_a, waits=[(s_sc, sc_ev)], inc=s_ve)
                gb_ = P.emit("vector", gew_b, waits=[(s_ve, ga_)], inc=s_ve)
                ew_mc[mc] = P.emit("vector", gew_c, waits=[(s_ve, gb_)], inc=s_ve)
            ew_last[par] = ew_mc[NCH - 1]
            prev_ew_all = ew_mc[NCH - 1]
            if t + 1 < T:
                cast_step(t + 1)

        uo_done = load_weight(1, wd["uoT"], waits_extra=[(s_te, gate_te_last[(T - 1) % 2])])
        wo_done = load_weight(0, wd["woT"], waits_extra=[(s_te, gate_te_last[(T - 1) % 2])])
        cfin_par = T % 2

        u_ev = {}
        for oc in range(NCH):
            pu = ps_c if oc % 2 == 0 else ps_d

            def umm(t_, oc=oc, pu=pu):
                last = None
                for k in range(NCH):
                    last = t_.matmul(pu[:, 0:PP],
                                     wslot[1][:, k * 1024 + oc * 128: k * 1024 + (oc + 1) * 128],
                                     CB[:, cfin_par * 2048 + k * PP: cfin_par * 2048 + (k + 1) * PP],
                                     start=(k == 0), stop=(k == NCH - 1))
                return last
            w_u = [(s_ve, prev_ew_all), (s_sc, uo_done)]
            if oc == 0:
                w_u.append((s_te, bcast_te[0]))
                w_u.append((s_te, bcast_te[1]))
            if oc >= 2:
                w_u.append((s_sc, u_ev[oc - 2]))
            ut = P.emit("tensor", umm, waits=w_u, inc=s_te)
            u_ev[oc] = P.emit("scalar", (lambda s, oc=oc, pu=pu:
                                         s.activation(C[:, oc * PP:(oc + 1) * PP], pu[:, 0:PP],
                                                      AF.Identity, bias=vc("bo", oc))),
                              waits=[(s_te, ut), (s_ve, prev_ew_all)], inc=s_sc)
        u_done_all = u_ev[NCH - 1]

        out_dma_cnt = 0
        prev_outdma = {}
        prev_h2o = [None, None]
        prev_h2co = [None, None]
        prev_bc_o = [None, None]
        prev_mm_o = {}
        prev_stt_o = {}
        for gp_ in range(NG):
            t0o = GT * gp_
            for t2 in range(GT):
                t = t0o + t2
                par = t2

                w_x = []
                if gp_ == 0:
                    w_x.append((s_sc, h2cast[par]))
                    w_x.append((s_ve, h2prep[par]))
                else:
                    w_x.append((s_ve, prev_h2o[par]))
                    w_x.append((s_ve, prev_stt_o[(gp_ - 1, NCH - 1, par)]))
                x2o = P.emit("sync", (lambda s, t=t, par=par:
                                      s.dma_start(out=XBF[:, par * 2048: par * 2048 + TF], in_=x2d[t])),
                             waits=w_x, inc=ldx2[par], inc_by=16)

                w_r = []
                if prev_bc_o[par] is not None:
                    w_r.append((s_te, prev_bc_o[par]))
                elif gp_ == 0:
                    w_r.append((s_te, bcast_te[par]))

                P.emit("sync", (lambda s, g=gp_, t2=t2, par=par:
                                s.dma_start(out=ROWSTAGE[0:1, par * 2 * PP: par * 2 * PP + PP],
                                            in_=rowsd[g, 0, t2 * PP:(t2 + 1) * PP])),
                       waits=w_r, inc=ldr[par], inc_by=16)
                rowo = P.emit("sync", (lambda s, g=gp_, t2=t2, par=par:
                                       s.dma_start(out=ROWSTAGE[0:1, par * 2 * PP + PP: par * 2 * PP + 2 * PP],
                                                   in_=rowsd[g, 1, t2 * PP:(t2 + 1) * PP])),
                              inc=ldr[par], inc_by=16)

                psbc = ps_c if par == 0 else ps_d
                w_bc = [(ldr[par], rowo), (s_sc, u_done_all)]
                if prev_h2o[par] is not None:
                    w_bc.append((s_ve, prev_h2o[par]))
                bco = P.emit("tensor", (lambda t_, par=par, psbc=psbc:
                                        t_.matmul(psbc[:, 0:2 * PP], ONESF[:, :],
                                                  ROWSTAGE[:, par * 2 * PP:(par + 1) * 2 * PP],
                                                  start=True, stop=True)),
                             waits=w_bc, inc=s_te)
                prev_bc_o[par] = bco

                def h2po_a(v, par=par, psbc=psbc):
                    xv = XBF[:, par * 2048: par * 2048 + TF].rearrange("p (c q) -> p c q", c=NCH)
                    yv = Y[:, :].rearrange("p (c q) -> p c q", c=NCH)
                    mb_bc = sap(psbc, 0, [[2 * PP, 128], [0, NCH], [1, PP]])
                    return v.tensor_tensor(yv, xv, mb_bc, OP.subtract)

                def h2po_b(v, psbc=psbc):
                    yv = Y[:, :].rearrange("p (c q) -> p c q", c=NCH)
                    rs_bc = sap(psbc, PP, [[2 * PP, 128], [0, NCH], [1, PP]])
                    return v.tensor_tensor(yv, yv, rs_bc, OP.mult)

                def h2po_c(v, t=t):
                    last = None
                    for c in range(NCH):
                        last = v.scalar_tensor_tensor(Y[:, c * PP:(c + 1) * PP], Y[:, c * PP:(c + 1) * PP],
                                                      modcol(4, c, t), modcol_bc(3, c, t, PP),
                                                      OP.mult, OP.add)
                    return last
                w_h2 = [(s_te, bco), (ldx2[par], x2o)]
                w_h2.append((s_sc, last_h2cast))
                if gp_ == 0 and t2 == 0:
                    w_h2.append((s_ve, prev_ew_all))
                hoa = P.emit("vector", h2po_a, waits=w_h2, inc=s_ve)
                hob = P.emit("vector", h2po_b, waits=[(s_ve, hoa)], inc=s_ve)
                h2o = P.emit("vector", h2po_c, waits=[(s_ve, hob)], inc=s_ve)
                prev_h2o[par] = h2o

                w_hc = [(s_ve, h2o)]
                if prev_mm_o.get(par) is not None:
                    w_hc.append((s_te, prev_mm_o[par]))
                h2co = P.emit("scalar", (lambda s, par=par:
                                         s.copy(H2B[:, par * 2048: par * 2048 + TF], Y[:, :])),
                              waits=w_hc, inc=s_sc)
                prev_h2co[par] = h2co
                last_h2cast = h2co

            def psout(t2, op_):
                if t2 == 0:
                    return (ps_a if op_ == 0 else ps_b)[:, 0:PP]
                return (ps_g if op_ == 0 else ps_mod)[:, 0:PP]
            mm_oc = {}
            for oc in range(NCH):
                def omm(t_, oc=oc):
                    last = None
                    for k in range(NCH):
                        for t2 in range(GT):
                            last = t_.matmul(psout(t2, oc % 2),
                                             wslot[0][:, k * 1024 + oc * 128: k * 1024 + (oc + 1) * 128],
                                             H2B[:, t2 * 2048 + k * PP: t2 * 2048 + (k + 1) * PP],
                                             start=(k == 0), stop=(k == NCH - 1))
                    return last
                w_m = [(s_sc, prev_h2co[1]), (s_sc, wo_done)]
                if gp_ == 0 and oc == 0:
                    w_m.append((s_ve, prev_ew_all))
                    w_m.append((s_sc, u_done_all))
                if oc >= 2:
                    w_m.append((s_ve, prev_stt_o[(gp_, oc - 2, 1)]))
                mm_oc[oc] = P.emit("tensor", omm, waits=w_m, inc=s_te)

                op_ = oc % 2
                for t2 in range(GT):
                    t = t0o + t2

                    def oev(v, oc=oc, op_=op_, t2=t2):
                        return v.tensor_tensor(Z3[:, op_ * PP:(op_ + 1) * PP],
                                               psout(t2, op_),
                                               C[:, oc * PP:(oc + 1) * PP], OP.add)
                    w_e = [(s_te, mm_oc[oc]), (s_sc, u_done_all)]
                    if (gp_, oc - 2, t2) in prev_stt_o or t2 > 0 or oc > 0 or gp_ > 0:
                        w_e.append((s_sc, P.val(s_sc)))
                    vz = P.emit("vector", oev, waits=w_e, inc=s_ve)
                    sz = P.emit("scalar", (lambda s, op_=op_: s.activation(Z3B[:, op_ * PP:(op_ + 1) * PP],
                                                                           Z3[:, op_ * PP:(op_ + 1) * PP], AF.Tanh)),
                                waits=[(s_ve, vz)], inc=s_sc)

                    def ostt(v, oc=oc, op_=op_, t=t, par=t2):
                        return v.scalar_tensor_tensor(U1s[:, op_ * PP:(op_ + 1) * PP],
                                                      Z3B[:, op_ * PP:(op_ + 1) * PP],
                                                      modcol(5, oc, t),
                                                      XBF[:, par * 2048 + oc * PP: par * 2048 + (oc + 1) * PP],
                                                      OP.mult, OP.add)
                    w_s = [(s_sc, sz)]
                    if prev_outdma.get(op_) is not None:
                        w_s.append((sqo[op_], prev_outdma[op_]))
                    so = P.emit("vector", ostt, waits=w_s, inc=s_ve)
                    prev_stt_o[(gp_, oc, t2)] = so

                    od = P.emit(SCQ, (lambda s, t=t, oc=oc, op_=op_:
                                           s.dma_start(out=outp[t][:, oc * PP:(oc + 1) * PP],
                                                       in_=U1s[:, op_ * PP:(op_ + 1) * PP])),
                                waits=[(s_ve, so)], inc=sqo[op_], inc_by=16)
                    prev_outdma[op_] = od
                    out_dma_cnt += 1

        if dbg:
            def dvd(s):
                s.dma_start(out=dbg_dv[:, 0:64], in_=VEC[:, :]).then_inc(ldm, 16)
                return s.dma_start(out=dbg_dv[:, 64:112], in_=DVEC[:, 0:48])
            P.cnt[id(ldm)] = P.val(ldm) + 16
            P.emit("sync", dvd, waits=[(sqs, P.val(sqs))], inc=ldm, inc_by=16)

        if dbg:
            first_w = [(sqs, P.val(sqs))]
            for t in range(T):
                P.emit("sync", (lambda s, t=t: s.dma_start(out=dbg_x2[t], in_=x2d[t])),
                       waits=(first_w if t == 0 else ()), inc=ldm, inc_by=16)
                P.emit("sync", (lambda s, t=t: s.dma_start(out=dbg_ci[t], in_=cind[t])),
                       inc=ldm, inc_by=16)
            P.emit("sync", lambda s: None, waits=[(ldm, P.val(ldm))])

        P.emit("sync", lambda s: None,
               waits=[(sqo[0], P.val(sqo[0])), (sqo[1], P.val(sqo[1])),
                      (sqs, P.val(sqs)), (sqr, P.val(sqr)), (ccs, P.val(ccs)),
                      (gqa[0], P.val(gqa[0])), (gqa[1], P.val(gqa[1]))])

        block = e(nc.Block())

        @block.sync
        def _(s):
            P.replay("sync", s)

        @block.tensor
        def _(t_):
            P.replay("tensor", t_)

        @block.vector
        def _(v):
            P.replay("vector", v)

        @block.scalar
        def _(s):
            P.replay("scalar", s)

        @block.gpsimd
        def _(g_):
            P.replay("gpsimd", g_)

    return nc



def _pack_vec(v):
    return np.ascontiguousarray(v.reshape(8, 128).T)


def prep_inputs(inputs, T, PP):
    NG = T // GT
    GF = NCH * GT * PP
    x = np.asarray(inputs["L0_attn_x"], dtype=np.float32)
    Wa = np.asarray(inputs["L0_attn_Wa"], np.float32)
    Wx = np.asarray(inputs["L0_attn_Wx"], np.float32)
    weights = {
        "WaT": np.ascontiguousarray(Wa.T), "WxT": np.ascontiguousarray(Wx.T),
        "wfT": np.ascontiguousarray(np.asarray(inputs["L0_ffn_wf"], np.float32).T),
        "wiT": np.ascontiguousarray(np.asarray(inputs["L0_ffn_wi"], np.float32).T),
        "wcT": np.ascontiguousarray(np.asarray(inputs["L0_ffn_wc"], np.float32).T),
        "ufT": np.ascontiguousarray(np.asarray(inputs["L0_ffn_uf"], np.float32).T),
        "uiT": np.ascontiguousarray(np.asarray(inputs["L0_ffn_ui"], np.float32).T),
        "ucT": np.ascontiguousarray(np.asarray(inputs["L0_ffn_uc"], np.float32).T),
        "woT": np.ascontiguousarray(np.asarray(inputs["L0_ffn_wo"], np.float32).T),
        "uoT": np.ascontiguousarray(np.asarray(inputs["L0_ffn_uo"], np.float32).T),
    }
    wmod_full = np.asarray(inputs["L0_wmod"], np.float32)
    cc = np.asarray(inputs["L0_c"], np.float32)
    scT = np.ascontiguousarray(cc.reshape(T, 8, 128).transpose(2, 1, 0).reshape(128, 8 * T))
    vecs = np.zeros((128, 64), np.float32)
    order = ["L0_attn_ba", "L0_attn_bx", "L0_ffn_bf", "L0_ffn_bi", "L0_ffn_bc",
             "L0_ffn_bo", "L0_ffn_in_c", "L0_attn_Lam"]
    for i, n in enumerate(order):
        vecs[:, 8 * i:8 * (i + 1)] = _pack_vec(np.asarray(inputs[n], np.float32))

    in_maps = []
    for core in range(NCORES):
        s0 = core * PP
        xc = x[:, s0:s0 + PP, :]
        xr = xc.reshape(NG, GT, PP, NCH, 128)
        xhost = np.ascontiguousarray(xr.transpose(0, 4, 3, 1, 2).reshape(NG, 128, GF))
        masks = np.zeros((128, 128), np.float32)
        for c in range(8):
            for j in range(8):
                masks[:, c * 8 + j] = 1.0 if j < core else 0.0
                masks[:, 64 + c * 8 + j] = 0.0 if j < core else 1.0
        m = {"xh": xhost, "scT": scT, "vecs": vecs, "cmask": masks,
             "wmod": np.ascontiguousarray(wmod_full[:, core * 768:(core + 1) * 768])}
        m.update(weights)
        in_maps.append(m)
    return in_maps


def assemble(results, T, PP):
    S = NCORES * PP
    out = np.empty((T, S, D), np.float32)
    for core in range(NCORES):
        oc = results[core]["outp"]
        o = oc.reshape(T, 128, NCH, PP).transpose(0, 3, 2, 1).reshape(T, PP, D)
        out[:, core * PP:(core + 1) * PP, :] = o
    return out


def kernel(**inputs):
    T, PP = 16, 256
    nc = build_nc(T, PP)
    in_maps = prep_inputs(inputs, T, PP)
    res = run_bass_kernel_spmd(nc, in_maps, core_ids=list(range(NCORES)))
    return assemble(res.results, T, PP)


# version 16
# speedup vs baseline: 1.0822x; 1.0148x over previous
import os as _os
import numpy as np
import concourse.bass as bass
import concourse.mybir as mybir
from concourse.bass_utils import run_bass_kernel_spmd
from contextlib import ExitStack

F32 = mybir.dt.float32
BF16 = mybir.dt.bfloat16
AF = mybir.ActivationFunctionType
OP = mybir.AluOpType
AX = mybir.AxisListType.X

NCORES = 8
D = 1024
NCH = 8
GT = 2
LN2C = 0.6931471805599453

ENGS = ("sync", "tensor", "vector", "scalar", "gpsimd")


PARANOID = _os.environ.get("KPARANOID", "0") == "1"
SCQ = "sync" if _os.environ.get("KSYNCQ", "0") == "1" else "scalar"


class Prog:

    def __init__(self):
        self.ops = {e: [] for e in ENGS}
        self.cnt = {}
        self.esem = {}

    def set_esems(self, m):
        self.esem = m

    def val(self, sem):
        return self.cnt.get(id(sem), 0)

    def emit(self, eng, builder, waits=(), inc=None, inc_by=1):
        waits = list(waits)
        if PARANOID and self.esem:
            for k in ("te", "ve", "sc", "gp"):
                s = self.esem[k]
                waits.append((s, self.val(s)))
            if inc is None and eng in self.esem_by_eng:
                inc = self.esem_by_eng[eng]
                inc_by = 1
        after = None
        if inc is not None:
            after = self.val(inc) + inc_by
            self.cnt[id(inc)] = after
        self.ops[eng].append((waits, builder, inc, inc_by))
        return after

    def replay(self, eng_name, e):
        for waits, builder, inc, inc_by in self.ops[eng_name]:
            for sem, v in waits:
                if v is None or v == 0:
                    continue
                e.wait_ge(sem, v)
            ins = builder(e)
            if inc is not None:
                if ins is None:
                    if PARANOID:
                        continue
                    assert False, "builder for inc op returned no instruction"
                ins.then_inc(inc, inc_by)


def sap(t, off, dims):
    return bass.AP(t, off, dims)


def build_nc(T, PP, dbg=False):
    NG = T // GT
    GPP = GT * PP
    GF = NCH * GPP
    TF = NCH * PP
    JC = 6 * D // 128
    MODW = JC * T

    nc = bass.Bass("TRN2", target_bir_lowering=False, detect_race_conditions=False)

    xh = nc.dram_tensor("xh", [NG, 128, GF], F32, kind="ExternalInput")
    wnames = ["WaT", "WxT", "wfT", "wiT", "wcT", "ufT", "uiT", "ucT", "woT", "uoT"]
    wd = {n: nc.dram_tensor(n, [D, D], F32, kind="ExternalInput") for n in wnames}
    wmodd = nc.dram_tensor("wmod", [D, 6 * D // NCORES], F32, kind="ExternalInput")
    sctd = nc.dram_tensor("scT", [128, NCH * T], F32, kind="ExternalInput")
    vecsd = nc.dram_tensor("vecs", [128, 64], F32, kind="ExternalInput")
    masksd = nc.dram_tensor("cmask", [128, 128], F32, kind="ExternalInput")
    outp = nc.dram_tensor("outp", [T, 128, TF], F32, kind="ExternalOutput")
    x2d = nc.dram_tensor("x2d", [T, 128, TF], F32)
    rowsd = nc.dram_tensor("rowsd", [NG, 2, GPP], F32)
    cind = nc.dram_tensor("cin", [T, 128, 16], F32)
    coutd = nc.dram_tensor("cout", [T, 8, 128, 16], F32)
    modind = nc.dram_tensor("modin", [128, 6 * NCH * T // NCORES], F32)
    modoutd = nc.dram_tensor("modout", [NCORES, 128, 6 * NCH * T // NCORES], F32)
    dbg_x2 = nc.dram_tensor("dbg_x2", [T, 128, NCH * PP], F32, kind="ExternalOutput") if dbg else None
    dbg_a = nc.dram_tensor("dbg_a", [T // GT, 128, NCH * GT * PP], F32, kind="ExternalOutput") if dbg else None
    dbg_b = nc.dram_tensor("dbg_b", [T // GT, 128, NCH * GT * PP], F32, kind="ExternalOutput") if dbg else None
    dbg_ci = nc.dram_tensor("dbg_ci", [T, 128, 16], F32, kind="ExternalOutput") if dbg else None
    dbg_h1 = nc.dram_tensor("dbg_h1", [T // GT, 128, NCH * GT * PP], F32, kind="ExternalOutput") if dbg else None
    dbg_dv = nc.dram_tensor("dbg_dv", [128, 128], F32, kind="ExternalOutput") if dbg else None
    dbg_cmb = nc.dram_tensor("dbg_cmb", [128, 512], F32, kind="ExternalOutput") if dbg else None
    dbg_mod = nc.dram_tensor("dbg_mod", [128, 6 * NCH * T], F32, kind="ExternalOutput") if dbg else None

    with ExitStack() as stack:
        e = stack.enter_context

        wslot = [e(nc.sbuf_tensor(f"wslot{i}", [128, 8192], BF16)) for i in range(6)]
        Bx = wslot[2].ap().bitcast(F32)
        Bh1 = wslot[3].ap().bitcast(F32)
        Ba = wslot[4].ap().bitcast(F32)
        Baa = wslot[5].ap().bitcast(F32)
        Bh1b = e(nc.sbuf_tensor("Bh1b", [128, 4096], BF16))
        XBIG = e(nc.sbuf_tensor("XBIG", [128, 8192], BF16))
        XBF = XBIG.ap().bitcast(F32)
        SCRA = e(nc.sbuf_tensor("SCRA", [128, max(2 * GPP, 1024)], F32))
        SCRX = e(nc.sbuf_tensor("SCRX", [128, max(2 * GPP, 1024)], F32))
        Y = e(nc.sbuf_tensor("Y", [128, TF], F32))
        MB = e(nc.sbuf_tensor("MB", [128, GPP], F32))
        RSTD = e(nc.sbuf_tensor("RSTD", [128, GPP], F32))
        MB2 = e(nc.sbuf_tensor("MB2", [128, GPP], F32))
        RSTD2 = e(nc.sbuf_tensor("RSTD2", [128, GPP], F32))
        VAR = e(nc.sbuf_tensor("VAR", [128, GPP], F32))
        VAR2 = e(nc.sbuf_tensor("VAR2", [128, GPP], F32))
        VT = e(nc.sbuf_tensor("VT", [128, GPP], F32))
        VTB = e(nc.sbuf_tensor("VTB", [128, GPP], F32))
        CARRY = e(nc.sbuf_tensor("CARRY", [128, 32], F32))
        AGB = e(nc.sbuf_tensor("AGB", [128, 256], F32))
        HIN = e(nc.sbuf_tensor("HIN", [128, 16], F32))
        CMB1 = e(nc.sbuf_tensor("CMB1", [128, 64], F32))
        CMB2 = e(nc.sbuf_tensor("CMB2", [128, 64], F32))
        CMB3 = e(nc.sbuf_tensor("CMB3", [128, 64], F32))
        ATOT = e(nc.sbuf_tensor("ATOT", [128, 16], F32))
        THS = e(nc.sbuf_tensor("THS", [128, 16], F32))
        MOD = e(nc.sbuf_tensor("MOD", [128, MODW], F32))
        SCT = e(nc.sbuf_tensor("SCT", [128, NCH * T], F32))
        SCT2 = e(nc.sbuf_tensor("SCT2", [128, NCH * T], F32))
        VEC = e(nc.sbuf_tensor("VEC", [128, 64], F32))
        DVEC = e(nc.sbuf_tensor("DVEC", [128, 64], F32))
        MASK = e(nc.sbuf_tensor("MASK", [128, 128], F32))
        ONESB = e(nc.sbuf_tensor("ONESB", [128, 128], BF16))
        ONESF = e(nc.sbuf_tensor("ONESF", [128, 128], F32))
        ONES5 = e(nc.sbuf_tensor("ONES5", [128, 512], F32))
        WST = e(nc.sbuf_tensor("WST", [128, 4096], BF16))
        WSTf = WST.ap().bitcast(F32)
        ROWSTAGE = e(nc.sbuf_tensor("ROWSTAGE", [128, 4 * PP], F32))
        C = e(nc.sbuf_tensor("C", [128, TF], F32))
        CB = e(nc.sbuf_tensor("CB", [128, 2 * 2048], BF16))
        H2B = e(nc.sbuf_tensor("H2B", [128, 2 * 2048], BF16))
        THF = e(nc.sbuf_tensor("THF", [128, 2 * PP], F32))
        THI = e(nc.sbuf_tensor("THI", [128, 2 * PP], F32))
        THG = e(nc.sbuf_tensor("THG", [128, 2 * PP], F32))
        U1s = e(nc.sbuf_tensor("U1s", [128, 2 * PP], F32))
        U2s = e(nc.sbuf_tensor("U2s", [128, 2 * PP], F32))
        Z3 = THF.ap()
        Z3B = U2s.ap()
        CONSTS = e(nc.sbuf_tensor("CONSTS", [128, 1], F32))

        ps_a = e(nc.psum_tensor("ps_a", [128, GPP], F32))
        ps_b = e(nc.psum_tensor("ps_b", [128, GPP], F32))
        ps_c = e(nc.psum_tensor("ps_c", [128, GPP], F32))
        ps_d = e(nc.psum_tensor("ps_d", [128, GPP], F32))
        ps_mod = e(nc.psum_tensor("ps_mod", [128, max(MODW, 3 * PP)], F32))
        ps_g = e(nc.psum_tensor("ps_g", [128, 3 * PP], F32))

        ldm = e(nc.semaphore("ldm"))
        ldw0 = e(nc.semaphore("ldw0"))
        ldw1 = e(nc.semaphore("ldw1"))
        ldv0 = e(nc.semaphore("ldv0"))
        ldv1 = e(nc.semaphore("ldv1"))
        ldx = e(nc.semaphore("ldx"))
        ldx2 = [e(nc.semaphore("ldx2_0")), e(nc.semaphore("ldx2_1"))]
        ldr = [e(nc.semaphore("ldr0")), e(nc.semaphore("ldr1"))]
        sqs = e(nc.semaphore("sqs"))
        sqr = e(nc.semaphore("sqr"))
        sqo = [e(nc.semaphore("sqo0")), e(nc.semaphore("sqo1"))]
        gqc = [e(nc.semaphore("gqc0")), e(nc.semaphore("gqc1"))]
        gqa = [e(nc.semaphore("gqa0")), e(nc.semaphore("gqa1"))]
        ccs = e(nc.semaphore("ccs"))
        s_te = e(nc.semaphore("s_te"))
        s_ve = e(nc.semaphore("s_ve"))
        s_sc = e(nc.semaphore("s_sc"))
        s_gp = e(nc.semaphore("s_gp"))

        MODL_BUF = VT
        P = Prog()
        P.esem_by_eng = {"tensor": s_te, "vector": s_ve, "scalar": s_sc, "gpsimd": s_gp}
        P.set_esems({"te": s_te, "ve": s_ve, "sc": s_sc, "gp": s_gp,
                     "ldm": ldm, "ldw0": ldw0, "ldw1": ldw1, "ldv0": ldv0, "ldv1": ldv1,
                     "ldx": ldx, "ldx20": ldx2[0], "ldx21": ldx2[1],
                     "ldr0": ldr[0], "ldr1": ldr[1], "sqs": sqs, "sqr": sqr,
                     "sqo0": sqo[0], "sqo1": sqo[1], "gqc0": gqc[0], "gqc1": gqc[1],
                     "gqa0": gqa[0], "gqa1": gqa[1], "ccs": ccs})

        VORD = ["ba", "bx", "bf", "bi", "bc", "bo", "inc", "lam"]
        vcol = {n: 8 * i for i, n in enumerate(VORD)}
        dcol = {"bah": 0, "bxh": 8, "bfh": 16, "bih": 24, "sh": 32, "s256": 40}

        def vc(name, c):
            return VEC[:, vcol[name] + c: vcol[name] + c + 1]

        def dc(name, c):
            return DVEC[:, dcol[name] + c: dcol[name] + c + 1]

        def modcol(g, c, t):
            return MOD[:, g * NCH * T + c * T + t: g * NCH * T + c * T + t + 1]

        def modcol_bc(g, c, t, n):
            off = g * NCH * T + c * T + t
            return sap(MOD, off, [[MODW, 128], [0, n]])

        def gsl(ap, c, t2=None):
            if t2 is None:
                return ap[:, c * GPP:(c + 1) * GPP]
            return ap[:, c * GPP + t2 * PP: c * GPP + (t2 + 1) * PP]

        def b(f):
            return f

        P.emit("sync", lambda s: s.dma_start(out=VEC[:, :], in_=vecsd[:, :]), inc=ldm, inc_by=16)
        P.emit("sync", lambda s: s.dma_start(out=SCT[:, :], in_=sctd[:, :]), inc=ldm, inc_by=16)
        ld_msk = P.emit("sync", lambda s: s.dma_start(out=MASK[:, :], in_=masksd[:, :]), inc=ldm, inc_by=16)
        if dbg:
            P.emit("sync", lambda s: s.dma_start(out=dbg_dv[:, 112:128], in_=masksd[:, 0:16]), inc=ldm, inc_by=16)

        gp_ones = P.emit("gpsimd", lambda g: (g.memset(ONESB[:, :], 1.0),
                                              g.memset(ONESF[:, :], 1.0),
                                              g.memset(ONES5[:, :], 1.0),
                                              g.memset(CONSTS[:, :], 0.25),
                                              g.memset(ROWSTAGE[:, :], 0.0))[-1],
                         inc=s_gp)

        def deriv_a(s):
            s.mul(DVEC[:, 0:8], VEC[:, vcol["ba"]:vcol["ba"] + 8], 0.5)
            s.mul(DVEC[:, 8:16], VEC[:, vcol["bx"]:vcol["bx"] + 8], 0.5)
            s.mul(DVEC[:, 16:24], VEC[:, vcol["bf"]:vcol["bf"] + 8], 0.5)
            s.mul(DVEC[:, 24:32], VEC[:, vcol["bi"]:vcol["bi"] + 8], 0.5)
            return s.activation(VT[:, 0:8], VEC[:, vcol["lam"]:vcol["lam"] + 8], AF.Exp, scale=-1.0)
        d1v = P.emit("scalar", deriv_a, waits=[(ldm, ld_msk)], inc=s_sc)
        d2v = P.emit("scalar", lambda s: s.activation(VT[:, 8:16], VT[:, 0:8], AF.Ln, bias=1.0),
                     waits=[(s_sc, d1v)], inc=s_sc)
        d3v = P.emit("scalar", lambda s: s.mul(DVEC[:, 32:40], VT[:, 8:16], -4.0),
                     waits=[(s_sc, d2v)], inc=s_sc)
        d4v = P.emit("scalar", lambda s: s.mul(DVEC[:, 40:48], DVEC[:, 32:40], float(PP)),
                     waits=[(s_sc, d3v)], inc=s_sc)
        sc_deriv = P.emit("scalar", lambda s: s.activation(SCT2[:, :], SCT[:, :], AF.Silu),
                          waits=[(s_sc, d4v)], inc=s_sc)

        JCL = JC // NCORES
        MODL = JCL * T
        halfW = JCL * 128
        stage_slots = [Bx, Bh1]
        band_ld = []
        mm_hist = []
        acc_hist = []
        for k in range(NCH):
            s0 = stage_slots[k % 2]
            w = [(s_te, mm_hist[k - 2] if k >= 2 else None)]
            lw = ldw0 if k % 2 == 0 else ldw1
            a1 = P.emit("sync", (lambda s, k=k, s0=s0: s.dma_start(out=s0[:, 0:halfW], in_=wmodd[k * 128:(k + 1) * 128, :])),
                        waits=w, inc=lw, inc_by=16)

            def mmb(t_, k=k, s0=s0):
                last = None
                for jc in range(JCL):
                    last = t_.matmul(ps_mod[:, jc * T:(jc + 1) * T],
                                     s0[:, jc * 128:(jc + 1) * 128],
                                     SCT2[:, k * T:(k + 1) * T],
                                     start=True, stop=True)
                return last
            w_mm = [(lw, a1), (s_sc, sc_deriv)]
            if k > 0:
                w_mm.append((s_ve, acc_hist[k - 1]))
            mm_hist.append(P.emit("tensor", mmb, waits=w_mm, inc=s_te))

            def accb(v, k=k):
                sv = Y[:, 0:MODL]
                if k == 0:
                    return v.tensor_copy(sv, ps_mod[:, 0:MODL])
                src_, dst_ = (sv, MODL_BUF[:, 0:MODL]) if k % 2 == 1 else (MODL_BUF[:, 0:MODL], sv)
                return v.tensor_tensor(dst_, src_, ps_mod[:, 0:MODL], OP.add)
            acc_hist.append(P.emit("vector", accb, waits=[(s_te, mm_hist[k])], inc=s_ve))
        mm_after = mm_hist[-1]

        md1 = P.emit("gpsimd", lambda g_: g_.dma_start(out=modind[:, :], in_=MODL_BUF[:, 0:MODL]),
                     waits=[(s_ve, acc_hist[-1])], inc=gqc[0], inc_by=16)
        mag = P.emit("gpsimd", lambda g_: g_.collective_compute(
            "AllGather", OP.bypass, replica_groups=[list(range(NCORES))],
            ins=[modind.ap().opt()], outs=[modoutd.ap().opt()]),
            waits=[(gqc[0], md1)], inc=ccs)
        md2 = P.emit("gpsimd", lambda g_: g_.dma_start(
            out=MOD[:, :],
            in_=sap(modoutd, 0, [[MODL, 128], [128 * MODL, NCORES], [1, MODL]])),
            waits=[(ccs, mag)], inc=gqc[0], inc_by=16)
        acc_hist.append(P.emit("vector", lambda v: v.engine_nop(), waits=[(gqc[0], md2)], inc=s_ve))

        NT = NCH * T
        e1 = P.emit("vector", lambda v: (v.tensor_scalar_add(VT[:, 0:NT], MOD[:, 1 * NT:2 * NT], 1.0),
                                         v.tensor_scalar_add(VTB[:, 0:NT], MOD[:, 4 * NT:5 * NT], 1.0))[-1],
                    waits=[(s_ve, acc_hist[-1])], inc=s_ve)
        ve_add1 = P.emit("vector", lambda v: (v.tensor_copy(MOD[:, 1 * NT:2 * NT], VT[:, 0:NT]),
                                              v.tensor_copy(MOD[:, 4 * NT:5 * NT], VTB[:, 0:NT]))[-1],
                         waits=[(s_ve, e1)], inc=s_ve)
        ve_modcl = ve_add1
        for gsec in (2, 5):
            lo = gsec * NT
            sA = P.emit("scalar", (lambda s, lo=lo: (
                s.sign(VT[:, 0:NT], MOD[:, lo:lo + NT]),
                s.activation(VTB[:, 0:NT], MOD[:, lo:lo + NT], AF.Abs))[-1]),
                waits=[(s_ve, ve_modcl)], inc=s_sc)
            eM = P.emit("vector", lambda v: v.tensor_scalar_max(SCRA[:, 0:NT], VTB[:, 0:NT], 0.25),
                        waits=[(s_sc, sA), (s_ve, ve_modcl)], inc=s_ve)
            ve_modcl = P.emit("vector", (lambda v, lo=lo:
                              v.tensor_tensor(MOD[:, lo:lo + NT], VT[:, 0:NT], SCRA[:, 0:NT], OP.mult)),
                              waits=[(s_ve, eM)], inc=s_ve)
        if dbg:
            ve_modcl = P.emit("vector", lambda v: v.engine_nop(), waits=[(s_ve, ve_modcl)], inc=s_ve)
            dmod_e = P.emit("sync", lambda s: s.dma_start(out=dbg_mod[:, :], in_=MOD[:, :]),
                            waits=[(s_ve, ve_modcl)], inc=ldm, inc_by=16)
            ve_modcl = P.emit("vector", lambda v: v.engine_nop(), waits=[(ldm, dmod_e)], inc=s_ve)
        mod_ready_v = ve_modcl

        def load_weight(slot_idx, dram, waits_extra=(), stages=None, cast_extra=(),
                        cast_gp=False):
            if stages is None:
                stages = [("w0", WSTf[:, 0:1024], ldv0), ("w1", WSTf[:, 1024:2048], ldv1)]
            done = None
            for k in range(NCH):
                key, st, lv = stages[k % len(stages)]
                w = []
                if key in load_weight.hist:
                    w.append(load_weight.hist[key])
                w += list(load_weight.first.pop(key, ()))
                if k < 2:
                    w += list(waits_extra)
                a = P.emit("sync", (lambda s, k=k, st=st, dram=dram:
                                    s.dma_start(out=st, in_=dram[k * 128:(k + 1) * 128, :])),
                           waits=w, inc=lv, inc_by=16)
                wc = [(lv, a)]
                if k == 0:
                    wc.extend(cast_extra)
                if cast_gp and k % 2 == 1:
                    P.emit("gpsimd", (lambda g_, k=k, st=st, si=slot_idx:
                                      g_.tensor_tensor(wslot[si][:, k * 1024:k * 1024 + 512],
                                                       st[:, 0:512], ONES5[:, 0:512], OP.mult)),
                           waits=wc, inc=s_gp)
                    v = P.emit("gpsimd", (lambda g_, k=k, st=st, si=slot_idx:
                                          g_.tensor_tensor(wslot[si][:, k * 1024 + 512:(k + 1) * 1024],
                                                           st[:, 512:1024], ONES5[:, 0:512], OP.mult)),
                               inc=s_gp)
                    done = (s_gp, v)
                else:
                    v = P.emit("scalar", (lambda s, k=k, st=st, si=slot_idx:
                                          s.copy(wslot[si][:, k * 1024:(k + 1) * 1024], st)),
                               waits=wc, inc=s_sc)
                    done = (s_sc, v)
                load_weight.hist[key] = done
            return done
        load_weight.hist = {}
        load_weight.first = {}


        CBf = CB.ap().bitcast(F32)
        NGq = NG

        def bx_pieces(g):
            if g % 2 == 0:
                return [(Bx, 0, 8)]
            return [(C.ap(), 0, 4), (CBf, 4, 4)]

        def bxsl(g, c, t2=None):
            for ap, base, n in bx_pieces(g):
                if base <= c < base + n:
                    off = (c - base) * GPP
                    if t2 is None:
                        return ap[:, off: off + GPP]
                    return ap[:, off + t2 * PP: off + (t2 + 1) * PP]

        def bxhalf(g, h):
            for ap, base, n in bx_pieces(g):
                if base <= 4 * h < base + n:
                    off = (4 * h - base) * GPP
                    return ap[:, off: off + 4 * GPP]

        def hsl(c, t2=None):
            if t2 is None:
                return XBF[:, c * GPP:(c + 1) * GPP]
            return XBF[:, c * GPP + t2 * PP: c * GPP + (t2 + 1) * PP]

        def lanex(g):
            return ldx if g % 2 == 0 else ldx2[1]

        x_ld = [None] * NGq
        x_ld_half = [None] * NGq
        pcast = [None] * NGq
        ln2cast_h = [None] * NGq
        spill_done = [None] * NGq
        ln2mm = [None] * NGq
        ln2ev = [None] * NGq
        ln2sc = [None] * NGq
        ln2cast = [None] * NGq
        rows_done = [None] * NGq
        stats_of = [None] * NGq
        var_of = [None] * NGq
        rstd_of = [None] * NGq
        last_rescan = [None, None]
        prev_x2 = None
        prev_b = None
        b_hi_ve = None
        prev_u_last = None
        prev_carry_dma = [None, None]
        prev_agb_cmb = [None, None]
        prev_zx_te = None
        prev_zx_thx_last = None
        prev_extract_gp = None
        prev_dbgb = None
        prev_dbgh1 = None
        sc_sr2 = None
        sc_atot = None
        ve_h1 = None

        def emit_xload(g):
            halves = []
            for h in range(2):
                w = []
                if g == 0 and h == 0:
                    w.append((s_te, mm_after))
                if g >= 2:
                    w.append((s_gp, ln2cast_h[g - 2][h]))
                    if h == 0:
                        w.append((sqs, spill_done[g - 2]))
                v = P.emit("sync", (lambda s, g=g, h=h:
                                    s.dma_start(out=bxhalf(g, h),
                                                in_=xh[g][:, h * 4 * GPP:(h + 1) * 4 * GPP])),
                           waits=w, inc=lanex(g), inc_by=16)
                halves.append(v)
            x_ld_half[g] = halves
            x_ld[g] = halves[-1]

        def emit_prelude_casts(g):
            w_free = []
            if g == 1:
                w_free.append((s_te, stats_of[0]))
            elif g >= 2:
                w_free.append((s_te, ln2mm[g - 2]))
            v = None
            for h in range(2):
                P.emit("scalar", (lambda s, g2=g, h=h:
                                  s.copy(WST[:, h * 4 * GPP:(h + 1) * 4 * GPP], bxhalf(g2, h))),
                       waits=[(lanex(g), x_ld_half[g][h])] + (w_free if h == 0 else []), inc=s_sc)
                v = P.emit("scalar", (lambda s, g2=g, h=h:
                                      s.activation(H2B[:, h * 4 * GPP:(h + 1) * 4 * GPP],
                                                   bxhalf(g2, h), AF.Square)),
                           inc=s_sc)
            pcast[g] = v

        def emit_prelude_stats(g):
            def stats_mm(t_):
                last = None
                for c in range(NCH):
                    t_.matmul(ps_c[:, :], ONESB[:, :], WST[:, c * GPP:(c + 1) * GPP],
                              start=(c == 0), stop=(c == NCH - 1))
                    last = t_.matmul(ps_d[:, :], ONESB[:, :], H2B[:, c * GPP:(c + 1) * GPP],
                                     start=(c == 0), stop=(c == NCH - 1))
                return last
            w_st = [(s_sc, pcast[g]), (s_gp, gp_ones)]
            if g > 0:
                w_st.append((s_ve, var_of[g - 1]))
            stats_of[g] = P.emit("tensor", stats_mm, waits=w_st, inc=s_te)

            sc_mb = P.emit("scalar", lambda s: s.activation(MB[:, :], ps_c[:, :], AF.Copy, scale=1.0 / D),
                           waits=[(s_te, stats_of[g])], inc=s_sc)
            v1 = P.emit("vector", lambda v: v.tensor_tensor(VT[:, :], MB[:, :], MB[:, :], OP.mult),
                        waits=[(s_sc, sc_mb)], inc=s_ve)
            v2 = P.emit("vector", lambda v: v.tensor_scalar_sub(VTB[:, :], VT[:, :], 1e-5),
                        waits=[(s_ve, v1)], inc=s_ve)
            var_of[g] = P.emit("vector", lambda v: v.scalar_tensor_tensor(VAR[:, :], ps_d[:, :], 1.0 / D,
                                                                          VTB[:, :], OP.mult, OP.subtract),
                               waits=[(s_ve, v2)], inc=s_ve)

        def emit_nlw2(g):
            nv = P.emit("scalar", lambda s: s.activation(VT[:, :], VAR[:, :], AF.Ln),
                        waits=[(s_ve, var_of[g])], inc=s_sc)
            rstd_of[g] = P.emit("scalar", lambda s: s.activation(RSTD[:, :], VT[:, :], AF.Exp, scale=-0.5),
                                waits=[(s_sc, nv)], inc=s_sc)

        def emit_lnblock(g):
            nonlocal ve_h1
            lt1 = None
            for ap, base, n in bx_pieces(g):
                def ln_t1(v, ap=ap, base=base, n=n):
                    bx3 = ap[:, 0:n * GPP].rearrange("p (c q) -> p c q", c=n)
                    bh3 = XBF[:, base * GPP:(base + n) * GPP].rearrange("p (c q) -> p c q", c=n)
                    mb_bc = sap(MB, 0, [[GPP, 128], [0, n], [1, GPP]])
                    return v.tensor_tensor(bh3, bx3, mb_bc, OP.subtract)
                w_t1 = []
                if base == 0:
                    w_t1 = [(s_sc, rstd_of[g]), (lanex(g), x_ld[g])]
                    if dbg and prev_dbgh1 is not None:
                        w_t1.append((ldm, prev_dbgh1))
                lt1 = P.emit("vector", ln_t1, waits=w_t1, inc=s_ve)

            def ln_t2(v):
                bh3 = XBF[:, 0:GF].rearrange("p (c q) -> p c q", c=NCH)
                rs_bc = sap(RSTD, 0, [[GPP, 128], [0, NCH], [1, GPP]])
                return v.tensor_tensor(bh3, bh3, rs_bc, OP.mult)
            lt2 = P.emit("vector", ln_t2, waits=[(s_ve, lt1)], inc=s_ve)

            def h1_build(v, g=g):
                last = None
                for c in range(NCH):
                    for t2 in range(GT):
                        t = GT * g + t2
                        last = v.scalar_tensor_tensor(
                            hsl(c, t2), hsl(c, t2),
                            modcol(1, c, t), modcol_bc(0, c, t, PP),
                            OP.mult, OP.add)
                return last
            ve_h1 = P.emit("vector", h1_build, waits=[(s_ve, mod_ready_v), (s_ve, lt2)], inc=s_ve)

        def emit_tail_scanA(g):
            nonlocal prev_extract_gp
            hbuf = [Y, Baa]
            ag_done = {}
            agv_of = {}
            for t2 in range(GT):
                t = GT * g + t2
                par = t % 2
                hb = hbuf[t2]

                def scan1(v, t2=t2, hb=hb):
                    last = None
                    for c in range(NCH):
                        last = v.tensor_tensor_scan(hb[:, c * PP:(c + 1) * PP],
                                                    gsl(Ba, c, t2), gsl(Bh1, c, t2),
                                                    0.0, OP.mult, OP.add)
                    return last
                ve_s1 = P.emit("vector", scan1, waits=[(s_gp, prev_b), (s_ve, b_hi_ve)], inc=s_ve)

                def extract(g_, t2=t2, par=par, hb=hb):
                    for c in range(NCH):
                        g_.tensor_copy(CARRY[:, par * 16 + c: par * 16 + c + 1],
                                       hb[:, c * PP + PP - 1: c * PP + PP])
                    return g_.tensor_copy(sap(CARRY, par * 16 + 8, [[32, 128], [1, 8]]),
                                          sap(ATOT, t2, [[16, 128], [GT, 8]]))
                w_ex = [(s_ve, ve_s1), (s_sc, sc_atot)]
                if prev_carry_dma[par] is not None:
                    w_ex.append((gqc[par], prev_carry_dma[par]))
                gp_ex = P.emit("gpsimd", extract, waits=w_ex, inc=s_gp)
                if t2 == GT - 1:
                    prev_extract_gp = gp_ex

                d1 = P.emit("gpsimd", (lambda g_, t=t, par=par:
                                       g_.dma_start(out=cind[t], in_=CARRY[:, par * 16:(par + 1) * 16])),
                            waits=[(s_gp, gp_ex)], inc=gqc[par], inc_by=16)

                agv_of[t2] = P.emit("gpsimd", (lambda g_, t=t: g_.collective_compute(
                    "AllGather", OP.bypass,
                    replica_groups=[list(range(NCORES))],
                    ins=[cind[t].opt()],
                    outs=[coutd[t].opt()])),
                    waits=[(gqc[par], d1)], inc=ccs)
                prev_carry_dma[par] = d1

            for t2 in range(GT):
                t = GT * g + t2
                par = t % 2
                w_ab = [(ccs, agv_of[t2])]
                if prev_agb_cmb[par] is not None:
                    w_ab.append((s_ve, prev_agb_cmb[par]))
                d2 = P.emit("gpsimd", (lambda g_, t=t, par=par:
                                       g_.dma_start(out=AGB[:, par * 128:(par + 1) * 128],
                                                    in_=sap(coutd, t * 8 * 128 * 16, [[16, 128], [2048, 8], [1, 16]]))),
                            waits=w_ab, inc=gqa[par], inc_by=16)
                ag_done[t2] = d2
            return hbuf, ag_done

        def emit_tail_scanB(g, hbuf, ag_done):
            nonlocal prev_x2
            for t2 in range(GT):
                t = GT * g + t2
                par = t % 2
                hb = hbuf[t2]

                def cmb_a(v, par=par):
                    at_ap = sap(AGB, par * 128 + 8, [[256, 128], [1, 8], [16, 8]])
                    he_ap = sap(AGB, par * 128 + 0, [[256, 128], [1, 8], [16, 8]])
                    am = MASK[:, 0:64].rearrange("p (c j) -> p c j", c=8)
                    c1 = CMB1[:, :].rearrange("p (c j) -> p c j", c=8)
                    c2 = CMB2[:, :].rearrange("p (c j) -> p c j", c=8)
                    v.tensor_tensor(c2, he_ap, am, OP.mult)
                    return v.tensor_tensor(c1, at_ap, am, OP.mult)

                def cmb_b(v):
                    c1 = CMB1[:, :].rearrange("p (c j) -> p c j", c=8)
                    bm = MASK[:, 64:128].rearrange("p (c j) -> p c j", c=8)
                    return v.tensor_tensor(c1, c1, bm, OP.add)

                def cmb_c(v):
                    last = None
                    for c in range(NCH):
                        last = v.tensor_tensor_scan(CMB3[:, c * 8:(c + 1) * 8],
                                                    CMB1[:, c * 8:(c + 1) * 8],
                                                    CMB2[:, c * 8:(c + 1) * 8],
                                                    0.0, OP.mult, OP.add)
                    return last
                w_cmb = [(gqa[par], ag_done[t2])]
                if t < 2:
                    w_cmb.append((ldm, ld_msk))
                ca_ = P.emit("vector", cmb_a, waits=w_cmb, inc=s_ve)
                cb_ = P.emit("vector", cmb_b, waits=[(s_ve, ca_)], inc=s_ve)
                cc_ = P.emit("vector", cmb_c, waits=[(s_ve, cb_)], inc=s_ve)
                cmb = P.emit("vector", (lambda v, par=par:
                                        v.tensor_copy(HIN[:, par * 8:(par + 1) * 8],
                                                      sap(CMB3, 7, [[64, 128], [8, 8]]))),
                             waits=[(s_ve, cc_)], inc=s_ve)
                prev_agb_cmb[par] = cmb

                def rescan(v, t2=t2, par=par, hb=hb):
                    last = None
                    for c in range(NCH):
                        last = v.tensor_tensor_scan(hb[:, c * PP:(c + 1) * PP],
                                                    gsl(Ba, c, t2), gsl(Bh1, c, t2),
                                                    HIN[:, par * 8 + c: par * 8 + c + 1],
                                                    OP.mult, OP.add)
                    return last
                ve_rs = P.emit("vector", rescan, waits=[(s_ve, cmb)], inc=s_ve)
                last_rescan[t2] = ve_rs

                def x2b(v, t2=t2, g=g, hb=hb):
                    t = GT * g + t2
                    last = None
                    for c in range(NCH):
                        last = v.scalar_tensor_tensor(bxsl(g, c, t2), hb[:, c * PP:(c + 1) * PP],
                                                      modcol(2, c, t), bxsl(g, c, t2),
                                                      OP.mult, OP.add)
                    return last
                prev_x2 = P.emit("vector", x2b, waits=[(s_ve, ve_rs)], inc=s_ve)

                sp = None
                for pi, (ap, base, n) in enumerate(bx_pieces(g)):
                    w_sp = []
                    if pi == 0:
                        w_sp.append((s_ve, prev_x2))
                        if t2 == 0 and g > 0:
                            w_sp.append((sqs, spill_done[g - 1]))
                    def spl(s, ap=ap, base=base, n=n, t2=t2, t=t):
                        src = ap[:, 0:n * GPP].rearrange("p (c u q) -> p c u q", c=n, u=GT)[:, :, t2, :]
                        return s.dma_start(out=x2d[t][:, base * PP:(base + n) * PP], in_=src)
                    sp = P.emit("sync", spl, waits=w_sp, inc=sqs, inc_by=16)
                if t2 == GT - 1:
                    spill_done[g] = sp

        def emit_main(g):
            nonlocal prev_b, b_hi_ve, prev_u_last, prev_zx_te, prev_zx_thx_last
            nonlocal sc_sr2, sc_atot, prev_dbgb, prev_dbgh1
            if dbg:
                prev_dbgh1 = P.emit("sync", (lambda s, g=g: s.dma_start(out=dbg_h1[g], in_=XBF[:, 0:GF])),
                                    waits=[(s_ve, ve_h1)], inc=ldm, inc_by=16)
            w_h1b = [(s_ve, ve_h1)]
            if prev_zx_te is not None:
                w_h1b.append((s_te, prev_zx_te))
            gp_h1b = P.emit("scalar", lambda s: s.copy(Bh1b[:, 0:GF], XBF[:, 0:GF]),
                            waits=w_h1b, inc=s_sc)

            th_done = {}
            red_done = {}
            a_done = {}
            aa_done = {}
            z_te = {}
            for c in range(NCH):
                psz = ps_a if c % 2 == 0 else ps_b

                def zmm(t_, c=c, psz=psz):
                    last = None
                    for k in range(NCH):
                        last = t_.matmul(psz[:, :],
                                         wslot[0][:, k * 1024 + c * 128: k * 1024 + (c + 1) * 128],
                                         Bh1b[:, k * GPP:(k + 1) * GPP],
                                         start=(k == 0), stop=(k == NCH - 1))
                    return last
                w_z = [(s_sc, gp_h1b)] if c == 0 else []
                if c == 0:
                    w_z.append(wa_done)
                    if g > 0:
                        w_z.append((s_sc, prev_zx_thx_last))
                if c >= 2:
                    w_z.append((s_sc, th_done[c - 2]))
                z_te[c] = P.emit("tensor", zmm, waits=w_z, inc=s_te)

                def tha(s, c=c, psz=psz):
                    return s.activation(SCRA[:, (c % 2) * GPP:(c % 2 + 1) * GPP], psz[:, :],
                                        AF.Tanh, scale=0.5, bias=dc("bah", c))
                w_th = [(s_te, z_te[c])]
                if c >= 2:
                    w_th.append((s_ve, red_done[c - 2]))
                    w_th.append((s_sc, a_done[c - 2]))
                th_done[c] = P.emit("scalar", tha, waits=w_th, inc=s_sc)

                w_a = [(s_sc, th_done[c])]
                if c == 0:
                    w_a.append((s_ve, last_rescan[1]))
                a_done[c] = P.emit("scalar",
                                   (lambda s, c=c: s.activation(gsl(Ba, c), SCRA[:, (c % 2) * GPP:(c % 2 + 1) * GPP],
                                                                AF.Exp, scale=dc("sh", c), bias=dc("sh", c))),
                                   waits=w_a, inc=s_sc)

                def red(v, c=c):
                    src = SCRA[:, (c % 2) * GPP:(c % 2 + 1) * GPP].rearrange("p (t q) -> p t q", t=GT)
                    return v.tensor_reduce(THS[:, c * GT:(c + 1) * GT], src, AX, OP.add)
                red_done[c] = P.emit("vector", red, waits=[(s_sc, th_done[c])], inc=s_ve)

                w_aa = [(s_sc, a_done[c])]
                if c == 0 and prev_x2 is not None:
                    w_aa.append((s_ve, prev_x2))
                aa_done[c] = P.emit("gpsimd",
                                    (lambda g_, c=c: g_.tensor_tensor(gsl(Baa, c), gsl(Ba, c), gsl(Ba, c), OP.mult)),
                                    waits=w_aa, inc=s_gp)

            u_done = {}
            thx_done = {}
            for c in range(NCH):
                psz = ps_a if c % 2 == 0 else ps_b

                def zmm(t_, c=c, psz=psz):
                    last = None
                    for k in range(NCH):
                        last = t_.matmul(psz[:, :],
                                         wslot[1][:, k * 1024 + c * 128: k * 1024 + (c + 1) * 128],
                                         Bh1b[:, k * GPP:(k + 1) * GPP],
                                         start=(k == 0), stop=(k == NCH - 1))
                    return last
                w_z = [(s_sc, th_done[NCH - 2 + (c % 2)] if c < 2 else thx_done[c - 2])]
                if c == 0:
                    w_z.append(wx_done)
                zx_te = P.emit("tensor", zmm, waits=w_z, inc=s_te)
                if c == NCH - 1:
                    prev_zx_te = zx_te

                def thx(s, c=c, psz=psz):
                    return s.activation(SCRX[:, (c % 2) * GPP:(c % 2 + 1) * GPP], psz[:, :],
                                        AF.Tanh, scale=0.5, bias=dc("bxh", c))
                w_th = [(s_te, zx_te)]
                if c >= 2:
                    w_th.append((s_ve, u_done[c - 2]))
                thx_done[c] = P.emit("scalar", thx, waits=w_th, inc=s_sc)

                def ub(v, c=c):
                    return v.scalar_tensor_tensor(gsl(Bh1, c), SCRX[:, (c % 2) * GPP:(c % 2 + 1) * GPP],
                                                  1.0, hsl(c), OP.add, OP.mult)
                u_done[c] = P.emit("vector", ub, waits=[(s_sc, thx_done[c])], inc=s_ve)
            prev_zx_thx_last = thx_done[NCH - 1]
            prev_u_last = u_done[NCH - 1]

            def atot(s, g=g):
                last = None
                for c in range(NCH):
                    last = s.activation(ATOT[:, c * GT:(c + 1) * GT], THS[:, c * GT:(c + 1) * GT],
                                        AF.Exp, scale=dc("sh", c), bias=dc("s256", c))
                return last
            w_at = [(s_ve, red_done[NCH - 1])]
            if prev_extract_gp is not None:
                w_at.append((s_gp, prev_extract_gp))
            sc_atot = P.emit("scalar", atot, waits=w_at, inc=s_sc)

            H2 = GF // 2
            w_nl4a = [(s_gp, aa_done[NCH // 2 - 1]), (s_ve, u_done[NCH // 2 - 1]),
                      (s_gp, gp_ones)]
            if dbg:
                w_nl4a.append((ldm, prev_dbgh1))
            l1a = P.emit("scalar",
                         lambda s: s.activation(XBF[:, 0:H2], Baa[:, 0:H2], AF.Ln, scale=-0.25, bias=CONSTS[:, 0:1]),
                         waits=w_nl4a, inc=s_sc)
            l1b = P.emit("scalar",
                         lambda s: s.activation(XBF[:, H2:GF], Baa[:, H2:GF], AF.Ln, scale=-0.25, bias=CONSTS[:, 0:1]),
                         waits=[(s_gp, aa_done[NCH - 1]), (s_ve, u_done[NCH - 1])], inc=s_sc)
            P.emit("scalar",
                   lambda s: s.activation(Baa[:, 0:H2], XBF[:, 0:H2], AF.Exp, scale=0.5),
                   waits=[(s_sc, l1a)], inc=s_sc)
            sc_sr2 = P.emit("scalar",
                            lambda s: s.activation(Baa[:, H2:GF], XBF[:, H2:GF], AF.Exp, scale=0.5),
                            waits=[(s_sc, l1b)], inc=s_sc)

            if g >= 2:
                w_n1 = [(s_ve, ln2ev[g - 2])]
                if g >= 3 and rows_done[g - 3] is not None:
                    w_n1.append((sqr, rows_done[g - 3]))
                nb1 = P.emit("scalar", lambda s: s.activation(VTB[:, :], VAR2[:, :], AF.Ln),
                             waits=w_n1, inc=s_sc)
                nb2 = P.emit("scalar", lambda s: s.activation(RSTD2[:, :], VTB[:, :], AF.Exp, scale=-0.5),
                             waits=[(s_sc, nb1)], inc=s_sc)
                P.emit(SCQ, (lambda s, g=g: s.dma_start(out=rowsd[g - 2, 0], in_=MB2[0:1, :])),
                       waits=[(s_sc, nb2)], inc=sqr, inc_by=16)
                rows_done[g - 2] = P.emit(SCQ, (lambda s, g=g: s.dma_start(out=rowsd[g - 2, 1], in_=RSTD2[0:1, :])),
                                          inc=sqr, inc_by=16)

            b_hi_ve = P.emit("vector",
                             lambda v: v.tensor_tensor(Bh1[:, 0:GF], Bh1[:, 0:GF], Baa[:, 0:GF], OP.mult),
                             waits=[(s_sc, sc_sr2), (s_ve, u_done[NCH - 1])], inc=s_ve)
            prev_b = None

            if dbg:
                P.emit("sync", (lambda s, g=g: s.dma_start(out=dbg_a[g], in_=Ba[:, 0:GF])),
                       waits=[(s_ve, b_hi_ve)], inc=ldm, inc_by=16)
                prev_dbgb = P.emit("sync", (lambda s, g=g: s.dma_start(out=dbg_b[g], in_=Bh1[:, 0:GF])),
                                   inc=ldm, inc_by=16)
                P.emit("vector", lambda v: v.engine_nop(), waits=[(ldm, prev_dbgb)])

        def emit_ln2_casts(g):
            w_c1 = [(s_ve, prev_x2)]
            if g + 1 < NGq:
                w_c1.append((s_te, stats_of[g + 1]))
            hv = []
            v = None
            for h in range(2):
                for c in range(4 * h, 4 * h + 4):
                    P.emit("gpsimd", (lambda g_, g2=g, c=c:
                                      g_.tensor_tensor(WST[:, c * GPP:(c + 1) * GPP], bxsl(g2, c),
                                                       ONES5[:, 0:GPP], OP.mult)),
                           waits=(w_c1 if (h == 0 and c == 0) else ()), inc=s_gp)
                v = P.emit("gpsimd", (lambda g_, g2=g, h=h:
                                      g_.tensor_tensor(H2B[:, h * 4 * GPP:(h + 1) * 4 * GPP],
                                                       bxhalf(g2, h), bxhalf(g2, h), OP.mult)),
                           inc=s_gp)
                hv.append(v)
            ln2cast_h[g] = hv
            ln2cast[g] = v

        def emit_ln2_mm(g):
            def stats2_mm(t_):
                last = None
                for c in range(NCH):
                    t_.matmul(ps_mod[:, 0:GPP], ONESB[:, :], WST[:, c * GPP:(c + 1) * GPP],
                              start=(c == 0), stop=(c == NCH - 1))
                    last = t_.matmul(ps_g[:, 0:GPP], ONESB[:, :], H2B[:, c * GPP:(c + 1) * GPP],
                                     start=(c == 0), stop=(c == NCH - 1))
                return last
            w_l2 = [(s_gp, ln2cast[g])]
            if g == 0:
                w_l2.append((s_ve, acc_hist[-1]))
            else:
                w_l2.append((s_ve, ln2ev[g - 1]))
                w_l2.append((s_sc, ln2sc[g - 1]))
            ln2mm[g] = P.emit("tensor", stats2_mm, waits=w_l2, inc=s_te)
            w_ls = [(s_te, ln2mm[g])]
            if P.val(sqr):
                w_ls.append((sqr, P.val(sqr)))
            ln2sc[g] = P.emit("scalar", lambda s: s.activation(MB2[:, :], ps_mod[:, 0:GPP], AF.Copy, scale=1.0 / D),
                              waits=w_ls, inc=s_sc)
            w1 = P.emit("vector", lambda v: v.tensor_tensor(VT[:, :], MB2[:, :], MB2[:, :], OP.mult),
                        waits=[(s_sc, ln2sc[g])], inc=s_ve)
            w2 = P.emit("vector", lambda v: v.tensor_scalar_sub(VTB[:, :], VT[:, :], 1e-5),
                        waits=[(s_ve, w1)], inc=s_ve)
            ln2ev[g] = P.emit("vector", lambda v: v.scalar_tensor_tensor(VAR2[:, :], ps_g[:, 0:GPP], 1.0 / D,
                                                                         VTB[:, :], OP.mult, OP.subtract),
                              waits=[(s_ve, w2)], inc=s_ve)

        emit_xload(0)
        emit_xload(1)
        wa_done = load_weight(0, wd["WaT"])
        wx_done = load_weight(1, wd["WxT"])
        tailst = None
        for g in range(NG):
            emit_prelude_casts(g)
            if g > 0:
                tailst = emit_tail_scanA(g - 1)
            emit_prelude_stats(g)
            emit_nlw2(g)
            emit_lnblock(g)
            if g > 0:
                emit_tail_scanB(g - 1, *tailst)
                emit_ln2_casts(g - 1)
            emit_main(g)
            if g > 0:
                emit_ln2_mm(g - 1)
            if 2 <= g + 1 < NG:
                emit_xload(g + 1)

        tailst = emit_tail_scanA(NG - 1)
        emit_tail_scanB(NG - 1, *tailst)
        emit_ln2_casts(NG - 1)

        def emit_rows_fin(gg):
            w_n1 = [(s_ve, ln2ev[gg])]
            if rows_done[gg - 1] is not None:
                w_n1.append((sqr, rows_done[gg - 1]))
            nf1 = P.emit("scalar", lambda s: s.activation(VTB[:, :], VAR2[:, :], AF.Ln),
                         waits=w_n1, inc=s_sc)
            sc_fin = P.emit("scalar", lambda s: s.activation(RSTD2[:, :], VTB[:, :], AF.Exp, scale=-0.5),
                            waits=[(s_sc, nf1)], inc=s_sc)
            P.emit(SCQ, (lambda s, gg=gg: s.dma_start(out=rowsd[gg, 0], in_=MB2[0:1, :])),
                   waits=[(s_sc, sc_fin)], inc=sqr, inc_by=16)
            rows_done[gg] = P.emit(SCQ, (lambda s, gg=gg: s.dma_start(out=rowsd[gg, 1], in_=RSTD2[0:1, :])),
                                   inc=sqr, inc_by=16)

        emit_rows_fin(NG - 2)
        emit_ln2_mm(NG - 1)
        emit_rows_fin(NG - 1)


        attn_done = [(s_ve, P.val(s_ve)), (s_gp, P.val(s_gp)),
                     (s_sc, P.val(s_sc)), (s_te, P.val(s_te))]
        attn_te_total = P.val(s_te)

        slot_free = {
            0: [(s_te, prev_zx_te)],
            1: [(s_te, prev_zx_te)],
            2: [(s_gp, ln2cast_h[NG - 2][1]), (sqs, spill_done[NG - 2])],
            3: [(s_ve, last_rescan[1])],
            4: [(s_ve, last_rescan[1])],
            5: [(s_ve, prev_x2)],
        }
        stage_free = [(s_ve, prev_u_last), (s_sc, sc_atot)]
        ffn_stages = [("sa", SCRA[:, 0:1024], ldv0), ("sx", SCRX[:, 0:1024], ldv1),
                      ("w0", WSTf[:, 0:1024], ldw0), ("w1", WSTf[:, 1024:2048], ldw1)]
        load_weight.first[("sa")] = list(stage_free)
        load_weight.first[("sx")] = list(stage_free)
        load_weight.first[("w0")] = [(s_te, ln2mm[NG - 1])]
        load_weight.first[("w1")] = [(s_te, ln2mm[NG - 1])]
        ffn_w = {}
        for i, n in enumerate(["wfT", "wiT", "wcT", "ufT", "uiT", "ucT"]):
            ffn_w[n] = load_weight(i, wd[n], stages=ffn_stages,
                                   cast_extra=slot_free[i], cast_gp=True)
        ffn_w_done = ffn_w["ucT"]

        w_c0 = list(attn_done) + [(sqs, P.val(sqs))]
        cz = P.emit("vector", lambda v: v.memset(Z3[:, 0:PP], 0.0), waits=w_c0, inc=s_ve)

        def c0_build(v):
            last = None
            for c in range(NCH):
                last = v.tensor_scalar_add(C[:, c * PP:(c + 1) * PP], Z3[:, 0:PP], vc("inc", c))
            return last
        c0b = P.emit("vector", c0_build, waits=[(s_ve, cz)], inc=s_ve)
        ve_c0 = P.emit("vector", lambda v: v.tensor_copy(CB[:, 0:TF], C[:, :]),
                       waits=[(s_ve, c0b)], inc=s_ve)

        sqs_total = P.val(sqs)
        sqr_total = P.val(sqr)
        x2_ld = [None, None]
        row_ld = [None, None]
        h2prep = [None, None]
        h2cast = [None, None]
        bcast_te = [None, None]
        gate_te_last = [None, None]
        ew_last = [None, None]
        prev_ew_all = ve_c0
        last_h2cast = None

        W3 = {0: "wfT", 1: "wiT", 2: "wcT"}
        U3 = {0: "ufT", 1: "uiT", 2: "ucT"}
        WIDX = {"wfT": 0, "wiT": 1, "wcT": 2, "ufT": 3, "uiT": 4, "ucT": 5}

        def prep_step(t):
            par = t % 2
            g = t // GT
            t2 = t % GT
            w_x = [(sqs, sqs_total)]
            if t < 2:
                w_x.append((s_te, attn_te_total))
            if h2prep[par] is not None:
                w_x.append((s_ve, h2prep[par]))
            x2_ld[par] = P.emit("sync", (lambda s, t=t, par=par:
                                         s.dma_start(out=XBF[:, par * 2048: par * 2048 + TF], in_=x2d[t])),
                                waits=w_x, inc=ldx2[par], inc_by=16)

            w_r = [(sqr, sqr_total)]
            if bcast_te[par] is not None:
                w_r.append((s_te, bcast_te[par]))
            P.emit("sync", (lambda s, g=g, t2=t2, par=par:
                            s.dma_start(out=ROWSTAGE[0:1, par * 2 * PP: par * 2 * PP + PP],
                                        in_=rowsd[g, 0, t2 * PP:(t2 + 1) * PP])),
                   waits=w_r, inc=ldr[par], inc_by=16)
            row_ld[par] = P.emit("sync", (lambda s, g=g, t2=t2, par=par:
                                          s.dma_start(out=ROWSTAGE[0:1, par * 2 * PP + PP: par * 2 * PP + 2 * PP],
                                                      in_=rowsd[g, 1, t2 * PP:(t2 + 1) * PP])),
                                 inc=ldr[par], inc_by=16)

            psbc = ps_c if par == 0 else ps_d
            w_bc = [(ldr[par], row_ld[par])]
            if h2prep[par] is not None:
                w_bc.append((s_ve, h2prep[par]))
            if t == 0:
                w_bc.append((s_te, P.val(s_te)))
                w_bc.append((s_ve, var_of[NG - 1]))
            bcast_te[par] = P.emit("tensor", (lambda t_, par=par, psbc=psbc:
                                              t_.matmul(psbc[:, 0:2 * PP], ONESF[:, :],
                                                        ROWSTAGE[:, par * 2 * PP:(par + 1) * 2 * PP],
                                                        start=True, stop=True)),
                                   waits=w_bc, inc=s_te)

            def h2p_a(v, par=par, psbc=psbc):
                xv = XBF[:, par * 2048: par * 2048 + TF].rearrange("p (c q) -> p c q", c=NCH)
                yv = Y[:, :].rearrange("p (c q) -> p c q", c=NCH)
                mb_bc = sap(psbc, 0, [[2 * PP, 128], [0, NCH], [1, PP]])
                return v.tensor_tensor(yv, xv, mb_bc, OP.subtract)

            def h2p_b(v, psbc=psbc):
                yv = Y[:, :].rearrange("p (c q) -> p c q", c=NCH)
                rs_bc = sap(psbc, PP, [[2 * PP, 128], [0, NCH], [1, PP]])
                return v.tensor_tensor(yv, yv, rs_bc, OP.mult)

            def h2p_c(v, t=t):
                last = None
                for c in range(NCH):
                    last = v.scalar_tensor_tensor(Y[:, c * PP:(c + 1) * PP], Y[:, c * PP:(c + 1) * PP],
                                                  modcol(4, c, t), modcol_bc(3, c, t, PP),
                                                  OP.mult, OP.add)
                return last
            w_h2 = [(s_te, bcast_te[par]), (ldx2[par], x2_ld[par])]
            if last_h2cast is not None:
                w_h2.append((s_sc, last_h2cast))
            if t == 0:
                w_h2.append((s_ve, ve_c0))
            hpa = P.emit("vector", h2p_a, waits=w_h2, inc=s_ve)
            hpb = P.emit("vector", h2p_b, waits=[(s_ve, hpa)], inc=s_ve)
            h2prep[par] = P.emit("vector", h2p_c, waits=[(s_ve, hpb)], inc=s_ve)

        def cast_step(t):
            nonlocal last_h2cast
            par = t % 2
            w_hc = [(s_ve, h2prep[par])]
            if gate_te_last[par] is not None:
                w_hc.append((s_te, gate_te_last[par]))
            h2cast[par] = P.emit("scalar", (lambda s, par=par:
                                            s.copy(H2B[:, par * 2048: par * 2048 + TF], Y[:, :])),
                                 waits=w_hc, inc=s_sc)
            last_h2cast = h2cast[par]

        prep_step(0)
        cast_step(0)

        for t in range(T):
            par = t % 2
            if t + 1 < T:
                prep_step(t + 1)

            ew_mc = {}
            sc_ev_mc = {}
            if t == 0:
                prev_psg_ev = {0: None, 1: None}
            for mc in range(NCH):
                psg = ps_g if mc % 2 == 0 else ps_mod
                def gmm_w(t_, gate, mc=mc, par=par, psg=psg):
                    last = None
                    for k in range(NCH):
                        mov = H2B[:, par * 2048 + k * PP: par * 2048 + (k + 1) * PP]
                        last = t_.matmul(psg[:, gate * PP:(gate + 1) * PP],
                                         wslot[WIDX[W3[gate]]][:, k * 1024 + mc * 128: k * 1024 + (mc + 1) * 128],
                                         mov, start=(k == 0), stop=False)
                    return last

                def gmm_u(t_, gate, mc=mc, par=par, psg=psg):
                    last = None
                    for k in range(NCH):
                        mov = CB[:, par * 2048 + k * PP: par * 2048 + (k + 1) * PP]
                        last = t_.matmul(psg[:, gate * PP:(gate + 1) * PP],
                                         wslot[WIDX[U3[gate]]][:, k * 1024 + mc * 128: k * 1024 + (mc + 1) * 128],
                                         mov, start=False, stop=(k == NCH - 1))
                    return last
                w_g = []
                if mc == 0:
                    w_g = [(s_sc, h2cast[par]), ffn_w_done]
                    if t == 0:
                        w_g.append((s_ve, ln2ev[NG - 1]))
                        w_g.append((s_sc, ln2sc[NG - 1]))
                    if ew_last[par] is not None:
                        w_g.append((s_ve, ew_last[par]))
                if mc >= 2:
                    w_g.append((s_sc, sc_ev_mc[mc - 2]))
                    w_g.append((s_ve, ew_mc.get(mc - 2)))
                else:
                    w_g.append((s_sc, prev_psg_ev[mc % 2]))
                import functools as _ft
                P.emit("tensor", _ft.partial(gmm_w, gate=0), waits=w_g, inc=s_te)
                w_gu = []
                if mc == 0:
                    w_gu = [(s_ve, prev_ew_all)]
                gt = P.emit("tensor", _ft.partial(gmm_u, gate=0), waits=w_gu, inc=s_te)
                for gate in range(1, 3):
                    P.emit("tensor", _ft.partial(gmm_w, gate=gate), inc=s_te)
                    gt = P.emit("tensor", _ft.partial(gmm_u, gate=gate), inc=s_te)
                if mc == NCH - 1:
                    gate_te_last[par] = gt

                mp = mc % 2

                def gevac(s, mc=mc, mp=mp, psg=psg):
                    s.activation(THF[:, mp * PP:(mp + 1) * PP], psg[:, 0:PP], AF.Tanh,
                                 scale=0.5, bias=dc("bfh", mc))
                    s.activation(THI[:, mp * PP:(mp + 1) * PP], psg[:, PP:2 * PP], AF.Tanh,
                                 scale=0.5, bias=dc("bih", mc))
                    return s.activation(THG[:, mp * PP:(mp + 1) * PP], psg[:, 2 * PP:3 * PP], AF.Tanh,
                                        bias=vc("bc", mc))
                w_e = [(s_te, gt)]
                if mc >= 2:
                    w_e.append((s_ve, ew_mc[mc - 2]))
                sc_ev = P.emit("scalar", gevac, waits=w_e, inc=s_sc)
                sc_ev_mc[mc] = sc_ev
                if mc >= NCH - 2:
                    prev_psg_ev[mc % 2] = sc_ev

                def gew_a(v, mc=mc, mp=mp):
                    v.scalar_tensor_tensor(U1s[:, mp * PP:(mp + 1) * PP], THF[:, mp * PP:(mp + 1) * PP],
                                           1.0, C[:, mc * PP:(mc + 1) * PP], OP.add, OP.mult)
                    return v.scalar_tensor_tensor(U2s[:, mp * PP:(mp + 1) * PP], THI[:, mp * PP:(mp + 1) * PP],
                                                  1.0, THG[:, mp * PP:(mp + 1) * PP], OP.add, OP.mult)

                def gew_b(v, mc=mc, mp=mp):
                    return v.tensor_tensor(U1s[:, mp * PP:(mp + 1) * PP], U1s[:, mp * PP:(mp + 1) * PP],
                                           U2s[:, mp * PP:(mp + 1) * PP], OP.add)

                def gew_c(v, mc=mc, mp=mp, t=t):
                    v.tensor_scalar_mul(C[:, mc * PP:(mc + 1) * PP], U1s[:, mp * PP:(mp + 1) * PP], 0.5)
                    return v.tensor_scalar_mul(CB[:, (1 - t % 2) * 2048 + mc * PP: (1 - t % 2) * 2048 + (mc + 1) * PP],
                                               U1s[:, mp * PP:(mp + 1) * PP], 0.5)
                ga_ = P.emit("vector", gew_a, waits=[(s_sc, sc_ev)], inc=s_ve)
                gb_ = P.emit("vector", gew_b, waits=[(s_ve, ga_)], inc=s_ve)
                ew_mc[mc] = P.emit("vector", gew_c, waits=[(s_ve, gb_)], inc=s_ve)
            ew_last[par] = ew_mc[NCH - 1]
            prev_ew_all = ew_mc[NCH - 1]
            if t + 1 < T:
                cast_step(t + 1)

        uo_done = load_weight(1, wd["uoT"], waits_extra=[(s_te, gate_te_last[(T - 1) % 2])])
        wo_done = load_weight(0, wd["woT"], waits_extra=[(s_te, gate_te_last[(T - 1) % 2])])
        cfin_par = T % 2

        u_ev = {}
        for oc in range(NCH):
            pu = ps_c if oc % 2 == 0 else ps_d

            def umm(t_, oc=oc, pu=pu):
                last = None
                for k in range(NCH):
                    last = t_.matmul(pu[:, 0:PP],
                                     wslot[1][:, k * 1024 + oc * 128: k * 1024 + (oc + 1) * 128],
                                     CB[:, cfin_par * 2048 + k * PP: cfin_par * 2048 + (k + 1) * PP],
                                     start=(k == 0), stop=(k == NCH - 1))
                return last
            w_u = [(s_ve, prev_ew_all), uo_done]
            if oc == 0:
                w_u.append((s_te, bcast_te[0]))
                w_u.append((s_te, bcast_te[1]))
            if oc >= 2:
                w_u.append((s_sc, u_ev[oc - 2]))
            ut = P.emit("tensor", umm, waits=w_u, inc=s_te)
            u_ev[oc] = P.emit("scalar", (lambda s, oc=oc, pu=pu:
                                         s.activation(C[:, oc * PP:(oc + 1) * PP], pu[:, 0:PP],
                                                      AF.Identity, bias=vc("bo", oc))),
                              waits=[(s_te, ut), (s_ve, prev_ew_all)], inc=s_sc)
        u_done_all = u_ev[NCH - 1]

        out_dma_cnt = 0
        prev_outdma = {}
        prev_h2o = [None, None]
        prev_h2co = [None, None]
        prev_bc_o = [None, None]
        prev_mm_o = {}
        prev_stt_o = {}
        for gp_ in range(NG):
            t0o = GT * gp_
            for t2 in range(GT):
                t = t0o + t2
                par = t2

                w_x = []
                if gp_ == 0:
                    w_x.append((s_sc, h2cast[par]))
                    w_x.append((s_ve, h2prep[par]))
                else:
                    w_x.append((s_ve, prev_h2o[par]))
                    w_x.append((s_ve, prev_stt_o[(gp_ - 1, NCH - 1, par)]))
                x2o = P.emit("sync", (lambda s, t=t, par=par:
                                      s.dma_start(out=XBF[:, par * 2048: par * 2048 + TF], in_=x2d[t])),
                             waits=w_x, inc=ldx2[par], inc_by=16)

                w_r = []
                if prev_bc_o[par] is not None:
                    w_r.append((s_te, prev_bc_o[par]))
                elif gp_ == 0:
                    w_r.append((s_te, bcast_te[par]))

                P.emit("sync", (lambda s, g=gp_, t2=t2, par=par:
                                s.dma_start(out=ROWSTAGE[0:1, par * 2 * PP: par * 2 * PP + PP],
                                            in_=rowsd[g, 0, t2 * PP:(t2 + 1) * PP])),
                       waits=w_r, inc=ldr[par], inc_by=16)
                rowo = P.emit("sync", (lambda s, g=gp_, t2=t2, par=par:
                                       s.dma_start(out=ROWSTAGE[0:1, par * 2 * PP + PP: par * 2 * PP + 2 * PP],
                                                   in_=rowsd[g, 1, t2 * PP:(t2 + 1) * PP])),
                              inc=ldr[par], inc_by=16)

                psbc = ps_c if par == 0 else ps_d
                w_bc = [(ldr[par], rowo), (s_sc, u_done_all)]
                if prev_h2o[par] is not None:
                    w_bc.append((s_ve, prev_h2o[par]))
                bco = P.emit("tensor", (lambda t_, par=par, psbc=psbc:
                                        t_.matmul(psbc[:, 0:2 * PP], ONESF[:, :],
                                                  ROWSTAGE[:, par * 2 * PP:(par + 1) * 2 * PP],
                                                  start=True, stop=True)),
                             waits=w_bc, inc=s_te)
                prev_bc_o[par] = bco

                def h2po_a(v, par=par, psbc=psbc):
                    xv = XBF[:, par * 2048: par * 2048 + TF].rearrange("p (c q) -> p c q", c=NCH)
                    yv = Y[:, :].rearrange("p (c q) -> p c q", c=NCH)
                    mb_bc = sap(psbc, 0, [[2 * PP, 128], [0, NCH], [1, PP]])
                    return v.tensor_tensor(yv, xv, mb_bc, OP.subtract)

                def h2po_b(v, psbc=psbc):
                    yv = Y[:, :].rearrange("p (c q) -> p c q", c=NCH)
                    rs_bc = sap(psbc, PP, [[2 * PP, 128], [0, NCH], [1, PP]])
                    return v.tensor_tensor(yv, yv, rs_bc, OP.mult)

                def h2po_c(v, t=t):
                    last = None
                    for c in range(NCH):
                        last = v.scalar_tensor_tensor(Y[:, c * PP:(c + 1) * PP], Y[:, c * PP:(c + 1) * PP],
                                                      modcol(4, c, t), modcol_bc(3, c, t, PP),
                                                      OP.mult, OP.add)
                    return last
                w_h2 = [(s_te, bco), (ldx2[par], x2o)]
                w_h2.append((s_sc, last_h2cast))
                if gp_ == 0 and t2 == 0:
                    w_h2.append((s_ve, prev_ew_all))
                hoa = P.emit("vector", h2po_a, waits=w_h2, inc=s_ve)
                hob = P.emit("vector", h2po_b, waits=[(s_ve, hoa)], inc=s_ve)
                h2o = P.emit("vector", h2po_c, waits=[(s_ve, hob)], inc=s_ve)
                prev_h2o[par] = h2o

                w_hc = [(s_ve, h2o)]
                if prev_mm_o.get(par) is not None:
                    w_hc.append((s_te, prev_mm_o[par]))
                h2co = P.emit("scalar", (lambda s, par=par:
                                         s.copy(H2B[:, par * 2048: par * 2048 + TF], Y[:, :])),
                              waits=w_hc, inc=s_sc)
                prev_h2co[par] = h2co
                last_h2cast = h2co

            def psout(t2, op_):
                if t2 == 0:
                    return (ps_a if op_ == 0 else ps_b)[:, 0:PP]
                return (ps_g if op_ == 0 else ps_mod)[:, 0:PP]
            mm_oc = {}
            for oc in range(NCH):
                def omm(t_, oc=oc):
                    last = None
                    for k in range(NCH):
                        for t2 in range(GT):
                            last = t_.matmul(psout(t2, oc % 2),
                                             wslot[0][:, k * 1024 + oc * 128: k * 1024 + (oc + 1) * 128],
                                             H2B[:, t2 * 2048 + k * PP: t2 * 2048 + (k + 1) * PP],
                                             start=(k == 0), stop=(k == NCH - 1))
                    return last
                w_m = [(s_sc, prev_h2co[1]), wo_done]
                if gp_ == 0 and oc == 0:
                    w_m.append((s_ve, prev_ew_all))
                    w_m.append((s_sc, u_done_all))
                if oc >= 2:
                    w_m.append((s_ve, prev_stt_o[(gp_, oc - 2, 1)]))
                mm_oc[oc] = P.emit("tensor", omm, waits=w_m, inc=s_te)

                op_ = oc % 2
                for t2 in range(GT):
                    t = t0o + t2

                    def oev(v, oc=oc, op_=op_, t2=t2):
                        return v.tensor_tensor(Z3[:, op_ * PP:(op_ + 1) * PP],
                                               psout(t2, op_),
                                               C[:, oc * PP:(oc + 1) * PP], OP.add)
                    w_e = [(s_te, mm_oc[oc]), (s_sc, u_done_all)]
                    if (gp_, oc - 2, t2) in prev_stt_o or t2 > 0 or oc > 0 or gp_ > 0:
                        w_e.append((s_sc, P.val(s_sc)))
                    vz = P.emit("vector", oev, waits=w_e, inc=s_ve)
                    sz = P.emit("scalar", (lambda s, op_=op_: s.activation(Z3B[:, op_ * PP:(op_ + 1) * PP],
                                                                           Z3[:, op_ * PP:(op_ + 1) * PP], AF.Tanh)),
                                waits=[(s_ve, vz)], inc=s_sc)

                    def ostt(v, oc=oc, op_=op_, t=t, par=t2):
                        return v.scalar_tensor_tensor(U1s[:, op_ * PP:(op_ + 1) * PP],
                                                      Z3B[:, op_ * PP:(op_ + 1) * PP],
                                                      modcol(5, oc, t),
                                                      XBF[:, par * 2048 + oc * PP: par * 2048 + (oc + 1) * PP],
                                                      OP.mult, OP.add)
                    w_s = [(s_sc, sz)]
                    if prev_outdma.get(op_) is not None:
                        w_s.append((sqo[op_], prev_outdma[op_]))
                    so = P.emit("vector", ostt, waits=w_s, inc=s_ve)
                    prev_stt_o[(gp_, oc, t2)] = so

                    od = P.emit(SCQ, (lambda s, t=t, oc=oc, op_=op_:
                                           s.dma_start(out=outp[t][:, oc * PP:(oc + 1) * PP],
                                                       in_=U1s[:, op_ * PP:(op_ + 1) * PP])),
                                waits=[(s_ve, so)], inc=sqo[op_], inc_by=16)
                    prev_outdma[op_] = od
                    out_dma_cnt += 1

        if dbg:
            def dvd(s):
                s.dma_start(out=dbg_dv[:, 0:64], in_=VEC[:, :]).then_inc(ldm, 16)
                return s.dma_start(out=dbg_dv[:, 64:112], in_=DVEC[:, 0:48])
            P.cnt[id(ldm)] = P.val(ldm) + 16
            P.emit("sync", dvd, waits=[(sqs, P.val(sqs))], inc=ldm, inc_by=16)

        if dbg:
            first_w = [(sqs, P.val(sqs))]
            for t in range(T):
                P.emit("sync", (lambda s, t=t: s.dma_start(out=dbg_x2[t], in_=x2d[t])),
                       waits=(first_w if t == 0 else ()), inc=ldm, inc_by=16)
                P.emit("sync", (lambda s, t=t: s.dma_start(out=dbg_ci[t], in_=cind[t])),
                       inc=ldm, inc_by=16)
            P.emit("sync", lambda s: None, waits=[(ldm, P.val(ldm))])

        P.emit("sync", lambda s: None,
               waits=[(sqo[0], P.val(sqo[0])), (sqo[1], P.val(sqo[1])),
                      (sqs, P.val(sqs)), (sqr, P.val(sqr)), (ccs, P.val(ccs)),
                      (gqa[0], P.val(gqa[0])), (gqa[1], P.val(gqa[1]))])

        block = e(nc.Block())

        @block.sync
        def _(s):
            P.replay("sync", s)

        @block.tensor
        def _(t_):
            P.replay("tensor", t_)

        @block.vector
        def _(v):
            P.replay("vector", v)

        @block.scalar
        def _(s):
            P.replay("scalar", s)

        @block.gpsimd
        def _(g_):
            P.replay("gpsimd", g_)

    return nc



def _pack_vec(v):
    return np.ascontiguousarray(v.reshape(8, 128).T)


def prep_inputs(inputs, T, PP):
    NG = T // GT
    GF = NCH * GT * PP
    x = np.asarray(inputs["L0_attn_x"], dtype=np.float32)
    Wa = np.asarray(inputs["L0_attn_Wa"], np.float32)
    Wx = np.asarray(inputs["L0_attn_Wx"], np.float32)
    weights = {
        "WaT": np.ascontiguousarray(Wa.T), "WxT": np.ascontiguousarray(Wx.T),
        "wfT": np.ascontiguousarray(np.asarray(inputs["L0_ffn_wf"], np.float32).T),
        "wiT": np.ascontiguousarray(np.asarray(inputs["L0_ffn_wi"], np.float32).T),
        "wcT": np.ascontiguousarray(np.asarray(inputs["L0_ffn_wc"], np.float32).T),
        "ufT": np.ascontiguousarray(np.asarray(inputs["L0_ffn_uf"], np.float32).T),
        "uiT": np.ascontiguousarray(np.asarray(inputs["L0_ffn_ui"], np.float32).T),
        "ucT": np.ascontiguousarray(np.asarray(inputs["L0_ffn_uc"], np.float32).T),
        "woT": np.ascontiguousarray(np.asarray(inputs["L0_ffn_wo"], np.float32).T),
        "uoT": np.ascontiguousarray(np.asarray(inputs["L0_ffn_uo"], np.float32).T),
    }
    wmod_full = np.asarray(inputs["L0_wmod"], np.float32)
    cc = np.asarray(inputs["L0_c"], np.float32)
    scT = np.ascontiguousarray(cc.reshape(T, 8, 128).transpose(2, 1, 0).reshape(128, 8 * T))
    vecs = np.zeros((128, 64), np.float32)
    order = ["L0_attn_ba", "L0_attn_bx", "L0_ffn_bf", "L0_ffn_bi", "L0_ffn_bc",
             "L0_ffn_bo", "L0_ffn_in_c", "L0_attn_Lam"]
    for i, n in enumerate(order):
        vecs[:, 8 * i:8 * (i + 1)] = _pack_vec(np.asarray(inputs[n], np.float32))

    in_maps = []
    for core in range(NCORES):
        s0 = core * PP
        xc = x[:, s0:s0 + PP, :]
        xr = xc.reshape(NG, GT, PP, NCH, 128)
        xhost = np.ascontiguousarray(xr.transpose(0, 4, 3, 1, 2).reshape(NG, 128, GF))
        masks = np.zeros((128, 128), np.float32)
        for c in range(8):
            for j in range(8):
                masks[:, c * 8 + j] = 1.0 if j < core else 0.0
                masks[:, 64 + c * 8 + j] = 0.0 if j < core else 1.0
        m = {"xh": xhost, "scT": scT, "vecs": vecs, "cmask": masks,
             "wmod": np.ascontiguousarray(wmod_full[:, core * 768:(core + 1) * 768])}
        m.update(weights)
        in_maps.append(m)
    return in_maps


def assemble(results, T, PP):
    S = NCORES * PP
    out = np.empty((T, S, D), np.float32)
    for core in range(NCORES):
        oc = results[core]["outp"]
        o = oc.reshape(T, 128, NCH, PP).transpose(0, 3, 2, 1).reshape(T, PP, D)
        out[:, core * PP:(core + 1) * PP, :] = o
    return out


def kernel(**inputs):
    T, PP = 16, 256
    nc = build_nc(T, PP)
    in_maps = prep_inputs(inputs, T, PP)
    res = run_bass_kernel_spmd(nc, in_maps, core_ids=list(range(NCORES)))
    return assemble(res.results, T, PP)
